# version 10
# baseline (speedup 1.0000x reference)
import numpy as np
import math
from contextlib import ExitStack

import concourse.bass as bass
import concourse.mybir as mybir
from concourse.alu_op_type import AluOpType
from concourse.bass_utils import run_bass_kernel_spmd

F32 = mybir.dt.float32
BF16 = mybir.dt.bfloat16
I32 = mybir.dt.int32
AF = mybir.ActivationFunctionType
OP = AluOpType
AX = mybir.AxisListType.X

T = 4096
D = 1024
L = 8192
NCH = D // 128
TS = T + 1
NK1 = 33
HID = 2730
INNER = 512
HEADS = 8
DH = 64
QK_SCALE = 8.0
ENGS = ("tensor", "vector", "scalar", "gpsimd", "sync")


class Sem:
    def __init__(self, es, nc, name):
        self.h = es.enter_context(nc.semaphore(name))
        self.v = 0

    def tick(self, k=1):
        self.v += k
        return self.v


class K:
    def __init__(self, dbg=()):
        self.dbg = set(dbg)
        self.nc = bass.Bass("TRN2", target_bir_lowering=False, debug=False)
        self.dbg_out = {}

    def declare_dram(self):
        nc = self.nc
        self.din = {}
        for name, shape in [
            ("x", [T, D]), ("Wp", [D, 4 * D]), ("bp", [4 * D]),
            ("Wf", [D, 3 * D]), ("bf", [3 * D]),
            ("Wf_my", [D, 384]), ("bf_my", [384]),
            ("wqkv", [2, D, 3 * INNER]), ("q_scale", [2, DH]), ("k_scale", [2, DH]),
            ("wo", [2, INNER, D]), ("ln_g", [2, D]), ("ln_b", [2, D]),
            ("w1", [2, D, 2 * HID]), ("b1", [2, 2 * HID]),
            ("w2", [2, HID, D]), ("b2", [2, D]),
        ]:
            self.din[name] = nc.dram_tensor(name, shape, F32, kind="ExternalInput").ap()
        self.out = nc.dram_tensor("out", [T, D], F32, kind="ExternalOutput").ap()
        self.xp_dram = nc.dram_tensor("xp_dram", [4 * D, T], BF16)
        self.f_dram = nc.dram_tensor("f_dram", [384, T], BF16)
        self.fsh_dram = nc.dram_tensor("fsh_dram", [128, 384 * 66], BF16)
        self.fall_dram = nc.dram_tensor("fall_dram", [8 * 128, 384 * 66], BF16)
        self.y3_dram = nc.dram_tensor("y3_dram", [128, 32 * D], BF16)
        self.pl_dram = nc.dram_tensor("pl_dram", [128, 4 * T], BF16)
        self.q_dram = nc.dram_tensor("q_dram", [4 * 128, T], BF16)
        self.k_dram = nc.dram_tensor("k_dram", [4 * 128, T], BF16)
        self.v_dram = nc.dram_tensor("v_dram", [T, INNER], BF16)
        self.w1bf = nc.dram_tensor("w1bf", [128, 8 * 2 * HID], BF16)
        self.w2bf = nc.dram_tensor("w2bf", [HID, D], BF16)

    def dbg_tap(self, name, shape, dt=F32):
        if name in self.dbg and name not in self.dbg_out:
            self.dbg_out[name] = self.nc.dram_tensor("dbg_" + name, shape, dt, kind="ExternalOutput").ap()
        return self.dbg_out.get(name)

    def alloc(self, es):
        nc = self.nc
        sb = lambda n, s, d=BF16: es.enter_context(nc.sbuf_tensor(n, s, d))
        self.es_root = es
        self.xstream = sb("xstream", [128, NCH * TS])
        self.ident = sb("ident", [128, 128])
        self.pi2 = sb("pi2", [128, 2], F32)
        self.iotap = sb("iotap", [128, 4], F32)
        self.r2pi = sb("r2pi", [128, 1], F32)
        self.scr = sb("scr", [128, 64], F32)
        self.lncol = sb("lncol", [128, 8], F32)
        self.w64stk = sb("w64stk", [32, 99])
        self.g_tab = sb("g_tab", [66, 32])
        self.CONSTS = [1e-5, -0.5, 0.5, -1.0, 2.0, 1e-24, 0.25, -0.25, -8.0,
                       8.0, 0.125, -2048.0, 0.00390625, 64.0, -64.0]
        self.fc = sb("fc", [128, len(self.CONSTS)], F32)
        for i, val in enumerate(self.CONSTS):
            nc.const_aps.register(F32, val, self.fc[:, i:i + 1])
        self.s_ld = Sem(es, nc, "s_ld")
        self.s_st = Sem(es, nc, "s_st")
        self.s_t = Sem(es, nc, "s_t")
        self.s_v = Sem(es, nc, "s_v")
        self.s_a = Sem(es, nc, "s_a")
        self.s_p = Sem(es, nc, "s_p")
        self.s_cc = Sem(es, nc, "s_cc")
        self.s_bar = Sem(es, nc, "s_bar")
        self.s_dma = [Sem(es, nc, f"s_dma{i}") for i in range(8)]
        self.s_gq = Sem(es, nc, "s_gq")

    def barrier(self, engines=("tensor", "vector", "scalar", "sync", "gpsimd")):
        nc = self.nc
        tgt = self.s_bar.tick(len(engines))
        with nc.Block() as b:
            for e in engines:
                def mk(e=e):
                    def fn(eng):
                        eng.drain().then_inc(self.s_bar.h, 1)
                        eng.wait_ge(self.s_bar.h, tgt)
                    return fn
                getattr(b, e)(mk())

    def p0_seed(self, plane, _unused=None):
        nc, s_p, s_v, s_a = self.nc, self.s_p, self.s_v, self.s_a
        pa = plane[:, 0:128]
        pb = plane[:, 128:256]
        pa_i, pb_i = pa.bitcast(I32), pb.bitcast(I32)
        t_io = s_p.tick()
        t_id = s_v.tick()
        t_pi = s_a.tick()
        t_r2 = s_v.tick()
        with nc.Block() as b:
            @b.gpsimd
            def _(g):
                g.iota(pa_i[:, :], [[1, 128]], channel_multiplier=0)
                g.iota(pb_i[:, :], [[0, 128]], channel_multiplier=1).then_inc(s_p.h, 1)

            @b.vector
            def _(v):
                v.wait_ge(s_p.h, t_io)
                v.tensor_copy(self.iotap[:, 0:1], pb_i[:, 0:1])
                v.tensor_tensor(pa[:, :], pa_i[:, :], pb_i[:, :], OP.is_equal)
                v.drain()
                v.tensor_copy(self.ident[:], pa[:, :])
                for i, val in enumerate(self.CONSTS):
                    v.memset(self.fc[:, i:i + 1], val)
                v.memset(self.xstream[:, :], 0.0)
                v.drain().then_inc(s_v.h, 1)
                ii = pb_i[:, 1:2]
                v.tensor_scalar(ii, pb_i[:, 0:1], 1, None, OP.arith_shift_right)
                v.drain()
                v.tensor_copy(self.iotap[:, 1:2], ii)
                v.drain()
                v.tensor_tensor(self.iotap[:, 2:3], self.iotap[:, 1:2], self.iotap[:, 1:2], OP.add)
                v.drain()
                v.tensor_tensor(self.iotap[:, 2:3], self.iotap[:, 0:1], self.iotap[:, 2:3], OP.subtract)
                v.drain()
                v.tensor_scalar(self.iotap[:, 2:3], self.iotap[:, 2:3], 0.25, None, OP.mult)
                v.drain()
                v.wait_ge(s_a.h, t_pi)
                v.reciprocal(self.r2pi[:, 0:1], self.pi2[:, 1:2])
                v.drain().then_inc(s_v.h, 1)

            @b.scalar
            def _(s):
                s.wait_ge(s_v.h, t_id)
                s.activation(self.pi2[:, 0:1], self.iotap[:, 0:1], AF.Identity, bias=1.0, scale=0.0)
                s.drain()
                s.activation(self.pi2[:, 0:1], self.pi2[:, 0:1], AF.Arctan)
                s.drain()
                s.mul(self.pi2[:, 0:1], self.pi2[:, 0:1], 4.0)
                s.drain()
                s.mul(self.pi2[:, 1:2], self.pi2[:, 0:1], 2.0)
                s.drain()
                s.activation(self.iotap[:, 3:4], self.iotap[:, 0:1], AF.Identity, bias=1.0, scale=0.0)
                s.drain()
                s.activation(self.iotap[:, 3:4], self.iotap[:, 3:4], AF.Ln, scale=10000.0)
                s.drain().then_inc(s_a.h, 1)
        self.t_seed_a = t_pi
        self.t_seed_v2 = t_r2

    def p1_ln1(self, pa, xtok, xn_tok, ps_t):
        nc = self.nc
        s_t, s_v, s_a = self.s_t, self.s_v, self.s_a
        sd = self.s_dma
        ld = [sd[tb % 2].tick(16) for tb in range(32)]
        a_sq, v_st, a_rs, v_bi, a_ap, t_tp, v_ev = [], [], [], [], [], [], []
        for tb in range(32):
            a_sq.append(s_a.tick())
            v_st.append(s_v.tick())
            a_rs.append(s_a.tick())
            v_bi.append(s_v.tick())
            a_ap.append(s_a.tick())
            t_tp.append(s_t.tick())
            v_ev.append(s_v.tick())
        self.p1_done_v = v_ev[31]

        with nc.Block() as b:
            @b.sync
            def _(sync):
                for tb in range(32):
                    if tb >= 2:
                        sync.wait_ge(s_a.h, a_ap[tb - 2])
                        sync.wait_ge(s_v.h, v_st[tb - 2])
                    sync.dma_start(out=xtok[tb % 2][:, :],
                                   in_=self.din["x"][tb * 128:(tb + 1) * 128, :]).then_inc(sd[tb % 2].h, 16)

            @b.scalar
            def _(s):
                c = self.lncol
                s.wait_ge(self.s_a.h, self.t_seed_a)
                for tb in range(32):
                    s.wait_ge(sd[tb % 2].h, ld[tb])
                    s.activation(pa[:, 0:D], xtok[tb % 2][:, :], AF.Square,
                                 accum_out=c[:, 1:2]).then_inc(s_a.h, 1)
                    s.wait_ge(s_v.h, v_st[tb])
                    s.activation(c[:, 5:6], c[:, 4:5], AF.Ln, bias=1e-5)
                    s.drain()
                    s.activation(c[:, 6:7], c[:, 5:6], AF.Exp, scale=-0.5)
                    s.drain().then_inc(s_a.h, 1)
                    s.wait_ge(s_v.h, v_bi[tb])
                    s.activation(xn_tok[:, :], xtok[tb % 2][:, :], AF.Identity,
                                 bias=c[:, 7:8], scale=c[:, 6:7]).then_inc(s_a.h, 1)

            @b.vector
            def _(v):
                c = self.lncol
                for tb in range(32):
                    v.wait_ge(sd[tb % 2].h, ld[tb])
                    v.wait_ge(s_a.h, a_sq[tb])
                    v.reduce_sum(c[:, 0:1], xtok[tb % 2][:, :], axis=AX)
                    v.drain()
                    v.tensor_scalar(c[:, 2:3], c[:, 0:1], 1.0 / D, None, OP.mult)
                    v.tensor_scalar(c[:, 3:4], c[:, 1:2], 1.0 / D, None, OP.mult)
                    v.drain()
                    v.tensor_tensor(c[:, 4:5], c[:, 2:3], c[:, 2:3], OP.mult)
                    v.drain()
                    v.tensor_tensor(c[:, 4:5], c[:, 3:4], c[:, 4:5], OP.subtract)
                    v.drain().then_inc(s_v.h, 1)
                    v.wait_ge(s_a.h, a_rs[tb])
                    v.tensor_tensor(c[:, 7:8], c[:, 2:3], c[:, 6:7], OP.mult)
                    v.drain()
                    v.tensor_scalar(c[:, 7:8], c[:, 7:8], -1.0, None, OP.mult)
                    v.drain().then_inc(s_v.h, 1)
                    v.wait_ge(s_t.h, t_tp[tb])
                    pt = ps_t[tb % 2]
                    v.tensor_copy(
                        self.xstream.rearrange("p (c t) -> p c t", t=TS)[:, :, 1 + tb * 128:1 + (tb + 1) * 128],
                        pt.rearrange("p (c t) -> p c t", t=128)[:, :, :],
                    ).then_inc(s_v.h, 1)

            @b.tensor
            def _(t):
                for tb in range(32):
                    t.wait_ge(s_a.h, a_ap[tb])
                    if tb >= 2:
                        t.wait_ge(s_v.h, v_ev[tb - 2])
                    pt = ps_t[tb % 2]
                    for dc in range(NCH):
                        mm = t.transpose(pt[:, dc * 128:(dc + 1) * 128],
                                         xn_tok[:, dc * 128:(dc + 1) * 128], self.ident[:])
                    mm.then_inc(s_t.h, 1)

        tap = self.dbg_tap("xn_feat", [128, NCH * TS], BF16)
        if tap is not None:
            with nc.Block() as b:
                td = self.s_st.tick(16)

                @b.sync
                def _(sync):
                    sync.wait_ge(s_v.h, v_ev[31])
                    sync.dma_start(out=tap, in_=self.xstream[:, :]).then_inc(self.s_st.h, 16)
                    sync.wait_ge(self.s_st.h, td)

    def p2_proj(self):
        nc = self.nc
        s_t, s_v, s_a, s_p = self.s_t, self.s_v, self.s_a, self.s_p
        sd = self.s_dma
        with ExitStack() as es:
            sb = lambda n, s, d=BF16: es.enter_context(nc.sbuf_tensor(n, s, d))
            ps = lambda n, s, d=F32: es.enter_context(nc.psum_tensor(n, s, d))
            posc = sb("p2_posc", [128, NCH * T])
            g = sb("p2_g", [128, T], F32)
            ucol = sb("p2_ucol", [128, 8], F32)
            upl = sb("p2_upl", [128, T], F32)
            fr = sb("p2_fr", [128, T], F32)
            wstg = [sb(f"p2_wstg{i}", [128, 1024], F32) for i in range(2)]
            wbf = [sb(f"p2_wbf{i}", [128, 1024]) for i in range(2)]
            ostg = [sb(f"p2_ostg{i}", [128, 512]) for i in range(2)]
            brows = sb("p2_brows", [35, 128], F32)
            brows_bf = sb("p2_brows_bf", [35, 128])
            bcols = sb("p2_bcols", [128, 35], F32)
            psA = ps("p2_psA", [128, 512])
            psB = ps("p2_psB", [128, 512])
            psbc = ps("p2_psbc", [128, 35], BF16)

            t_bld = sd[6].tick(16 * 2)
            t_bcast = s_v.tick()
            t_bT = s_t.tick()
            t_bc = s_v.tick()
            with nc.Block() as b:
                @b.sync
                def _(sync):
                    sync.dma_start(out=brows[0:32, :],
                                   in_=self.din["bp"].rearrange("(a c) -> a c", c=128)).then_inc(sd[6].h, 16)
                    sync.dma_start(out=brows[32:35, :],
                                   in_=self.din["bf_my"].rearrange("(a c) -> a c", c=128)).then_inc(sd[6].h, 16)

                @b.vector
                def _(v):
                    v.wait_ge(sd[6].h, t_bld)
                    v.tensor_copy(brows_bf[:, :], brows[:, :])
                    v.drain().then_inc(s_v.h, 1)
                    v.wait_ge(s_t.h, t_bT)
                    v.tensor_copy(bcols[:, :], psbc[:, :])
                    v.drain().then_inc(s_v.h, 1)

                @b.tensor
                def _(t):
                    t.wait_ge(s_v.h, t_bcast)
                    t.transpose(psbc[:, :], brows_bf[:, :], self.ident[0:35, 0:35]).then_inc(s_t.h, 1)

            p_io, v_e, a_if, v_tf, a_up, v_fr, a_sin = [], [], [], [], [], [], []
            for ck in range(NCH):
                p_io.append(s_p.tick())
                v_e.append(s_v.tick())
                a_if.append(s_a.tick())
                v_tf.append(s_v.tick())
                a_up.append(s_a.tick())
                v_fr.append(s_v.tick())
                a_sin.append(s_a.tick())
            with nc.Block() as b:
                @b.gpsimd
                def _(gp):
                    for ck in range(NCH):
                        if ck >= 1:
                            gp.wait_ge(s_v.h, v_fr[ck - 1])
                        gp.iota(upl.bitcast(I32)[:, 0:T], [[1, T]], channel_multiplier=0).then_inc(s_p.h, 1)

                @b.vector
                def _(v):
                    for ck in range(NCH):
                        if ck == 0:
                            v.wait_ge(s_v.h, self.t_seed_v2)
                        v.tensor_scalar(ucol[:, 0:1], self.iotap[:, 1:2], -1.0 / 512, None, OP.mult)
                        v.drain()
                        v.tensor_scalar(ucol[:, 0:1], ucol[:, 0:1], -ck * 0.125, None, OP.add)
                        v.drain()
                        v.drain().then_inc(s_v.h, 1)
                        v.wait_ge(s_a.h, a_if[ck])
                        v.tensor_tensor(ucol[:, 2:3], ucol[:, 1:2], self.r2pi[:, 0:1], OP.mult)
                        v.drain()
                        v.wait_ge(s_p.h, p_io[ck])
                        v.tensor_copy(upl[:, :], upl.bitcast(I32)[:, 0:T])
                        v.drain().then_inc(s_v.h, 1)
                        v.wait_ge(s_a.h, a_up[ck])
                        v.tensor_copy(g.bitcast(I32)[:, 0:T], upl[:, :])
                        v.drain()
                        v.tensor_copy(g[:, :], g.bitcast(I32)[:, 0:T])
                        v.drain()
                        v.tensor_tensor(fr[:, :], upl[:, :], g[:, :], OP.subtract)
                        v.drain()
                        v.tensor_scalar(g[:, :], fr[:, :], 0.5, None, OP.is_gt)
                        v.drain()
                        v.tensor_tensor(fr[:, :], fr[:, :], g[:, :], OP.subtract)
                        v.drain()
                        v.tensor_scalar(g[:, :], fr[:, :], -0.5, None, OP.is_lt)
                        v.drain()
                        v.tensor_tensor(fr[:, :], fr[:, :], g[:, :], OP.add)
                        v.drain().then_inc(s_v.h, 1)

                @b.scalar
                def _(s):
                    for ck in range(NCH):
                        s.wait_ge(s_v.h, v_e[ck])
                        s.activation(ucol[:, 1:2], ucol[:, 0:1], AF.Exp,
                                     scale=self.iotap[:, 3:4]).then_inc(s_a.h, 1)
                        s.wait_ge(s_v.h, v_tf[ck])
                        s.activation(upl[:, :], upl[:, :], AF.Identity,
                                     scale=ucol[:, 2:3], bias=self.iotap[:, 2:3]).then_inc(s_a.h, 1)
                        s.wait_ge(s_v.h, v_fr[ck])
                        s.activation(posc[:, ck * T:(ck + 1) * T], fr[:, :], AF.Sin,
                                     scale=self.pi2[:, 1:2]).then_inc(s_a.h, 1)

            NT2 = 35
            w_ld = [sd[2 + ft % 2].tick(16) for ft in range(NT2)]
            v_cast = [s_v.tick() for _ in range(NT2)]
            t_mm, a_ev, st_out = [], [], []
            for ft in range(NT2):
                t_mm.append([s_t.tick() for _ in range(8)])
                a_ev.append([s_a.tick() for _ in range(8)])
                st_out.append([sd[4 + (ft * 8 + tt) % 2].tick(16) for tt in range(8)])

            with nc.Block() as b:
                @b.sync
                def _(sync):
                    def emit_load(ft):
                        if ft >= 2:
                            sync.wait_ge(s_v.h, v_cast[ft - 2])
                        if ft < 32:
                            wsrc = self.din["Wp"][:, ft * 128:(ft + 1) * 128]
                        else:
                            wsrc = self.din["Wf_my"][:, (ft - 32) * 128:(ft - 31) * 128]
                        wsrc3 = wsrc.rearrange("(c p) f -> c p f", p=128).transpose([1, 0, 2])
                        sync.dma_start(out=wstg[ft % 2].rearrange("p (c f) -> p c f", f=128),
                                       in_=wsrc3).then_inc(sd[2 + ft % 2].h, 16)

                    def emit_stores(ft):
                        for tt in range(8):
                            u = ft * 8 + tt
                            sync.wait_ge(s_a.h, a_ev[ft][tt])
                            dst = self.xp_dram if ft < 32 else self.f_dram
                            r0 = ft * 128 if ft < 32 else (ft - 32) * 128
                            sync.dma_start(out=dst.ap()[r0:r0 + 128, tt * 512:(tt + 1) * 512],
                                           in_=ostg[u % 2][:, :]).then_inc(sd[4 + u % 2].h, 16)
                    for ft in range(NT2):
                        emit_load(ft)
                        if ft >= 1:
                            emit_stores(ft - 1)
                    emit_stores(NT2 - 1)
                    sync.wait_ge(sd[4].h, sd[4].v)
                    sync.wait_ge(sd[5].h, sd[5].v)

                @b.vector
                def _(v):
                    for ft in range(NT2):
                        v.wait_ge(sd[2 + ft % 2].h, w_ld[ft])
                        if ft >= 2:
                            v.wait_ge(s_t.h, t_mm[ft - 2][7])
                        v.tensor_copy(wbf[ft % 2][:, :], wstg[ft % 2][:, :])
                        v.drain().then_inc(s_v.h, 1)

                @b.tensor
                def _(t):
                    for ft in range(NT2):
                        t.wait_ge(s_v.h, v_cast[ft])
                        if ft == 0:
                            t.wait_ge(s_v.h, self.p1_done_v)
                        if ft == 32:
                            t.wait_ge(s_a.h, a_sin[7])
                        for tt in range(8):
                            u = ft * 8 + tt
                            pp = psA if u % 2 == 0 else psB
                            if u >= 2:
                                t.wait_ge(s_a.h, a_ev[(u - 2) // 8][(u - 2) % 8])
                            for ck in range(NCH):
                                if ft < 32:
                                    rhs = self.xstream.rearrange("p (c t) -> p c t", t=TS)[:, ck, 1 + tt * 512:1 + (tt + 1) * 512]
                                else:
                                    rhs = posc[:, ck * T + tt * 512:ck * T + (tt + 1) * 512]
                                mm = t.matmul(pp[:, :], wbf[ft % 2][:, ck * 128:(ck + 1) * 128], rhs,
                                              start=(ck == 0), stop=(ck == 7))
                            mm.then_inc(s_t.h, 1)

                @b.scalar
                def _(s):
                    for ft in range(NT2):
                        bc = bcols[:, ft:ft + 1]
                        s.wait_ge(s_v.h, t_bc)
                        for tt in range(8):
                            u = ft * 8 + tt
                            pp = psA if u % 2 == 0 else psB
                            s.wait_ge(s_t.h, t_mm[ft][tt])
                            if u >= 2:
                                s.wait_ge(sd[4 + u % 2].h, st_out[(u - 2) // 8][(u - 2) % 8])
                            s.activation(ostg[u % 2][:, :], pp[:, :], AF.Identity,
                                         bias=bc, scale=1.0).then_inc(s_a.h, 1)
        self.p2_end = dict(v=self.s_v.v, a=self.s_a.v, t=self.s_t.v)

    def p_hyena(self):
        nc = self.nc
        s_t, s_v, s_a, s_p, s_cc = self.s_t, self.s_v, self.s_a, self.s_p, self.s_cc
        sd = self.s_dma
        NG = 16
        with ExitStack() as es:
            sb = lambda n, s, d=BF16: es.enter_context(nc.sbuf_tensor(n, s, d))
            ps = lambda n, s, d=F32: es.enter_context(nc.psum_tensor(n, s, d))
            m_re = sb("hy_mre", [128, NK1 * 128])
            m_im = sb("hy_mim", [128, NK1 * 128])
            n_re = sb("hy_nre", [128, NK1 * 128])
            n_im = sb("hy_nim", [128, NK1 * 128])
            n_imn = sb("hy_nimn", [128, NK1 * 128])
            w64r = sb("hy_w64r", [128, 99])
            w64b = sb("hy_w64b", [128, 396])
            g_tab = sb("hy_gtab", [128, 32])
            qtb = sb("hy_qtb", [128, 8192])
            ytime = [sb(f"hy_yt{i}", [128, 2048]) for i in range(2)]
            xpg = sb("hy_xpg", [128, 2048])
            ps1 = [ps("hy_ps1a", [128, 396]), ps("hy_ps1b", [128, 396])]
            psX = [ps("hy_xre0", [128, 512]), ps("hy_xim0", [128, 512]),
                   ps("hy_xre1", [128, 512]), ps("hy_xim1", [128, 512])]
            psQ = [ps("hy_qre", [128, 512]), ps("hy_qim", [128, 512])]
            psT = psX[1].bitcast(BF16)
            psT2 = psX[3].bitcast(BF16)
            psI2 = psX[0]
            psI2b = psX[2]
            def s1ps(bb):
                return [ps1[0][:, :], ps1[1][:, :], psQ[0][:, 0:396], psQ[1][:, 0:396]][bb % 4]

            with ExitStack() as es2:
                sb2 = lambda n, s, d=F32: es2.enter_context(nc.sbuf_tensor(n, s, d))
                ta = sb2("tg_a", [128, NK1 * 128])
                tb = sb2("tg_b", [128, NK1 * 128])
                tgs = sb2("tg_s", [128, NK1 * 128])

                def wrap(v, fr, gg):
                    v.tensor_copy(gg.bitcast(I32), fr)
                    v.drain()
                    v.tensor_copy(gg, gg.bitcast(I32))
                    v.drain()
                    v.tensor_tensor(fr, fr, gg, OP.subtract)
                    v.drain()
                    v.tensor_scalar(gg, fr, 0.5, None, OP.is_gt)
                    v.drain()
                    v.tensor_tensor(fr, fr, gg, OP.subtract)
                    v.drain()
                    v.tensor_scalar(gg, fr, -0.5, None, OP.is_lt)
                    v.drain()
                    v.tensor_tensor(fr, fr, gg, OP.add)
                    v.drain()

                ticks_v = [s_v.tick() for _ in range(8)]
                ticks_a = [s_a.tick() for _ in range(8)]
                ticks_p = [s_p.tick() for _ in range(3)]
                with nc.Block() as b:
                    @b.gpsimd
                    def _(g):
                        g.iota(ta[:, :].bitcast(I32), [[1, NK1], [64, 128]], channel_multiplier=0)
                        g.iota(tb[:, :].bitcast(I32), [[0, NK1], [0, 128]], channel_multiplier=1).then_inc(s_p.h, 1)
                        g.wait_ge(s_a.h, ticks_a[1])
                        g.iota(ta[:, :].bitcast(I32), [[0, NK1], [1, 128]], channel_multiplier=0)
                        g.iota(tb[:, :].bitcast(I32), [[1, NK1], [0, 128]], channel_multiplier=64).then_inc(s_p.h, 1)
                        g.wait_ge(s_a.h, ticks_a[4])
                        g.iota(tb[:, :].bitcast(I32), [[1, NK1], [0, 128]], channel_multiplier=0)
                        g.iota(tgs[:, :].bitcast(I32)[:, 0:1], [[0, 1]], channel_multiplier=1)
                        g.iota(tgs[:, :].bitcast(I32)[:, 1:34], [[1, 33]], channel_multiplier=0)
                        g.iota(tgs[:, :].bitcast(I32)[:, 34:66], [[1, 32]], channel_multiplier=0).then_inc(s_p.h, 1)

                    @b.vector
                    def _(v):
                        v.wait_ge(s_p.h, ticks_p[0])
                        v.tensor_tensor(ta[:, :].bitcast(I32), ta[:, :].bitcast(I32), tb[:, :].bitcast(I32), OP.mult)
                        v.drain()
                        v.tensor_copy(ta[:, :], ta[:, :].bitcast(I32))
                        v.drain()
                        v.tensor_scalar(ta[:, :], ta[:, :], 2.0 ** -13, None, OP.mult)
                        v.drain()
                        v.tensor_scalar(tb[:, :], ta[:, :], -1.0, None, OP.mult)
                        v.drain()
                        wrap(v, tb[:, :], tgs[:, :])
                        v.drain().then_inc(s_v.h, 1)
                        v.wait_ge(s_a.h, ticks_a[0])
                        v.tensor_scalar(tb[:, :], ta[:, :], 0.25, None, OP.add)
                        v.drain()
                        wrap(v, tb[:, :], tgs[:, :])
                        v.drain().then_inc(s_v.h, 1)
                        v.wait_ge(s_p.h, ticks_p[1])
                        v.wait_ge(s_a.h, ticks_a[1])
                        v.tensor_tensor(ta[:, :].bitcast(I32), ta[:, :].bitcast(I32), tb[:, :].bitcast(I32), OP.mult)
                        v.drain()
                        v.tensor_copy(ta[:, :], ta[:, :].bitcast(I32))
                        v.drain()
                        v.tensor_scalar(ta[:, :], ta[:, :], 2.0 ** -13, None, OP.mult)
                        v.drain()
                        v.tensor_copy(tb[:, :], ta[:, :])
                        v.drain()
                        wrap(v, tb[:, :], tgs[:, :])
                        v.drain().then_inc(s_v.h, 1)
                        v.wait_ge(s_a.h, ticks_a[2])
                        v.tensor_scalar(tb[:, :], ta[:, :], 0.25, None, OP.add)
                        v.drain()
                        wrap(v, tb[:, :], tgs[:, :])
                        v.drain().then_inc(s_v.h, 1)
                        v.wait_ge(s_a.h, ticks_a[3])
                        v.drain().then_inc(s_v.h, 1)
                        v.wait_ge(s_p.h, ticks_p[2])
                        v.tensor_copy(tb[:, :], tb[:, :].bitcast(I32))
                        v.drain()
                        v.tensor_scalar(ta[:, :], tb[:, :], 0.0, None, OP.is_equal)
                        v.drain()
                        v.tensor_scalar(tb[:, :], tb[:, :], 32.0, None, OP.is_equal)
                        v.drain()
                        v.tensor_tensor(ta[:, :], ta[:, :], tb[:, :], OP.add)
                        v.drain()
                        v.tensor_scalar(tb[:, :], ta[:, :], -1.0, None, OP.mult)
                        v.drain()
                        v.tensor_scalar(tb[:, :], tb[:, :], 2.0, None, OP.add)
                        v.drain()
                        v.tensor_scalar(tb[:, :], tb[:, :], 2.0 ** -13, None, OP.mult)
                        v.drain()
                        v.tensor_tensor(n_re[:, :], n_re[:, :], tb[:, :], OP.mult)
                        v.drain()
                        v.tensor_tensor(n_im[:, :], n_im[:, :], tb[:, :], OP.mult)
                        v.drain()
                        v.tensor_scalar(n_imn[:, :], n_im[:, :], -1.0, None, OP.mult)
                        v.drain().then_inc(s_v.h, 1)
                        ic = tgs[:, :].bitcast(I32)
                        v.memset(w64b[:, :], 0.0)
                        v.drain()
                        v.tensor_scalar(ic[:, 70:71], ic[:, 0:1], 5, None, OP.arith_shift_right)
                        v.drain()
                        v.tensor_scalar(ic[:, 70:71], ic[:, 70:71], 5, None, OP.arith_shift_left)
                        v.drain()
                        v.tensor_tensor(ic[:, 70:71], ic[:, 0:1], ic[:, 70:71], OP.subtract)
                        v.drain()
                        v.tensor_copy(self.scr[:, 9:10], ic[:, 70:71])
                        v.drain()
                        v.tensor_copy(ta[:, 0:33], ic[:, 1:34])
                        v.drain()
                        v.tensor_tensor(ta[:, 0:33], ta[:, 0:33],
                                        self.scr[:, 9:10].broadcast_to([128, 33]), OP.mult)
                        v.drain()
                        v.tensor_scalar(ta[:, 0:33], ta[:, 0:33], 1.0 / 64, None, OP.mult)
                        v.drain()
                        v.tensor_scalar(tb[:, 0:33], ta[:, 0:33], 0.25, None, OP.add)
                        v.tensor_scalar(tb[:, 33:66], ta[:, 0:33], -1.0, None, OP.mult)
                        v.tensor_copy(tb[:, 66:99], ta[:, 0:33])
                        v.drain()
                        wrap(v, tb[:, 0:99], tgs[:, 100:199])
                        v.drain().then_inc(s_v.h, 1)
                        v.wait_ge(s_a.h, ticks_a[5])
                        v.tensor_copy(ta[:, 0:32], ic[:, 34:66])
                        v.drain()
                        v.tensor_tensor(ta[:, 0:32], ta[:, 0:32],
                                        self.iotap[:, 1:2].broadcast_to([128, 32]), OP.mult)
                        v.drain()
                        v.tensor_scalar(ta[:, 0:32], ta[:, 0:32], 1.0 / 64, None, OP.mult)
                        v.drain()
                        v.tensor_scalar(self.scr[:, 10:11], self.iotap[:, 2:3], -8.0, None, OP.mult)
                        v.drain()
                        v.tensor_scalar(self.scr[:, 10:11], self.scr[:, 10:11], 1.0, None, OP.add)
                        v.drain()
                        v.tensor_scalar(self.scr[:, 11:12], self.iotap[:, 2:3], -1.0, None, OP.mult)
                        v.drain()
                        v.tensor_scalar(self.scr[:, 11:12], self.scr[:, 11:12], 0.25, None, OP.add)
                        v.drain()
                        v.tensor_tensor(ta[:, 0:32], ta[:, 0:32],
                                        self.scr[:, 10:11].broadcast_to([128, 32]), OP.mult)
                        v.drain()
                        v.tensor_tensor(ta[:, 0:32], ta[:, 0:32],
                                        self.scr[:, 11:12].broadcast_to([128, 32]), OP.add)
                        v.drain()
                        wrap(v, ta[:, 0:32], tgs[:, 200:232])
                        v.drain().then_inc(s_v.h, 1)

                    @b.scalar
                    def _(s):
                        s.wait_ge(s_v.h, ticks_v[0])
                        s.activation(m_im[:, :], tb[:, :], AF.Sin, scale=self.pi2[:, 1:2])
                        s.drain().then_inc(s_a.h, 1)
                        s.wait_ge(s_v.h, ticks_v[1])
                        s.activation(m_re[:, :], tb[:, :], AF.Sin, scale=self.pi2[:, 1:2])
                        s.drain().then_inc(s_a.h, 1)
                        s.wait_ge(s_v.h, ticks_v[2])
                        s.activation(n_im[:, :], tb[:, :], AF.Sin, scale=self.pi2[:, 1:2])
                        s.drain().then_inc(s_a.h, 1)
                        s.wait_ge(s_v.h, ticks_v[3])
                        s.activation(n_re[:, :], tb[:, :], AF.Sin, scale=self.pi2[:, 1:2])
                        s.drain().then_inc(s_a.h, 1)
                        s.drain().then_inc(s_a.h, 1)
                        s.wait_ge(s_v.h, ticks_v[5])
                        s.drain().then_inc(s_a.h, 1)
                        s.wait_ge(s_v.h, ticks_v[6])
                        s.activation(w64r[:, :], tb[:, 0:99], AF.Sin, scale=self.pi2[:, 1:2])
                        s.drain()
                        for q_ in range(4):
                            s.activation(w64b[32 * q_:32 * (q_ + 1), q_ * 99:(q_ + 1) * 99],
                                         tb[32 * q_:32 * (q_ + 1), 0:99], AF.Sin,
                                         scale=self.pi2[32 * q_:32 * (q_ + 1), 1:2])
                            s.drain()
                        s.drain().then_inc(s_a.h, 1)
                        s.wait_ge(s_v.h, ticks_v[7])
                        s.activation(g_tab[:, :], ta[:, 0:32], AF.Sin, scale=self.pi2[:, 1:2])
                        s.drain().then_inc(s_a.h, 1)

            tapw = self.dbg_tap("w64r", [128, 99], BF16)
            tapm = self.dbg_tap("m_re", [128, NK1 * 128], BF16)
            tapmi = self.dbg_tap("m_im", [128, NK1 * 128], BF16)
            tapn = self.dbg_tap("n_re", [128, NK1 * 128], BF16)
            tapg = self.dbg_tap("g_tab", [128, 32], BF16)
            if tapw is not None:
                with nc.Block() as b:
                    tdd = self.s_st.tick(16 * 5)
                    @b.sync
                    def _(sync):
                        sync.wait_ge(s_a.h, s_a.v)
                        sync.wait_ge(s_v.h, s_v.v)
                        sync.dma_start(out=tapw, in_=w64r[:, :]).then_inc(self.s_st.h, 16)
                        sync.dma_start(out=tapm, in_=m_re[:, :]).then_inc(self.s_st.h, 16)
                        sync.dma_start(out=tapmi, in_=m_im[:, :]).then_inc(self.s_st.h, 16)
                        sync.dma_start(out=tapn, in_=n_re[:, :]).then_inc(self.s_st.h, 16)
                        sync.dma_start(out=tapg, in_=g_tab[:, :]).then_inc(self.s_st.h, 16)
                        sync.wait_ge(self.s_st.h, tdd)

            with ExitStack() as es3:
                sb3 = lambda n, s, d=BF16: es3.enter_context(nc.sbuf_tensor(n, s, d))
                cbuf = sb3("hy_cbuf", [128, 99 * 64])
                pbuf = sb3("hy_pbuf", [128, 66 * 64])
                qbuf = sb3("hy_qbuf", [128, 66 * 64])
                ftile = sb3("hy_ftile", [128, 4224])
                fmt = [sb3(f"hy_fmt{i}", [128, 512]) for i in range(4)]
                y3st = sb3("hy_y3st", [128, 1024])
                OCTS = [(0, 8), (8, 8), (16, 8), (24, 8), (32, 1)]

                def v1_read_ap(dram, row_base):
                    a = dram.ap()
                    a4 = a.rearrange("(r1 q) t -> r1 q t", q=4)
                    a5 = a4.rearrange("r1 q (hi lo) -> r1 q hi lo", lo=128)
                    sl = a5[row_base // 4:row_base // 4 + 16, :, :, :]
                    return sl.transpose([1, 2, 0, 3])

                def fwd_block(ysrc, g, st, fmode, fg=None):
                    ynxt = ytime[(st + 1) % 2]
                    ld_y0 = sd[0].tick(16) if (fmode == 'P' and st == 0) or fmode == 'F' else None
                    ld_f = sd[1].tick(16) if fmode == 'P' else None
                    ld_g = sd[2].tick(16) if fmode == 'P' else None
                    v_s1 = [s_v.tick() for _ in range(8)]
                    a_s1 = [s_a.tick() for _ in range(8)]
                    t_s1 = [s_t.tick() for _ in range(16)]
                    t_s3 = [s_t.tick() for _ in range(len(OCTS))]
                    if fmode == 'F':
                        v_fm = [s_v.tick() for _ in range(len(OCTS))]
                        st_f = sd[3].tick(16)
                    else:
                        a_ev = [s_a.tick() for _ in range(len(OCTS))]
                        v_fm = [s_v.tick() for _ in range(len(OCTS))]
                        t_i1 = [s_t.tick() for _ in range(len(OCTS))]
                        vq = [s_v.tick() for _ in range(len(OCTS))]
                        aq = [s_a.tick() for _ in range(len(OCTS))]
                        t_T = [s_t.tick() for _ in range(8)]
                        v_T = [s_v.tick() for _ in range(8)]
                        t_i2 = [s_t.tick() for _ in range(4)]
                        v_g = [s_v.tick() for _ in range(4)]

                    with nc.Block() as b:
                        @b.sync
                        def _(sync):
                            if fmode == 'F':
                                sync.dma_start(out=ytime[0][:, :],
                                               in_=v1_read_ap(self.f_dram, fg * 64)).then_inc(sd[0].h, 16)
                                sync.wait_ge(s_v.h, v_fm[-1])
                                sync.dma_start(out=self.fsh_dram.ap()[:, fg * 64 * 66:(fg + 1) * 64 * 66],
                                               in_=pbuf[:, :]).then_inc(sd[3].h, 16)
                                sync.wait_ge(sd[3].h, st_f)
                            else:
                                if st == 0:
                                    sync.dma_start(out=ytime[0][:, :],
                                                   in_=v1_read_ap(self.xp_dram, g * 64)).then_inc(sd[0].h, 16)
                                sync.dma_start(out=ftile[:, :],
                                               in_=self.fall_dram.ap()[(g // 2) * 128:(g // 2) * 128 + 128,
                                                                       (st * 128 + (g % 2) * 64) * 66:
                                                                       (st * 128 + (g % 2) * 64 + 64) * 66]).then_inc(sd[1].h, 16)
                                sync.dma_start(out=xpg[:, :],
                                               in_=v1_read_ap(self.xp_dram, (st + 1) * D + g * 64)).then_inc(sd[2].h, 16)

                        @b.tensor
                        def _(t):
                            ysrc_t = ytime[0] if (fmode == 'F' or st == 0) else ytime[st % 2]
                            for bb in range(16):
                                if bb == 0:
                                    if fmode == 'F' or st == 0:
                                        t.wait_ge(sd[0].h, ld_y0)
                                if bb >= 4:
                                    if (bb - 4) % 2 == 0:
                                        t.wait_ge(s_v.h, v_s1[(bb - 4) // 2])
                                    else:
                                        t.wait_ge(s_a.h, a_s1[(bb - 4) // 2])
                                mm = t.matmul(s1ps(bb)[:, 0:396],
                                              ysrc_t[:, bb * 128:(bb + 1) * 128],
                                              w64b[:, :],
                                              start=True, stop=True)
                                mm.then_inc(s_t.h, 1)
                            for j, (k0, kn) in enumerate(OCTS):
                                t.wait_ge(s_v.h, v_s1[7])
                                t.wait_ge(s_a.h, a_s1[7])
                                if fmode == 'F':
                                    if j >= 2:
                                        t.wait_ge(s_v.h, v_fm[j - 2])
                                else:
                                    if j >= 2:
                                        t.wait_ge(s_a.h, a_ev[j - 2])
                                xre, xim = psX[2 * (j % 2)], psX[2 * (j % 2) + 1]
                                for kk in range(kn):
                                    k1 = k0 + kk
                                    c3 = cbuf[:, :].rearrange("p (d s) -> p d s", s=99)
                                    are = c3[:, :, k1]
                                    aim = c3[:, :, 33 + k1]
                                    aimn = c3[:, :, 66 + k1]
                                    mt_re = m_re[:, k1 * 128:(k1 + 1) * 128]
                                    mt_im = m_im[:, k1 * 128:(k1 + 1) * 128]
                                    co = kk * 64
                                    t.matmul(xre[:, co:co + 64], mt_re, are, start=True, stop=False)
                                    t.matmul(xim[:, co:co + 64], mt_im, are, start=True, stop=False)
                                    t.matmul(xim[:, co:co + 64], mt_re, aim, start=False, stop=True)
                                    mm = t.matmul(xre[:, co:co + 64], mt_im, aimn, start=False, stop=True)
                                mm.then_inc(s_t.h, 1)
                            if fmode == 'P':
                                for j, (k0, kn) in enumerate(OCTS):
                                    t.wait_ge(s_v.h, v_fm[j])
                                    if j >= 2:
                                        t.wait_ge(s_v.h, vq[j - 2])
                                        t.wait_ge(s_a.h, aq[j - 2])
                                    qre, qim = (psQ[0], psQ[1]) if j % 2 == 0 else (psX[2], psX[3])
                                    if j == 0:
                                        t.wait_ge(s_a.h, a_ev[-1])
                                        t.wait_ge(s_v.h, v_fm[-1])
                                    for kk in range(kn):
                                        k1 = k0 + kk
                                        pre = pbuf[:, k1 * 64:k1 * 64 + 64]
                                        pim = pbuf[:, 2112 + k1 * 64:2112 + k1 * 64 + 64]
                                        nt_re = n_re[:, k1 * 128:(k1 + 1) * 128]
                                        nt_im = n_im[:, k1 * 128:(k1 + 1) * 128]
                                        nt_imn = n_imn[:, k1 * 128:(k1 + 1) * 128]
                                        co = kk * 64
                                        t.matmul(qre[:, co:co + 64], nt_re, pre, start=True, stop=False)
                                        t.matmul(qim[:, co:co + 64], nt_im, pre, start=True, stop=False)
                                        t.matmul(qim[:, co:co + 64], nt_re, pim, start=False, stop=True)
                                        mm = t.matmul(qre[:, co:co + 64], nt_imn, pim, start=False, stop=True)
                                    mm.then_inc(s_t.h, 1)
                                for tb8 in range(8):
                                    t.wait_ge(s_v.h, vq[-1])
                                    t.wait_ge(s_a.h, aq[-1])
                                    if tb8 >= 2:
                                        t.wait_ge(s_v.h, v_T[tb8 - 2])
                                    pstc = [psT, psT2][tb8 % 2]
                                    for ss in range(8):
                                        d = tb8 * 8 + ss
                                        mm = t.transpose(pstc[0:66, ss * 128:(ss + 1) * 128],
                                                         qbuf[:, d * 66:(d + 1) * 66], self.ident[:, :])
                                    mm.then_inc(s_t.h, 1)
                                qt4 = qtb[0:66, :].rearrange("p (d m) -> p d m", m=128)
                                for gg in range(4):
                                    t.wait_ge(s_v.h, v_T[(gg * 16 + 15) // 8])
                                    if gg >= 2:
                                        t.wait_ge(s_v.h, v_g[gg - 2])
                                    psic = [psI2, psI2b][gg % 2]
                                    for q in range(4):
                                        rhs4 = qt4[:, gg * 16 + q:gg * 16 + q + 13:4, :]
                                        mm = t.matmul(psic[32 * q:32 * (q + 1), 0:512].rearrange("p (d4 m) -> p d4 m", m=128),
                                                      g_tab[0:66, :],
                                                      rhs4,
                                                      start=True, stop=True, tile_position=(0, 32 * q))
                                    mm.then_inc(s_t.h, 1)

                        @b.vector
                        def _(v):
                            for bb in range(0, 16, 2):
                                v.wait_ge(s_t.h, t_s1[bb])
                                v.tensor_copy(cbuf[:, bb * 396:(bb + 1) * 396], s1ps(bb)[:, 0:396]).then_inc(s_v.h, 1)
                            for j, (k0, kn) in enumerate(OCTS):
                                nn = kn * 64
                                xre, xim = psX[2 * (j % 2)], psX[2 * (j % 2) + 1]
                                if fmode == 'F':
                                    v.wait_ge(s_t.h, t_s3[j])
                                    v.tensor_copy(pbuf[:, k0 * 64:k0 * 64 + nn], xre[:, 0:nn])
                                    v.tensor_copy(pbuf[:, 2112 + k0 * 64:2112 + k0 * 64 + nn], xim[:, 0:nn]).then_inc(s_v.h, 1)
                                else:
                                    if j == 0:
                                        v.wait_ge(sd[1].h, ld_f)
                                    v.wait_ge(s_a.h, a_ev[j])
                                    fre = ftile[:, k0 * 64:k0 * 64 + nn]
                                    fim = ftile[:, 2112 + k0 * 64:2112 + k0 * 64 + nn]
                                    fA = fmt[0][:, 0:nn]
                                    fB = fmt[1][:, 0:nn]
                                    t1 = fmt[2][:, 0:nn]
                                    t2 = fmt[3][:, 0:nn]
                                    pre_dst = pbuf[:, k0 * 64:k0 * 64 + nn]
                                    pim_dst = pbuf[:, 2112 + k0 * 64:2112 + k0 * 64 + nn]
                                    v.tensor_tensor(t1, fA, fre, OP.mult)
                                    v.tensor_tensor(t2, fB, fim, OP.mult)
                                    v.drain()
                                    v.tensor_tensor(pre_dst, t1, t2, OP.subtract)
                                    v.drain()
                                    v.tensor_tensor(t1, fA, fim, OP.mult)
                                    v.tensor_tensor(t2, fB, fre, OP.mult)
                                    v.drain()
                                    v.tensor_tensor(pim_dst, t1, t2, OP.add).then_inc(s_v.h, 1)
                            if fmode == 'P':
                                for j, (k0, kn) in enumerate(OCTS):
                                    qre, qim = (psQ[0], psQ[1]) if j % 2 == 0 else (psX[2], psX[3])
                                    v.wait_ge(s_t.h, t_i1[j])
                                    v.tensor_copy(
                                        qbuf[:, :].rearrange("p (d s) -> p d s", s=66)[:, :, 2 * k0:2 * (k0 + kn):2],
                                        qre[:, :].rearrange("p (k d) -> p k d", d=64)[:, 0:kn, :].transpose([0, 2, 1])).then_inc(s_v.h, 1)
                                for tb8 in range(8):
                                    v.wait_ge(s_t.h, t_T[tb8])
                                    v.tensor_copy(qtb[0:66, tb8 * 1024:(tb8 + 1) * 1024],
                                                  [psT, psT2][tb8 % 2][0:66, 0:1024]).then_inc(s_v.h, 1)
                                for gg in range(4):
                                    v.wait_ge(s_t.h, t_i2[gg])
                                    v.wait_ge(sd[2].h, ld_g)
                                    v.tensor_tensor(ynxt[:, gg * 512:(gg + 1) * 512],
                                                    [psI2, psI2b][gg % 2][:, :],
                                                    xpg[:, gg * 512:(gg + 1) * 512], OP.mult).then_inc(s_v.h, 1)

                        @b.scalar
                        def _(s):
                            for bb in range(1, 16, 2):
                                s.wait_ge(s_t.h, t_s1[bb])
                                s.activation(cbuf[:, bb * 396:(bb + 1) * 396], s1ps(bb)[:, 0:396], AF.Identity).then_inc(s_a.h, 1)
                            if fmode == 'P':
                                for j, (k0, kn) in enumerate(OCTS):
                                    s.wait_ge(s_t.h, t_s3[j])
                                    if j >= 1:
                                        s.wait_ge(s_v.h, v_fm[j - 1])
                                    xre, xim = psX[2 * (j % 2)], psX[2 * (j % 2) + 1]
                                    s.activation(fmt[0][:, 0:kn * 64], xre[:, 0:kn * 64], AF.Identity)
                                    s.activation(fmt[1][:, 0:kn * 64], xim[:, 0:kn * 64], AF.Identity).then_inc(s_a.h, 1)
                                for j in range(len(OCTS)):
                                    s.wait_ge(s_t.h, t_i1[j])
                                    qre, qim = (psQ[0], psQ[1]) if j % 2 == 0 else (psX[2], psX[3])
                                    k0, kn = OCTS[j]
                                    s.activation(
                                        qbuf[:, :].rearrange("p (d s) -> p d s", s=66)[:, :, 2 * k0 + 1:2 * (k0 + kn):2],
                                        qim[:, :].rearrange("p (k d) -> p k d", d=64)[:, 0:kn, :].transpose([0, 2, 1]),
                                        AF.Identity).then_inc(s_a.h, 1)

                if getattr(self, 'hy_upto', 3) < 1:
                    return
                for fg in range(min(6, getattr(self, 'hy_fblocks', 6))):
                    fwd_block(None, 0, 0, 'F', fg=fg)
                if getattr(self, 'hy_upto', 3) < 2:
                    return
                cc_t = s_cc.tick(1 if self.cores > 1 else 16)
                with nc.Block() as b:
                    if self.cores > 1 and not getattr(self, 'force_no_cc', False):
                        @b.gpsimd
                        def _(g):
                            g.collective_compute(
                                "AllGather", OP.bypass,
                                replica_groups=[list(range(self.cores))],
                                ins=[self.fsh_dram.ap().opt()],
                                outs=[self.fall_dram.ap().opt()],
                            ).then_inc(s_cc.h, 1)
                    else:
                        @b.gpsimd
                        def _(g):
                            g.dma_start(out=self.fall_dram.ap()[0:128, :],
                                        in_=self.fsh_dram.ap()[:, :]).then_inc(s_cc.h, 16)
                    @b.sync
                    def _(sync):
                        sync.wait_ge(s_cc.h, cc_t)

                if getattr(self, 'hy_upto', 3) < 3:
                    return
                for g in range(self.hy_groups):
                    for st in range(3):
                        fwd_block(None, g, st, 'P')
                    t_y3 = [s_t.tick() for _ in range(2)]
                    v_y3 = [s_v.tick() for _ in range(2)]
                    st_y3 = [sd[4].tick(16) for _ in range(2)]
                    with nc.Block() as b:
                        @b.tensor
                        def _(t):
                            for hh in range(2):
                                for d4 in range(8):
                                    dd4 = hh * 8 + d4
                                    mm = t.transpose([psT, psT2][hh][:, d4 * 128:(d4 + 1) * 128],
                                                     ytime[1][:, dd4 * 128:(dd4 + 1) * 128],
                                                     self.ident[:, :])
                                mm.then_inc(s_t.h, 1)
                        @b.vector
                        def _(v):
                            for hh in range(2):
                                v.wait_ge(s_t.h, t_y3[hh])
                                if hh == 1:
                                    v.wait_ge(sd[4].h, st_y3[0])
                                v.tensor_copy(
                                    y3st[:, :].rearrange("p (m d4 q) -> p d4 q m", d4=8, q=4),
                                    [psT, psT2][hh][:, 0:1024].rearrange("p (d4 q m) -> p d4 q m", d4=8, q=4)).then_inc(s_v.h, 1)
                        @b.sync
                        def _(sync):
                            for hh in range(2):
                                sync.wait_ge(s_v.h, v_y3[hh])
                                base = g * 64 + hh * 32
                                sync.dma_start(
                                    out=self.y3_dram.ap().rearrange("p (m dcol) -> p m dcol", dcol=1024)[:, :, base:base + 32],
                                    in_=y3st[:, :]).then_inc(sd[4].h, 16)
                            sync.wait_ge(sd[4].h, st_y3[1])
        self.hy_v = self.s_v.v

    def p_ffn(self):
        nc = self.nc
        xs3 = self.xstream.rearrange("p (c t) -> p c t", t=TS)

        def xsl(ck, t0, t1, shift):
            off = 0 if (shift and ck >= 4) else 1
            return xs3[:, ck, t0 + off:t1 + off]

        with ExitStack() as es:
            sb = lambda n, s, d=BF16: es.enter_context(nc.sbuf_tensor(n, s, d))

            class NS:
                pass
            B = NS()
            B.xsl = xsl
            B.selb = sb("ff_sel", [128, 128])
            B.selT = sb("ff_selT", [16, 128 * 8])
            B.selq = sb("ff_selq", [16, 512])
            B.qs_col = sb("ff_qscol", [128, 8], F32)
            B.fmask = sb("ff_fmask", [128, 256], F32)
            B.onecol = sb("ff_onecol", [128, 128])
            B.onesr = sb("ff_onesr", [16, 512])
            B.lnrowg = sb("ff_lnrowg", [2, 1024])
            B.lnrowb = sb("ff_lnrowb", [2, 1024])
            B.b2row = sb("ff_b2row", [128, 64], F32)
            B.msnap = sb("ff_msnap", [128, 4])
            with nc.named_scope("ffn_prolog"):
                self._ffn_prologue(B)
            for l in range(getattr(self, "ffn_layers", 2)):
                with nc.named_scope(f"qkv{l}"):
                    self._ffn_layer_qkv(l, B)
                    self.barrier()
                with nc.named_scope(f"attn{l}"):
                    self._ffn_layer_attn(l, B)
                with nc.named_scope(f"mlp{l}"):
                    self._ffn_layer_mlp(l, B)
                self.barrier()
        tap = self.dbg_tap("ffn_x", [128, NCH * TS], BF16)
        if tap is not None:
            td = self.s_ld.tick(16)
            with nc.Block() as b:
                @b.sync
                def _(sync):
                    sync.dma_start(out=tap[:, :], in_=self.xstream[:, :]).then_inc(self.s_ld.h, 16)
                    sync.wait_ge(self.s_ld.h, td)

    def _ffn_prologue(self, B):
        nc = self.nc
        s_t, s_v, s_a, s_p = self.s_t, self.s_v, self.s_a, self.s_p
        sd = self.s_dma
        with ExitStack() as es2:
            sb2 = lambda n, s, d=F32: es2.enter_context(nc.sbuf_tensor(n, s, d))
            ua = sb2("fp_ua", [128, T])
            ub = sb2("fp_ub", [128, T])
            ug = sb2("fp_ug", [128, T])
            xsb = sb2("fp_xs", [128, T])
            plb = sb2("fp_plb", [128, 4 * T], BF16)
            pst = es2.enter_context(nc.psum_tensor("fp_pst", [128, 128], BF16))
            scr = self.scr

            def wrap(v, fr, gg):
                v.tensor_copy(gg.bitcast(I32), fr)
                v.drain()
                v.tensor_copy(gg, gg.bitcast(I32))
                v.drain()
                v.tensor_tensor(fr, fr, gg, OP.subtract)
                v.drain()
                v.tensor_scalar(gg, fr, 0.5, None, OP.is_gt)
                v.drain()
                v.tensor_tensor(fr, fr, gg, OP.subtract)
                v.drain()
                v.tensor_scalar(gg, fr, -0.5, None, OP.is_lt)
                v.drain()
                v.tensor_tensor(fr, fr, gg, OP.add)
                v.drain()

            NV, NA, NP, NT_ = 5, 5, 2, 8
            tv = [s_v.tick() for _ in range(NV)]
            ta_ = [s_a.tick() for _ in range(NA)]
            tp = [s_p.tick() for _ in range(NP)]
            tt_ = [s_t.tick() for _ in range(NT_)]
            tv2 = [s_v.tick() for _ in range(NT_)]
            st_pl = sd[0].tick(16)
            with nc.Block() as b:
                @b.gpsimd
                def _(g):
                    g.iota(ua[:, :].bitcast(I32)[:, 0:T], [[1, T]], channel_multiplier=0).then_inc(s_p.h, 1)
                    g.wait_ge(s_v.h, tv[3])
                    g.iota(ua[:, :].bitcast(I32)[:, 0:256], [[0, 2], [1, 128]], channel_multiplier=0)
                    g.iota(ua[:, :].bitcast(I32)[:, 256:272], [[1, 16]], channel_multiplier=0)
                    g.iota(ua[:, :].bitcast(I32)[:, 928:1440], [[1, 8], [0, 64]], channel_multiplier=0).then_inc(s_p.h, 1)

                @b.vector
                def _(v):
                    ic = scr[:, :].bitcast(I32)
                    v.tensor_copy(ic[:, 30:31], self.iotap[:, 0:1])
                    v.drain()
                    v.tensor_scalar(ic[:, 31:32], ic[:, 30:31], 5, None, OP.arith_shift_right)
                    v.drain()
                    v.tensor_copy(scr[:, 19:20], ic[:, 31:32])
                    v.drain()
                    v.tensor_scalar(ic[:, 31:32], ic[:, 31:32], 5, None, OP.arith_shift_left)
                    v.drain()
                    v.tensor_tensor(ic[:, 31:32], ic[:, 30:31], ic[:, 31:32], OP.subtract)
                    v.drain()
                    v.tensor_copy(scr[:, 12:13], ic[:, 31:32])
                    v.drain()
                    v.tensor_scalar(scr[:, 13:14], scr[:, 12:13], -1.0 / 32, None, OP.mult)
                    v.drain()
                    v.tensor_scalar(scr[:, 16:17], scr[:, 12:13], 10.0, None, OP.mult)
                    v.drain()
                    v.tensor_scalar(scr[:, 16:17], scr[:, 16:17], 128.0, None, OP.add)
                    v.drain()
                    v.tensor_scalar(scr[:, 16:17], scr[:, 16:17], 1.0 / 448, None, OP.mult)
                    v.drain()
                    v.wait_ge(s_p.h, tp[0])
                    v.tensor_copy(ug[:, :], ua[:, :].bitcast(I32)[:, 0:T])
                    v.drain().then_inc(s_v.h, 1)
                    v.wait_ge(s_a.h, ta_[0])
                    v.tensor_tensor(scr[:, 15:16], scr[:, 14:15], self.r2pi[:, 0:1], OP.mult)
                    v.drain().then_inc(s_v.h, 1)
                    v.wait_ge(s_a.h, ta_[1])
                    v.tensor_scalar(ub[:, 0:T], ua[:, 0:T], 0.25, None, OP.add)
                    v.drain()
                    wrap(v, ub[:, 0:T], xsb[:, 0:T])
                    wrap(v, ua[:, 0:T], xsb[:, 0:T])
                    v.drain().then_inc(s_v.h, 1)
                    v.wait_ge(s_a.h, ta_[3])
                    v.tensor_tensor(plb[:, 0 * T:1 * T], ub[:, 0:T], xsb[:, 0:T], OP.mult)
                    v.drain()
                    v.tensor_tensor(plb[:, 1 * T:2 * T], ua[:, 0:T], xsb[:, 0:T], OP.mult)
                    v.drain()
                    v.reciprocal(xsb[:, 0:T], xsb[:, 0:T])
                    v.drain()
                    v.tensor_tensor(plb[:, 2 * T:3 * T], ub[:, 0:T], xsb[:, 0:T], OP.mult)
                    v.drain()
                    v.tensor_tensor(plb[:, 3 * T:4 * T], ua[:, 0:T], xsb[:, 0:T], OP.mult)
                    v.drain().then_inc(s_v.h, 1)
                    v.wait_ge(s_p.h, tp[1])
                    ia = ua[:, :].bitcast(I32)
                    v.tensor_copy(ub[:, 0:272], ia[:, 0:272])
                    v.drain()
                    v.tensor_tensor(ub[:, 512:640], ub[:, 0:128],
                                    self.iotap[:, 0:1].broadcast_to([128, 128]), OP.is_ge)
                    v.drain()
                    v.tensor_tensor(ub[:, 640:768], ub[:, 128:256],
                                    self.iotap[:, 0:1].broadcast_to([128, 128]), OP.is_le)
                    v.drain()
                    v.tensor_scalar(ub[:, 512:768], ub[:, 512:768], -1.0, None, OP.add)
                    v.drain()
                    v.tensor_scalar(ub[:, 512:768], ub[:, 512:768], 1e30, None, OP.mult)
                    v.drain()
                    v.tensor_copy(B.fmask[:, :], ub[:, 512:768])
                    v.drain()
                    for c in range(8):
                        base = [0.0, 0.0, 4.0, 4.0, 8.0, 8.0, 12.0, 12.0][c]
                        v.tensor_scalar(scr[:, 20:21], scr[:, 19:20], base, None, OP.add)
                        v.drain()
                        v.tensor_tensor(ub[:, 800 + c * 16:816 + c * 16], ub[:, 256:272],
                                        scr[:, 20:21].broadcast_to([128, 16]), OP.is_equal)
                        v.drain()
                    v.tensor_copy(B.selb[:, :], ub[:, 800:928])
                    v.tensor_copy(ub[:, 928:1440], ua[:, :].bitcast(I32)[:, 928:1440])
                    v.drain()
                    v.tensor_tensor(ub[:, 928:1440], ub[:, 928:1440],
                                    self.iotap[:, 0:1].broadcast_to([128, 512]), OP.is_equal)
                    v.drain()
                    v.tensor_copy(B.selq[0:16, :], ub[0:16, 928:1440])
                    v.drain()
                    v.memset(B.onecol[:, :], 0.0)
                    v.drain()
                    v.memset(B.onecol[:, 0:1], 1.0)
                    v.memset(B.onecol[:, 96:97], 1.0)
                    v.memset(B.onesr[:, :], 0.0)
                    v.drain()
                    v.memset(B.onesr[0:1, :], 1.0)
                    v.drain().then_inc(s_v.h, 1)
                    for c in range(8):
                        v.wait_ge(s_t.h, tt_[c])
                        v.tensor_copy(B.selT[0:16, c * 128:(c + 1) * 128], pst[0:16, 0:128])
                        v.drain().then_inc(s_v.h, 1)

                @b.scalar
                def _(s):
                    s.wait_ge(s_v.h, tv[0])
                    s.activation(scr[:, 14:15], scr[:, 13:14], AF.Exp, scale=self.iotap[:, 3:4])
                    s.drain()
                    s.activation(scr[:, 21:22], scr[:, 16:17], AF.Ln)
                    s.drain().then_inc(s_a.h, 1)
                    s.wait_ge(s_v.h, tv[1])
                    s.activation(ua[:, 0:T], ug[:, 0:T], AF.Identity, scale=scr[:, 15:16])
                    s.drain().then_inc(s_a.h, 1)
                    s.wait_ge(s_v.h, tv[2])
                    s.activation(ub[:, 0:T], ub[:, 0:T], AF.Sin, scale=self.pi2[:, 1:2])
                    s.drain()
                    s.activation(ua[:, 0:T], ua[:, 0:T], AF.Sin, scale=self.pi2[:, 1:2])
                    s.drain().then_inc(s_a.h, 1)
                    s.activation(xsb[:, 0:T], ug[:, 0:T], AF.Identity, bias=-8.0, scale=0.00390625)
                    s.drain()
                    s.activation(xsb[:, 0:T], xsb[:, 0:T], AF.Exp, scale=scr[:, 21:22])
                    s.drain().then_inc(s_a.h, 1)
                    s.drain().then_inc(s_a.h, 1)

                @b.tensor
                def _(t):
                    t.wait_ge(s_v.h, tv[4])
                    for c in range(8):
                        if c >= 1:
                            t.wait_ge(s_v.h, tv2[c - 1])
                        t.transpose(pst[0:16, 0:128], B.selb[:, c * 16:(c + 1) * 16],
                                    self.ident[:, :]).then_inc(s_t.h, 1)

                @b.sync
                def _(sync):
                    sync.wait_ge(s_v.h, tv[3])
                    sync.dma_start(out=self.pl_dram.ap()[:, :], in_=plb[:, :]).then_inc(sd[0].h, 16)
                    sync.wait_ge(sd[0].h, st_pl)

    def _ffn_layer_qkv(self, l, B):
        nc = self.nc
        s_t, s_v, s_a, s_p = self.s_t, self.s_v, self.s_a, self.s_p
        sd = self.s_dma
        with ExitStack() as es:
            sb = lambda n, s, d=BF16: es.enter_context(nc.sbuf_tensor(f"l{l}" + n, s, d))
            ps = lambda n, s, d=F32: es.enter_context(nc.psum_tensor(f"l{l}" + n, s, d))
            wqkvb = sb("fq_wqkv", [128, 8 * 1536])
            stg = sb("fq_stg", [128, 1536], F32)
            bstg = sb("fq_bstg", [52, 128], F32)
            bstgb = sb("fq_bstgb", [52, 128])
            plg = sb("fq_plg", [128, 4096])
            qg = sb("fq_qg", [128, 4096])
            kg = sb("fq_kg", [128, 4096])
            vg = sb("fq_vg", [128, 4096])
            sqst = sb("fq_sq", [128, 2048])
            rsd = sb("fq_rsd", [16, 2048])
            rpl = sb("fq_rpl", [128, 1024])
            tmA = sb("fq_tA", [128, 1024])
            tmB = sb("fq_tB", [128, 1024])
            ps_a = ps("fq_psa", [128, 512])
            ps_b = ps("fq_psb", [128, 512])
            ps_v = ps("fq_psv", [128, 512])
            ps_pl = ps("fq_pspl", [128, 512])
            pbt = es.enter_context(nc.psum_tensor(f"l{l}fq_pbt", [128, 52], BF16))

            w = self.din["wqkv"][l]
            for c in range(8):
                ld = sd[0].tick(16 * 9)
                cst = s_v.tick()
                with nc.Block() as b:
                    @b.sync
                    def _(sync):
                        rows = w[c * 128:(c + 1) * 128, :]
                        for part in range(2):
                            src4 = rows[:, part * 512:(part + 1) * 512].rearrange(
                                "p (g hh e j) -> p g hh e j", g=2, hh=4, e=2)
                            for g2 in range(2):
                                for e in range(2):
                                    o0 = part * 512 + (g2 * 2 + e) * 128
                                    dst = stg[:, o0:o0 + 128].rearrange("p (hh j) -> p hh j", hh=4)
                                    sync.dma_start(out=dst, in_=src4[:, g2, :, e, :]).then_inc(sd[0].h, 16)
                        sync.dma_start(out=stg[:, 1024:1536], in_=rows[:, 1024:1536]).then_inc(sd[0].h, 16)

                    @b.vector
                    def _(v):
                        v.wait_ge(sd[0].h, ld)
                        v.tensor_copy(wqkvb[:, c * 1536:(c + 1) * 1536], stg[:, :])
                        v.drain().then_inc(s_v.h, 1)

            vz = s_v.tick()
            with nc.Block() as b:
                @b.vector
                def _(v):
                    v.memset(bstg[32:52, :], 0.0)
                    v.drain().then_inc(s_v.h, 1)
            ldq = sd[0].tick(16 * 16)
            ldl = sd[1].tick(16 * 2)
            ldb = sd[2].tick(16 * 4)
            tT = s_t.tick()
            tvb = s_v.tick()
            tvc = s_v.tick()
            with nc.Block() as b:
                @b.sync
                def _(sync):
                    qs = self.din["q_scale"][l]
                    ks = self.din["k_scale"][l]
                    for i, srcv in enumerate([qs[0:32], qs[32:64], ks[0:32], ks[32:64]]):
                        for r in range(4):
                            sync.dma_start(out=B.qs_col[32 * r:32 * r + 32, i:i + 1],
                                           in_=srcv.rearrange("(j o) -> j o", o=1)).then_inc(sd[0].h, 16)
                    sync.dma_start(out=stg[0:1, 0:1024], in_=self.din["ln_g"][l].rearrange("(o d) -> o d", o=1)).then_inc(sd[1].h, 16)
                    sync.dma_start(out=stg[32:33, 0:1024], in_=self.din["ln_b"][l].rearrange("(o d) -> o d", o=1)).then_inc(sd[1].h, 16)
                    b1 = self.din["b1"][l]
                    sync.dma_start(out=bstg[0:22, :], in_=b1[0:2816].rearrange("(a c) -> a c", c=128)).then_inc(sd[2].h, 16)
                    sync.dma_start(out=bstg[22:43, :], in_=b1[2730:5418].rearrange("(a c) -> a c", c=128)).then_inc(sd[2].h, 16)
                    sync.dma_start(out=bstg[43:44, 0:42], in_=b1[5418:5460].rearrange("(o a) -> o a", o=1)).then_inc(sd[2].h, 16)
                    sync.dma_start(out=bstg[44:52, :], in_=self.din["b2"][l].rearrange("(a c) -> a c", c=128)).then_inc(sd[2].h, 16)

                @b.vector
                def _(v):
                    v.wait_ge(sd[1].h, ldl)
                    v.tensor_copy(B.lnrowg[0:1, :], stg[0:1, 0:1024])
                    v.drain()
                    v.tensor_copy(B.lnrowb[0:1, :], stg[32:33, 0:1024])
                    v.drain()
                    v.wait_ge(sd[2].h, ldb)
                    v.tensor_copy(bstgb[:, :], bstg[:, :])
                    v.drain().then_inc(s_v.h, 1)
                    v.wait_ge(s_t.h, tT)
                    v.tensor_copy(B.b2row[:, 0:52], pbt[:, 0:52])
                    v.drain().then_inc(s_v.h, 1)

                @b.tensor
                def _(t):
                    t.wait_ge(s_v.h, tvb)
                    t.transpose(pbt[:, 0:52], bstgb[:, :], self.ident[0:52, 0:52]).then_inc(s_t.h, 1)

            prev_st = {"q": 0, "k": 0, "v": 0}
            for g in range(4):
                t0 = g * 1024
                ld_pl = sd[2].tick(16)
                t_q, t_k, t_v = [], [], []
                t_s = []
                t_pl = []
                for _ in range(4):
                    t_q.append(s_t.tick())
                t_s.append(s_t.tick())
                for _ in range(4):
                    t_q.append(s_t.tick())
                t_s.append(s_t.tick())
                for _ in range(4):
                    t_k.append(s_t.tick())
                t_s.append(s_t.tick())
                for _ in range(4):
                    t_k.append(s_t.tick())
                t_s.append(s_t.tick())
                for _ in range(8):
                    t_v.append(s_t.tick())
                for _ in range(8):
                    t_pl.append(s_t.tick())
                a_u = []
                a_ln = []
                for ph in range(4):
                    for _ in range(4):
                        a_u.append(s_a.tick())
                    a_ln.append(s_a.tick())
                v_cl = [s_v.tick() for _ in range(4)]
                v_vv = [s_v.tick() for _ in range(8)]
                v_pl = []
                v_rot = []
                for pr4 in range(4):
                    v_pl.append(s_v.tick())
                    v_pl.append(s_v.tick())
                    v_rot.append(s_v.tick())
                st_v = sd[3].tick(16)
                st_q = [sd[6].tick(16) for _ in range(2)]
                st_k = [sd[7].tick(16) for _ in range(2)]

                with nc.Block() as b:
                    @b.tensor
                    def _(t):
                        for ph in range(4):
                            kind = "q" if ph < 2 else "k"
                            tt = ph % 2
                            base = 0 if kind == "q" else 512
                            tks = t_q if kind == "q" else t_k
                            for c in range(4):
                                i = (ph % 2) * 4 + c if kind == "q" else (ph - 2) * 4 + c
                                ui = ph * 4 + c
                                pp = ps_a if ui % 2 == 0 else ps_b
                                if ui >= 2:
                                    t.wait_ge(s_a.h, a_u[ui - 2])
                                for ck in range(8):
                                    mm = t.matmul(pp[:, :],
                                                  wqkvb[:, ck * 1536 + base + c * 128:ck * 1536 + base + c * 128 + 128],
                                                  B.xsl(ck, t0 + tt * 512, t0 + (tt + 1) * 512, True),
                                                  start=(ck == 0), stop=(ck == 7))
                                mm.then_inc(s_t.h, 1)
                            t.wait_ge(s_a.h, a_u[ph * 4 + 3])
                            if ph >= 1:
                                t.wait_ge(s_v.h, v_cl[ph - 1])
                            for c in range(4):
                                mm = t.matmul(ps_pl[0:16, 0:512],
                                              B.selb[:, c * 16:(c + 1) * 16],
                                              sqst[:, c * 512:(c + 1) * 512],
                                              start=(c == 0), stop=(c == 3))
                            mm.then_inc(s_t.h, 1)
                        for vt in range(8):
                            pp = ps_v if vt % 2 == 0 else ps_a
                            if vt == 1:
                                t.wait_ge(s_a.h, a_u[14])
                            if vt >= 2:
                                t.wait_ge(s_v.h, v_vv[vt - 2])
                            for ck in range(8):
                                mm = t.matmul(pp[:, 0:512],
                                              B.xsl(ck, t0 + vt * 128, t0 + (vt + 1) * 128, True),
                                              wqkvb[:, ck * 1536 + 1024:ck * 1536 + 1536],
                                              start=(ck == 0), stop=(ck == 7))
                            mm.then_inc(s_t.h, 1)
                        for idx in range(8):
                            kind = idx // 4
                            pr = (idx % 4) // 2
                            tt = idx % 2
                            if idx == 0:
                                t.wait_ge(s_v.h, v_cl[3])
                            else:
                                t.wait_ge(s_v.h, v_pl[idx - 1])
                            t.wait_ge(s_a.h, a_ln[kind * 2 + tt])
                            mm = t.matmul(ps_pl[:, 0:512],
                                          B.selT[0:16, (2 * pr) * 128:(2 * pr + 1) * 128],
                                          rsd[0:16, kind * 1024 + tt * 512:kind * 1024 + (tt + 1) * 512],
                                          start=True, stop=True)
                            mm.then_inc(s_t.h, 1)

                    @b.scalar
                    def _(s):
                        for ph in range(4):
                            kind = "q" if ph < 2 else "k"
                            tt = ph % 2
                            dstb = qg if kind == "q" else kg
                            tks = t_q if kind == "q" else t_k
                            for c in range(4):
                                ui = ph * 4 + c
                                pp = ps_a if ui % 2 == 0 else ps_b
                                if c == 0 and ph >= 1:
                                    s.wait_ge(s_t.h, t_s[ph - 1])
                                if c == 0 and tt == 0 and g >= 1:
                                    s.wait_ge(sd[6 if kind == "q" else 7].h, prev_st[kind])
                                if c == 0 and ph == 0 and g == 0:
                                    s.wait_ge(sd[0].h, ldq)
                                s.wait_ge(s_t.h, tks[tt * 4 + c])
                                s.activation(sqst[:, c * 512:(c + 1) * 512], pp[:, :], AF.Square)
                                scol = B.qs_col[:, (0 if kind == "q" else 2) + c % 2:
                                                (0 if kind == "q" else 2) + c % 2 + 1]
                                s.activation(dstb[:, c * 1024 + tt * 512:c * 1024 + (tt + 1) * 512],
                                             pp[:, :], AF.Identity, scale=scol).then_inc(s_a.h, 1)
                            s.wait_ge(s_v.h, v_cl[ph])
                            kk0 = (0 if kind == "q" else 1024) + tt * 512
                            s.activation(rsd[0:16, kk0:kk0 + 512], rsd[0:16, kk0:kk0 + 512], AF.Ln)
                            s.drain()
                            s.activation(rsd[0:16, kk0:kk0 + 512], rsd[0:16, kk0:kk0 + 512], AF.Exp, scale=-0.5).then_inc(s_a.h, 1)

                    @b.vector
                    def _(v):
                        for ph in range(4):
                            kind = 0 if ph < 2 else 1
                            tt = ph % 2
                            v.wait_ge(s_t.h, t_s[ph])
                            kk0 = kind * 1024 + tt * 512
                            v.tensor_scalar(rsd[0:16, kk0:kk0 + 512], ps_pl[0:16, 0:512],
                                            1e-24, None, OP.max).then_inc(s_v.h, 1)
                        for vt in range(8):
                            pp = ps_v if vt % 2 == 0 else ps_a
                            if vt == 0 and g >= 1:
                                v.wait_ge(sd[3].h, prev_st["v"])

                            v.wait_ge(s_t.h, t_v[vt])
                            v.tensor_copy(vg[:, vt * 512:(vt + 1) * 512], pp[:, 0:512]).then_inc(s_v.h, 1)
                        v.wait_ge(sd[2].h, ld_pl)
                        for pr4 in range(4):
                            kind = pr4 // 2
                            pr = pr4 % 2
                            buf = qg if kind == 0 else kg
                            for tt in range(2):
                                v.wait_ge(s_t.h, t_pl[pr4 * 2 + tt])
                                v.drain()
                                v.tensor_copy(rpl[:, tt * 512:(tt + 1) * 512], ps_pl[:, 0:512]).then_inc(s_v.h, 1)
                            lo = buf[:, (2 * pr) * 1024:(2 * pr) * 1024 + 1024]
                            hi = buf[:, (2 * pr + 1) * 1024:(2 * pr + 1) * 1024 + 1024]
                            plc = plg[:, (0 if kind == 0 else 2) * 1024:(0 if kind == 0 else 2) * 1024 + 1024]
                            pls = plg[:, (1 if kind == 0 else 3) * 1024:(1 if kind == 0 else 3) * 1024 + 1024]
                            v.tensor_tensor(tmA[:, :], hi, pls, OP.mult)
                            v.drain()
                            v.tensor_tensor(tmB[:, :], hi, plc, OP.mult)
                            v.drain()
                            v.tensor_tensor(hi, lo, pls, OP.mult)
                            v.drain()
                            v.tensor_tensor(hi, hi, tmB[:, :], OP.add)
                            v.drain()
                            v.tensor_tensor(lo, lo, plc, OP.mult)
                            v.drain()
                            v.tensor_tensor(lo, lo, tmA[:, :], OP.subtract)
                            v.drain()
                            v.tensor_tensor(lo, lo, rpl[:, :], OP.mult)
                            v.tensor_tensor(hi, hi, rpl[:, :], OP.mult).then_inc(s_v.h, 1)

                    @b.sync
                    def _(sync):
                        sync.dma_start(out=plg[:, :].rearrange("p (pi t) -> p pi t", t=1024),
                                       in_=self.pl_dram.ap().rearrange("p (pi t) -> p pi t", t=T)[:, :, t0:t0 + 1024]).then_inc(sd[2].h, 16)
                        sync.wait_ge(s_v.h, v_vv[7])
                        sync.dma_start(out=self.v_dram.ap()[t0:t0 + 1024, :].rearrange("(vt p) i -> p vt i", p=128),
                                       in_=vg[:, :].rearrange("p (vt i) -> p vt i", i=512)).then_inc(sd[3].h, 16)
                        for pr4 in range(4):
                            kind = pr4 // 2
                            pr = pr4 % 2
                            buf = qg if kind == 0 else kg
                            dr = self.q_dram if kind == 0 else self.k_dram
                            sync.wait_ge(s_v.h, v_rot[pr4])
                            sync.dma_start(
                                out=dr.ap().rearrange("(c p) t -> p c t", p=128)[:, 2 * pr:2 * pr + 2, t0:t0 + 1024],
                                in_=buf[:, (2 * pr) * 1024:(2 * pr + 2) * 1024].rearrange("p (c t) -> p c t", t=1024)).then_inc(sd[6 if kind == 0 else 7].h, 16)

                prev_st = {"q": st_q[1], "k": st_k[1], "v": st_v}
            with ExitStack() as esc:
                cwst = [esc.enter_context(nc.sbuf_tensor(f"l{l}cw{i}", [128, 2048], F32)) for i in range(2)]
                cbf = [esc.enter_context(nc.sbuf_tensor(f"l{l}cb{i}", [128, 2048], BF16)) for i in range(2)]
                w1s = self.din["w1"][l]
                w2s = self.din["w2"][l]
                pieces = [(0, 2048), (2048, 2048), (4096, 1364)]
                jobs = []
                for c in range(8):
                    for off, wdt in pieces:
                        jobs.append(("w1", c, off, wdt))
                for kk in range(22):
                    jobs.append(("w2", kk, 0, 1024))
                ldsem = [self.s_st, sd[2]]
                stsem = [self.s_ld, sd[0]]
                ldc = []
                stc = []
                vcst = []
                for i in range(len(jobs)):
                    ldc.append(ldsem[i % 2].tick(16))
                    stc.append(stsem[i % 2].tick(16))
                    vcst.append(s_v.tick())
                with nc.Block() as b:
                    @b.sync
                    def _(sync):
                        for i, (kind, c, off, wdt) in enumerate(jobs):
                            if i >= 2:
                                sync.wait_ge(s_v.h, vcst[i - 2])
                            if kind == "w1":
                                sync.dma_start(out=cwst[i % 2][:, 0:wdt],
                                               in_=w1s[c * 128:(c + 1) * 128, off:off + wdt]).then_inc(ldsem[i % 2].h, 16)
                            else:
                                wj = min(128, HID - c * 128)
                                sync.dma_start(out=cwst[i % 2][0:wj, 0:1024],
                                               in_=w2s[c * 128:c * 128 + wj, :]).then_inc(ldsem[i % 2].h, 16)
                            if i >= 1:
                                km, cm, om, wm = jobs[i - 1]
                                sync.wait_ge(s_v.h, vcst[i - 1])
                                if km == "w1":
                                    sync.dma_start(out=self.w1bf.ap()[:, cm * 5460 + om:cm * 5460 + om + wm],
                                                   in_=cbf[(i - 1) % 2][:, 0:wm]).then_inc(stsem[(i - 1) % 2].h, 16)
                                else:
                                    wj = min(128, HID - cm * 128)
                                    sync.dma_start(out=self.w2bf.ap()[cm * 128:cm * 128 + wj, :],
                                                   in_=cbf[(i - 1) % 2][0:wj, 0:1024]).then_inc(stsem[(i - 1) % 2].h, 16)
                        km, cm, om, wm = jobs[-1]
                        sync.wait_ge(s_v.h, vcst[-1])
                        sync.dma_start(out=self.w2bf.ap()[cm * 128:cm * 128 + min(128, HID - cm * 128), :],
                                       in_=cbf[(len(jobs) - 1) % 2][0:min(128, HID - cm * 128), 0:1024]).then_inc(stsem[(len(jobs) - 1) % 2].h, 16)
                        le = max(i for i in range(len(jobs)) if i % 2 == 0)
                        lo = max(i for i in range(len(jobs)) if i % 2 == 1)
                        sync.wait_ge(stsem[0].h, stc[le])
                        sync.wait_ge(stsem[1].h, stc[lo])

                    @b.vector
                    def _(v):
                        for i, (kind, c, off, wdt) in enumerate(jobs):
                            v.wait_ge(ldsem[i % 2].h, ldc[i])
                            if i >= 2:
                                v.wait_ge(stsem[i % 2].h, stc[i - 2])
                            if kind == "w1":
                                v.tensor_copy(cbf[i % 2][:, 0:wdt], cwst[i % 2][:, 0:wdt])
                            else:
                                wj = min(128, HID - c * 128)
                                v.tensor_copy(cbf[i % 2][0:wj, 0:1024], cwst[i % 2][0:wj, 0:1024])
                            v.drain().then_inc(s_v.h, 1)
            fin3, fin6, fin7 = sd[3].v, sd[6].v, sd[7].v
            with nc.Block() as b:
                @b.sync
                def _(sync):
                    sync.wait_ge(sd[3].h, fin3)
                    sync.wait_ge(sd[6].h, fin6)
                    sync.wait_ge(sd[7].h, fin7)

    def _ffn_layer_attn(self, l, B):
        nc = self.nc
        s_t, s_v, s_a, s_p = self.s_t, self.s_v, self.s_a, self.s_p
        sd = self.s_dma
        es_l = ExitStack()
        sb = lambda n, s, d=BF16: es_l.enter_context(nc.sbuf_tensor(f"l{l}" + n, s, d))
        ps = lambda n, s, d=F32: es_l.enter_context(nc.psum_tensor(f"l{l}" + n, s, d))
        B.wob = sb("ff_wo", [128, 4 * 1024])
        B.qbuf = [sb(f"ff_q{i}", [128, 4 * 1024]) for i in range(2)]
        B.kbuf = [sb(f"ff_k{i}", [128, 4 * 1536]) for i in range(2)]
        B.vbuf = [sb(f"ff_v{i}", [128, 12 * 512]) for i in range(2)]
        B.pbufA = sb("ff_pA", [128, 640])
        B.ptA = sb("ff_ptA", [128, 640])
        B.obuf = sb("ff_obuf", [128, 4 * 1024])
        B.oplane = sb("ff_oplane", [128, 512])
        B.nmax = sb("ff_nmax", [128, 16], F32)
        B.rs8 = sb("ff_rs8", [128, 8], F32)
        B.scbuf = sb("ff_scbuf", [128, 640], F32)
        B.rcq = sb("ff_rcq", [128, 8])
        B.rcT = sb("ff_rcT", [16, 128])
        stgo = sb("ff_stgo", [128, 1024], F32)
        B.ps_sb = [ps("ff_pss0", [128, 512]), ps("ff_pss1", [128, 512])]
        B.ps_ss = [ps("ff_pst0", [128, 128]), ps("ff_pst1", [128, 128])]
        B.ps_T = ps("ff_psT", [128, 1024], BF16)
        B.ps_o = ps("ff_pso", [128, 512])
        B.ps_pl = ps("ff_pspl", [128, 512])
        B.ps_a = B.ps_sb[0]
        B.ps_b = B.ps_sb[1]

        wo = self.din["wo"][l]
        for c in range(4):
            ld = sd[0].tick(16)
            cst = s_v.tick()
            with nc.Block() as b:
                @b.sync
                def _(sync):
                    sync.dma_start(out=stgo[:, 0:1024], in_=wo[c * 128:(c + 1) * 128, :]).then_inc(sd[0].h, 16)

                @b.vector
                def _(v):
                    v.wait_ge(sd[0].h, ld)
                    v.tensor_copy(B.wob[:, c * 1024:(c + 1) * 1024], stgo[:, 0:1024])
                    v.drain().then_inc(s_v.h, 1)

        def emit_band_load(bb):
            t0 = bb * 1024
            kw0 = max(0, t0 - 512)
            kwN = t0 + 1024 - kw0
            slot = 4 + bb % 2
            tk = sd[slot].tick(16 * 3)
            with nc.Block() as b:
                @b.sync
                def _(sync):
                    sync.dma_start(out=B.qbuf[bb % 2][:, :].rearrange("p (c t) -> p c t", t=1024),
                                   in_=self.q_dram.ap().rearrange("(c p) t -> p c t", p=128)[:, :, t0:t0 + 1024]).then_inc(sd[slot].h, 16)
                    sync.dma_start(out=B.kbuf[bb % 2][:, :].rearrange("p (c t) -> p c t", t=1536)[:, :, 0:kwN],
                                   in_=self.k_dram.ap().rearrange("(c p) t -> p c t", p=128)[:, :, kw0:kw0 + kwN]).then_inc(sd[slot].h, 16)
                    sync.dma_start(out=B.vbuf[bb % 2][:, 0:kwN * 4].rearrange("p (vt i) -> p vt i", i=512),
                                   in_=self.v_dram.ap()[kw0:kw0 + kwN, :].rearrange("(vt p) i -> p vt i", p=128)).then_inc(sd[slot].h, 16)
            return tk

        ldt = [emit_band_load(0), emit_band_load(1)]
        for bb in range(4):
            t0 = bb * 1024
            kw0 = max(0, t0 - 512)
            kwN = t0 + 1024 - kw0
            B.qr = B.qbuf[bb % 2]
            B.kr = B.kbuf[bb % 2]
            B.vtok = B.vbuf[bb % 2]
            B.ld_wait = (4 + bb % 2, ldt[bb % 2])
            for qg_ in range(2):
                for pp4 in range(4):
                    self._attn_inner(l, B, bb, t0, kw0, qg_, pp4)
            t_w = [s_t.tick() for _ in range(16)]
            v_w = [s_v.tick() for _ in range(16)]
            with nc.Block() as b:
                @b.tensor
                def _(t):
                    u = 0
                    for dt_ in range(8):
                        for tt in range(2):
                            pp = B.ps_a if u % 2 == 0 else B.ps_b
                            if u >= 2:
                                t.wait_ge(s_v.h, v_w[u - 2])
                            for oc in range(4):
                                mm = t.matmul(pp[:, :], B.wob[:, oc * 1024 + dt_ * 128:oc * 1024 + (dt_ + 1) * 128],
                                              B.obuf[:, oc * 1024 + tt * 512:oc * 1024 + (tt + 1) * 512],
                                              start=(oc == 0), stop=(oc == 3))
                            mm.then_inc(s_t.h, 1)
                            u += 1

                @b.vector
                def _(v):
                    xs3 = self.xstream.rearrange("p (c t) -> p c t", t=TS)
                    u = 0
                    for dt_ in range(8):
                        for tt in range(2):
                            pp = B.ps_a if u % 2 == 0 else B.ps_b
                            v.wait_ge(s_t.h, t_w[u])
                            dstx = xs3[:, dt_, 1 + t0 + tt * 512:1 + t0 + (tt + 1) * 512]
                            v.tensor_tensor(dstx, dstx, pp[:, :], OP.add).then_inc(s_v.h, 1)
                            u += 1
            if bb + 2 < 4:
                ldt[bb % 2] = emit_band_load(bb + 2)
        self.barrier()
        es_l.close()

    def _attn_inner(self, l, B, bb, t0, kw0, qg, pp4):
        nc = self.nc
        s_t, s_v, s_a, s_p = self.s_t, self.s_v, self.s_a, self.s_p
        units = [(qb, hh) for qb in range(4) for hh in range(2)]
        t_sc = {}
        t_pt = {}
        t_pv = {}
        for i, u in enumerate(units):
            t_sc[u] = s_t.tick()
            if i >= 1:
                t_pt[units[i - 1]] = s_t.tick()
                t_pv[units[i - 1]] = s_t.tick()
        t_pt[units[-1]] = s_t.tick()
        t_pv[units[-1]] = s_t.tick()
        t_rcT = s_t.tick()
        t_plo = s_t.tick()
        v_sc = {}
        v_mx = {}
        v_pt = {}
        for u in units:
            v_sc[u] = s_v.tick()
            v_mx[u] = s_v.tick()
            v_pt[u] = s_v.tick()
        v_rc8 = s_v.tick()
        v_rcT = s_v.tick()
        v_oev = s_v.tick()
        a_cp = {}
        a_ex = {}
        for u in units:
            a_cp[u] = s_a.tick()
            a_ex[u] = s_a.tick()

        def kbs(qb):
            qbg = (t0 // 128) + qg * 4 + qb
            lo = max(0, qbg - 4)
            return [kb for kb in range(lo, qbg + 1)], qbg

        def sc_dst(u_i, ki):
            if ki < 4:
                return B.ps_sb[u_i % 2][:, ki * 128:(ki + 1) * 128]
            return B.ps_ss[u_i % 2][:, 0:128]

        with nc.Block() as b:
            @b.tensor
            def _(t):
                if qg == 0 and pp4 == 0:
                    t.wait_ge(self.s_dma[B.ld_wait[0]].h, B.ld_wait[1])

                def emit_scores(u):
                    qb, hh = u
                    h = 2 * pp4 + hh
                    ch = (h // 4) * 2
                    st_ = (h % 4) * 32
                    kl, qbg = kbs(qb)
                    qcol = qg * 512 + qb * 128
                    u_i = qb * 2 + hh
                    if u not in (units[0], units[1]):
                        prev = units[units.index(u) - 2]
                        t.wait_ge(s_v.h, v_sc[prev])
                        t.wait_ge(s_a.h, a_cp[prev])
                    nkb = len(kl)
                    nb4 = min(nkb, 4)
                    kcol0 = kl[0] * 128 - kw0
                    qlo = B.qr[st_:st_ + 32, ch * 1024 + qcol:ch * 1024 + qcol + 128]
                    qhi = B.qr[st_:st_ + 32, (ch + 1) * 1024 + qcol:(ch + 1) * 1024 + qcol + 128]
                    t.matmul(B.ps_sb[u_i % 2][:, 0:nb4 * 128], qlo,
                             B.kr[st_:st_ + 32, ch * 1536 + kcol0:ch * 1536 + kcol0 + nb4 * 128],
                             start=True, stop=False, tile_position=(st_, 0))
                    mm = t.matmul(B.ps_sb[u_i % 2][:, 0:nb4 * 128], qhi,
                                  B.kr[st_:st_ + 32, (ch + 1) * 1536 + kcol0:(ch + 1) * 1536 + kcol0 + nb4 * 128],
                                  start=False, stop=True, tile_position=(st_, 0))
                    if nkb == 5:
                        t.matmul(B.ps_ss[u_i % 2][:, 0:128], qlo,
                                 B.kr[st_:st_ + 32, ch * 1536 + kcol0 + 512:ch * 1536 + kcol0 + 640],
                                 start=True, stop=False, tile_position=(st_, 0))
                        mm = t.matmul(B.ps_ss[u_i % 2][:, 0:128], qhi,
                                      B.kr[st_:st_ + 32, (ch + 1) * 1536 + kcol0 + 512:(ch + 1) * 1536 + kcol0 + 640],
                                      start=False, stop=True, tile_position=(st_, 0))
                    mm.then_inc(s_t.h, 1)

                def emit_tpv(u):
                    qb, hh = u
                    h = 2 * pp4 + hh
                    kl, qbg = kbs(qb)
                    nkb = len(kl)
                    t.wait_ge(s_a.h, a_ex[u])
                    for ki in range(nkb):
                        mm = t.transpose(B.ps_T[0:128, ki * 128:(ki + 1) * 128],
                                         B.pbufA[:, ki * 128:(ki + 1) * 128], self.ident[:, :])
                    mm.then_inc(s_t.h, 1)
                    t.wait_ge(s_v.h, v_pt[u])
                    for ki, kb in enumerate(kl):
                        vloc = kb * 128 - kw0
                        mm = t.matmul(B.ps_o[hh * 64:hh * 64 + 64, qb * 128:(qb + 1) * 128],
                                      B.vtok[:, (vloc // 128) * 512 + h * 64:(vloc // 128) * 512 + (h + 1) * 64],
                                      B.ptA[:, ki * 128:(ki + 1) * 128],
                                      start=(ki == 0), stop=(ki == nkb - 1),
                                      tile_position=(0, hh * 64))
                    mm.then_inc(s_t.h, 1)

                for i, u in enumerate(units):
                    emit_scores(u)
                    if i >= 1:
                        emit_tpv(units[i - 1])
                emit_tpv(units[-1])
                t.wait_ge(s_v.h, v_rc8)
                t.transpose(B.ps_T[0:8, 896:1024],
                            B.rcq[:, 0:8], self.ident[:, :]).then_inc(s_t.h, 1)
                t.wait_ge(s_v.h, v_rcT)
                for qb in range(4):
                    mm = t.matmul(B.ps_pl[:, qb * 128:(qb + 1) * 128], B.selq[0:8, qb * 128:(qb + 1) * 128],
                                  B.rcT[0:8, 0:128], start=True, stop=True)
                mm.then_inc(s_t.h, 1)

            @b.vector
            def _(v):
                for (qb, hh) in units:
                    kl, qbg = kbs(qb)
                    nkb = len(kl)
                    u_i = qb * 2 + hh
                    v.wait_ge(s_t.h, t_sc[(qb, hh)])
                    ops = []
                    for ki in range(nkb):
                        src = sc_dst(u_i, ki)
                        dstc = B.scbuf[:, ki * 128:(ki + 1) * 128]
                        if ki == nkb - 1:
                            ops.append((dstc, src, B.fmask[:, 128:256]))
                        elif ki == 0 and qbg >= 4:
                            ops.append((dstc, src, B.fmask[:, 0:128]))
                    for d_, s_, m_ in ops[:-1]:
                        v.tensor_tensor(d_, s_, m_, OP.add)
                    v.tensor_tensor(*ops[-1], OP.add).then_inc(s_v.h, 1)
                    v.wait_ge(s_a.h, a_cp[(qb, hh)])
                    v.drain()
                    v.reduce_max(B.nmax[:, 0:1], B.scbuf[:, 0:nkb * 128], axis=AX)
                    v.drain()
                    v.tensor_scalar(B.nmax[:, 1:2], B.nmax[:, 0:1], -8.0, None, OP.mult).then_inc(s_v.h, 1)
                    v.wait_ge(s_t.h, t_pt[(qb, hh)])
                    v.tensor_copy(B.ptA[:, 0:nkb * 128], B.ps_T[0:128, 0:nkb * 128]).then_inc(s_v.h, 1)
                v.wait_ge(s_a.h, a_ex[units[-1]])
                with nc.allow_low_precision(reason="1/Z stored bf16 for PE transpose, as before"):
                    v.reciprocal(B.rcq[:, 0:8], B.rs8[:, 0:8]).then_inc(s_v.h, 1)
                v.wait_ge(s_t.h, t_rcT)
                v.tensor_copy(B.rcT[0:8, 0:128], B.ps_T[0:8, 896:1024]).then_inc(s_v.h, 1)
                v.wait_ge(s_t.h, t_plo)
                v.tensor_copy(B.oplane[:, 0:512], B.ps_pl[:, 0:512])
                v.drain()
                v.wait_ge(s_t.h, t_pv[units[-1]])
                v.tensor_tensor(B.obuf[:, pp4 * 1024 + qg * 512:pp4 * 1024 + (qg + 1) * 512],
                                B.ps_o[:, 0:512], B.oplane[:, 0:512], OP.mult).then_inc(s_v.h, 1)

            @b.scalar
            def _(s):
                for (qb, hh) in units:
                    kl, qbg = kbs(qb)
                    nkb = len(kl)
                    u_i = qb * 2 + hh
                    s.wait_ge(s_t.h, t_sc[(qb, hh)])
                    s.drain()
                    m0 = 1 if qbg >= 4 else 0
                    m1_ = min(nkb - 1, 4)
                    if m1_ > m0:
                        s.activation(B.scbuf[:, m0 * 128:m1_ * 128],
                                     B.ps_sb[u_i % 2][:, m0 * 128:m1_ * 128],
                                     AF.Identity).then_inc(s_a.h, 1)
                    else:
                        s.drain().then_inc(s_a.h, 1)
                    s.wait_ge(s_v.h, v_mx[(qb, hh)])
                    iu = units.index((qb, hh))
                    if iu >= 1:
                        s.wait_ge(s_t.h, t_pt[units[iu - 1]])
                    s.activation(B.pbufA[:, 0:nkb * 128], B.scbuf[:, 0:nkb * 128], AF.Exp,
                                 scale=8.0, bias=B.nmax[:, 1:2],
                                 accum_out=B.rs8[:, u_i:u_i + 1]).then_inc(s_a.h, 1)

    def _ffn_layer_mlp(self, l, B):
        nc = self.nc
        s_t, s_v, s_a, s_p = self.s_t, self.s_v, self.s_a, self.s_p
        sd = self.s_dma
        xs3 = self.xstream.rearrange("p (c t) -> p c t", t=TS)
        with ExitStack() as es:
            sb = lambda n, s, d=BF16: es.enter_context(nc.sbuf_tensor(f"l{l}" + n, s, d))
            ps = lambda n, s, d=F32: es.enter_context(nc.psum_tensor(f"l{l}" + n, s, d))
            ubuf = sb("fm_ubuf", [128, 2 * 8192])
            h2 = sb("fm_h2", [128, 22 * 1024])
            w1b = [sb(f"fm_w1b{i}", [128, 2048]) for i in range(2)]
            w2b = [sb(f"fm_w2b{i}", [128, 256]) for i in range(2)]
            stA = sb("fm_stA", [16, 1536], F32)
            stB = sb("fm_stB", [16, 2048])
            sq = [sb(f"fm_sq{i}", [128, 512]) for i in range(2)]
            gb = [sb(f"fm_gb{i}", [128, 512]) for i in range(2)]
            ab = [sb(f"fm_ab{i}", [128, 512]) for i in range(2)]
            pP = sb("fm_pP", [128, 512])
            pQ = sb("fm_pQ", [128, 512])
            rtmp = sb("fm_rtmp", [128, 512])
            rtmp2 = sb("fm_rtmp2", [128, 512])
            xbuf4 = sb("fm_xb4", [128, 4 * 512])
            ps8 = [ps(f"fm_ps{i}", [128, 512]) for i in range(8)]

            def xlnv(qt):
                return ubuf[:, (qt % 2) * 8192:(qt % 2) * 8192 + 8192].rearrange(
                    "p (c t) -> p c t", t=1024)

            def xsrc(ck, tt, qt, w0, w1_):
                if qt >= 1 and tt == 0 and ck >= 4:
                    return xbuf4[:, (ck - 4) * 512 + w0:(ck - 4) * 512 + w1_]
                ts0 = qt * 1024 + tt * 512
                return B.xsl(ck, ts0 + w0, ts0 + w1_, True)

            TT_ = {}
            AA = {}
            VV = {}
            DD = {}
            cast_done = self.s_ld.v
            cast_done2 = sd[0].v

            def alloc_m1_t(qt):
                for tt in range(2):
                    for ck in range(8):
                        TT_[('st', qt, tt, ck)] = s_t.tick()
                for tt in range(2):
                    for ck in range(8):
                        TT_[('pl', qt, tt, ck)] = s_t.tick()

            def alloc_m1_a(qt):
                for tt in range(2):
                    for ck in range(8):
                        AA[('sq', qt, tt, ck)] = s_a.tick()
                    AA[('ln', qt, tt)] = s_a.tick()

            def alloc_m1v_stats(qt):
                for tt in range(2):
                    VV[('stA', qt, tt)] = s_v.tick()
                    VV[('stB', qt, tt)] = s_v.tick()

            def alloc_m1v_planes(qt):
                for tt in range(2):
                    for ck in range(8):
                        VV[('pl', qt, tt, ck)] = s_v.tick()

            def alloc_m1_v(qt):
                alloc_m1v_stats(qt)
                alloc_m1v_planes(qt)

            for qt in range(4):
                if qt == 0:
                    alloc_m1_t(0)
                for jt in range(22):
                    for tt in range(2):
                        TT_[('m2', qt, jt, tt)] = s_t.tick()
                for dtg in range(4):
                    for kk in range(22):
                        TT_[('m3', qt, dtg, kk)] = s_t.tick()
                    if dtg == 1 and qt < 3:
                        for tt in range(2):
                            for ck in range(8):
                                TT_[('st', qt + 1, tt, ck)] = s_t.tick()
                    if dtg == 2 and qt < 3:
                        for tt in range(2):
                            for ck in range(8):
                                TT_[('pl', qt + 1, tt, ck)] = s_t.tick()
            for qt in range(4):
                if qt == 0:
                    alloc_m1_a(0)
                for jt in range(22):
                    for tt in range(2):
                        AA[('m2', qt, jt, tt)] = s_a.tick()
                for pidx in range(4):
                    AA[('m3', qt, 0, pidx)] = s_a.tick()
                if qt < 3:
                    alloc_m1_a(qt + 1)
                for dtg in range(1, 4):
                    for pidx in range(4):
                        AA[('m3', qt, dtg, pidx)] = s_a.tick()
            def alloc_res(qt, dtg):
                for pidx in range(4):
                    VV[('res', qt, dtg, pidx)] = s_v.tick()
            for qt in range(4):
                if qt == 0:
                    alloc_m1_v(0)
                for jt in range(1, 23):
                    for tt in range(2):
                        VV[('h2', qt, jt - 1, tt)] = s_v.tick()
                if qt < 3:
                    VV[('xb', qt + 1)] = s_v.tick()
                alloc_res(qt, 0)
                if qt < 3:
                    for tt in range(2):
                        VV[('stA', qt + 1, tt)] = s_v.tick()
                        VV[('stB', qt + 1, tt)] = s_v.tick()
                alloc_res(qt, 1)
                if qt < 3:
                    alloc_m1v_planes(qt + 1)
                alloc_res(qt, 2)
                alloc_res(qt, 3)
            for qt in range(4):
                for jt in range(22):
                    DD[('m2', qt, jt)] = sd[3 + jt % 2].tick(32)
                for dtg in range(4):
                    for kk in range(22):
                        DD[('m3', qt, dtg, kk)] = (sd[5] if kk % 2 == 0 else self.s_gq).tick(16)

            w1bf = self.w1bf.ap()
            w2bf = self.w2bf.ap()

            with nc.Block() as b:
                @b.sync
                def _(sync):
                    sync.wait_ge(self.s_ld.h, cast_done)
                    sync.wait_ge(sd[0].h, cast_done2)
                    for qt in range(4):
                        for jt in range(22):
                            wj = min(128, HID - jt * 128)
                            if jt >= 2:
                                sync.wait_ge(s_t.h, TT_[('m2', qt, jt - 2, 1)])
                            elif qt >= 1:
                                sync.wait_ge(s_t.h, TT_[('m2', qt - 1, jt + 20, 1)])
                            w3 = w1bf.rearrange("p (c e) -> p c e", e=5460)
                            sync.dma_start(out=w1b[jt % 2][:, 0:8 * wj].rearrange("p (c w) -> p c w", w=wj),
                                           in_=w3[:, :, jt * 128:jt * 128 + wj]).then_inc(sd[3 + jt % 2].h, 16)
                            sync.dma_start(out=w1b[jt % 2][:, 1024:1024 + 8 * wj].rearrange("p (c w) -> p c w", w=wj),
                                           in_=w3[:, :, HID + jt * 128:HID + jt * 128 + wj]).then_inc(sd[3 + jt % 2].h, 16)
                        for dtg in range(4):
                            for kk in range(0, 22, 2):
                                wj = min(128, HID - kk * 128)
                                if kk >= 2:
                                    sync.wait_ge(s_t.h, TT_[('m3', qt, dtg, kk - 2)])
                                elif dtg >= 1:
                                    sync.wait_ge(s_t.h, TT_[('m3', qt, dtg - 1, 20)])
                                elif qt >= 1:
                                    sync.wait_ge(s_t.h, TT_[('m3', qt - 1, 3, 20)])
                                sync.dma_start(out=w2b[0][0:wj, 0:256],
                                               in_=w2bf[kk * 128:kk * 128 + wj, dtg * 256:(dtg + 1) * 256]).then_inc(sd[5].h, 16)

                @b.gpsimd
                def _(g_):
                    for qt in range(4):
                        for dtg in range(4):
                            for kk in range(1, 22, 2):
                                wj = min(128, HID - kk * 128)
                                if kk >= 3:
                                    g_.wait_ge(s_t.h, TT_[('m3', qt, dtg, kk - 2)])
                                elif dtg >= 1:
                                    g_.wait_ge(s_t.h, TT_[('m3', qt, dtg - 1, 21)])
                                elif qt >= 1:
                                    g_.wait_ge(s_t.h, TT_[('m3', qt - 1, 3, 21)])
                                g_.dma_start(out=w2b[1][0:wj, 0:256],
                                             in_=w2bf[kk * 128:kk * 128 + wj, dtg * 256:(dtg + 1) * 256]).then_inc(self.s_gq.h, 16)

                @b.tensor
                def _(t):
                    def emit_m1_stats(qt):
                        for tt in range(2):
                            if tt == 1:
                                t.wait_ge(s_v.h, VV[('stA', qt, 0)])
                            for ck in range(8):
                                t.wait_ge(s_a.h, AA[('sq', qt, tt, ck)])
                                t.matmul(ps8[4][0:64, 0:512], B.onecol[:, 0:64],
                                         xsrc(ck, tt, qt, 0, 512),
                                         start=(ck == 0), stop=False)
                                t.matmul(ps8[4][0:64, 0:512], B.onecol[:, 64:128],
                                         sq[ck % 2][:, :],
                                         start=False, stop=(ck == 7)).then_inc(s_t.h, 1)

                    def emit_m1_planes(qt):
                        t.wait_ge(s_v.h, VV[('stB', qt, 1)])
                        for tt in range(2):
                            for ck in range(8):
                                i = tt * 8 + ck
                                pp_, qq_ = (ps8[5], ps8[6]) if i % 2 == 0 else (ps8[7], ps8[4])
                                if i >= 2:
                                    t.wait_ge(s_v.h, VV[('pl', qt, (i - 2) // 8, (i - 2) % 8)])
                                t.matmul(pp_[:, :], B.lnrowg[0:1, ck * 128:(ck + 1) * 128],
                                         stB[0:1, tt * 1024:tt * 1024 + 512], start=True, stop=True)
                                t.matmul(qq_[:, :], B.lnrowg[0:1, ck * 128:(ck + 1) * 128],
                                         stB[0:1, tt * 1024 + 512:tt * 1024 + 1024], start=True, stop=False)
                                t.matmul(qq_[:, :], B.lnrowb[0:1, ck * 128:(ck + 1) * 128],
                                         B.onesr[0:1, :], start=False, stop=True).then_inc(s_t.h, 1)

                    for qt in range(4):
                        if qt == 0:
                            emit_m1_stats(0)
                            emit_m1_planes(0)
                        for jt in range(22):
                            wj = min(128, HID - jt * 128)
                            t.wait_ge(sd[3 + jt % 2].h, DD[('m2', qt, jt)])
                            for tt in range(2):
                                if jt >= 1:
                                    t.wait_ge(s_a.h, AA[('m2', qt, jt - 1, tt)])
                                elif qt >= 1:
                                    t.wait_ge(s_a.h, AA[('m3', qt - 1, 3, 3)])
                                if jt == 0:
                                    t.wait_ge(s_v.h, VV[('pl', qt, tt, 7)])
                                for ck in range(8):
                                    t.matmul(ps8[tt][0:wj, :], w1b[jt % 2][:, ck * wj:(ck + 1) * wj],
                                             xlnv(qt)[:, ck, tt * 512:(tt + 1) * 512],
                                             start=(ck == 0), stop=(ck == 7))
                                    mm = t.matmul(ps8[2 + tt][0:wj, :], w1b[jt % 2][:, 1024 + ck * wj:1024 + (ck + 1) * wj],
                                                  xlnv(qt)[:, ck, tt * 512:(tt + 1) * 512],
                                                  start=(ck == 0), stop=(ck == 7))
                                mm.then_inc(s_t.h, 1)
                        for dtg in range(4):
                            if dtg == 0:
                                t.wait_ge(s_a.h, AA[('m2', qt, 21, 1)])
                                t.wait_ge(s_v.h, VV[('h2', qt, 21, 0)])
                                t.wait_ge(s_v.h, VV[('h2', qt, 21, 1)])
                            for kk in range(22):
                                wj = min(128, HID - kk * 128)
                                t.wait_ge((sd[5] if kk % 2 == 0 else self.s_gq).h, DD[('m3', qt, dtg, kk)])
                                if kk == 0 and dtg >= 1:
                                    for pidx in range(4):
                                        t.wait_ge(s_a.h, AA[('m3', qt, dtg - 1, pidx)])
                                for ii in range(2):
                                    for tt in range(2):
                                        pidx = ii * 2 + tt
                                        mm = t.matmul(ps8[pidx][:, :],
                                                      w2b[kk % 2][0:wj, ii * 128:(ii + 1) * 128],
                                                      h2[0:wj, kk * 1024 + tt * 512:kk * 1024 + (tt + 1) * 512],
                                                      start=(kk == 0), stop=(kk == 21))
                                mm.then_inc(s_t.h, 1)
                            if dtg == 1 and qt < 3:
                                emit_m1_stats(qt + 1)
                            if dtg == 2 and qt < 3:
                                emit_m1_planes(qt + 1)

                @b.scalar
                def _(s):
                    def emit_sq(qt):
                        for tt in range(2):
                            for ck in range(8):
                                if qt >= 1 and tt == 0 and ck == 4:
                                    s.wait_ge(s_v.h, VV[('xb', qt)])
                                if tt * 8 + ck >= 2:
                                    pc = tt * 8 + ck - 2
                                    s.wait_ge(s_t.h, TT_[('st', qt, pc // 8, pc % 8)])
                                s.activation(sq[ck % 2][:, :], xsrc(ck, tt, qt, 0, 512), AF.Square).then_inc(s_a.h, 1)
                            s.wait_ge(s_v.h, VV[('stA', qt, tt)])
                            s.activation(stA[0:1, 512:1024], stA[0:1, 512:1024], AF.Ln, bias=1e-5)
                            s.drain()
                            s.activation(stA[0:1, 512:1024], stA[0:1, 512:1024], AF.Exp, scale=-0.5).then_inc(s_a.h, 1)

                    for qt in range(4):
                        if qt == 0:
                            emit_sq(0)
                        for jt in range(22):
                            wj = min(128, HID - jt * 128)
                            for tt in range(2):
                                s.wait_ge(s_t.h, TT_[('m2', qt, jt, tt)])
                                if jt >= 1:
                                    s.wait_ge(s_v.h, VV[('h2', qt, jt - 1, tt)])
                                elif qt >= 1:
                                    s.wait_ge(s_v.h, VV[('res', qt - 1, 3, 3)])
                                s.activation(ab[tt][0:wj, :], ps8[tt][0:wj, :], AF.Identity,
                                             bias=B.b2row[0:wj, jt:jt + 1])
                                s.activation(gb[tt][0:wj, :], ps8[2 + tt][0:wj, :], AF.Gelu,
                                             bias=B.b2row[0:wj, 22 + jt:23 + jt]).then_inc(s_a.h, 1)
                        def emit_m3a(dtg):
                            s.wait_ge(s_t.h, TT_[('m3', qt, dtg, 21)])
                            for pidx in range(4):
                                ii, tt = pidx // 2, pidx % 2
                                dt_ = dtg * 2 + ii
                                dst = [rtmp, ab[1], gb[1], rtmp2][pidx]
                                if dtg >= 1:
                                    s.wait_ge(s_v.h, VV[('res', qt, dtg - 1, pidx)])
                                s.activation(dst[:, :], ps8[pidx][:, :], AF.Identity,
                                             bias=B.b2row[:, 44 + dt_:45 + dt_]).then_inc(s_a.h, 1)
                        emit_m3a(0)
                        if qt < 3:
                            emit_sq(qt + 1)
                        for dtg in range(1, 4):
                            emit_m3a(dtg)

                @b.vector
                def _(v):
                    def emit_xb(qt):
                        for ck in range(4, 8):
                            v.tensor_copy(xbuf4[:, (ck - 4) * 512:(ck - 4) * 512 + 1],
                                          xs3[:, ck, qt * 1024:qt * 1024 + 1])
                            v.drain()
                            v.tensor_copy(xbuf4[:, (ck - 4) * 512 + 1:(ck - 4) * 512 + 512],
                                          xs3[:, ck, qt * 1024 + 1:qt * 1024 + 512])
                            v.drain()
                        v.drain().then_inc(s_v.h, 1)

                    def emit_stats_chain(qt):
                        for tt in range(2):
                            v.wait_ge(s_t.h, TT_[('st', qt, tt, 7)])
                            v.tensor_scalar(stA[0:1, 0:512], ps8[4][0:1, 0:512], 1.0 / D, None, OP.mult)
                            v.drain()
                            v.tensor_scalar(stA[0:1, 512:1024], ps8[4][32:33, 0:512], 1.0 / D, None, OP.mult)
                            v.drain()
                            v.tensor_tensor(stA[0:1, 1024:1536], stA[0:1, 0:512], stA[0:1, 0:512], OP.mult)
                            v.drain()
                            v.tensor_tensor(stA[0:1, 512:1024], stA[0:1, 512:1024], stA[0:1, 1024:1536], OP.subtract)
                            v.drain().then_inc(s_v.h, 1)
                            v.wait_ge(s_a.h, AA[('ln', qt, tt)])
                            v.tensor_copy(stB[0:1, tt * 1024:tt * 1024 + 512], stA[0:1, 512:1024])
                            v.drain()
                            v.tensor_tensor(stA[0:1, 1024:1536], stA[0:1, 0:512], stA[0:1, 512:1024], OP.mult)
                            v.drain()
                            v.tensor_scalar(stB[0:1, tt * 1024 + 512:tt * 1024 + 1024], stA[0:1, 1024:1536], -1.0, None, OP.mult)
                            v.drain().then_inc(s_v.h, 1)

                    def emit_planes_apply(qt):
                        for tt in range(2):
                            for ck in range(8):
                                i = tt * 8 + ck
                                pp_, qq_ = (ps8[5], ps8[6]) if i % 2 == 0 else (ps8[7], ps8[4])
                                v.wait_ge(s_t.h, TT_[('pl', qt, tt, ck)])
                                v.drain()
                                v.tensor_copy(pP[:, :], pp_[:, :])
                                v.tensor_copy(pQ[:, :], qq_[:, :])
                                v.drain()
                                dst = xlnv(qt)[:, ck, tt * 512:(tt + 1) * 512]
                                v.tensor_tensor(dst, xsrc(ck, tt, qt, 0, 512), pP[:, :], OP.mult)
                                v.drain()
                                v.tensor_tensor(dst, dst, pQ[:, :], OP.add).then_inc(s_v.h, 1)

                    def emit_res(qt, dtg):
                        for pidx in range(4):
                            ii, tt = pidx // 2, pidx % 2
                            dt_ = dtg * 2 + ii
                            v.wait_ge(s_a.h, AA[('m3', qt, dtg, pidx)])
                            src = [rtmp, ab[1], gb[1], rtmp2][pidx]
                            dstx = xs3[:, dt_, 1 + qt * 1024 + tt * 512:1 + qt * 1024 + (tt + 1) * 512]
                            v.tensor_tensor(dstx, dstx, src[:, :], OP.add).then_inc(s_v.h, 1)

                    for qt in range(4):
                        if qt == 0:
                            emit_stats_chain(0)
                            emit_planes_apply(0)
                        for jt in range(1, 23):
                            wjp = min(128, HID - (jt - 1) * 128)
                            for tt in range(2):
                                v.wait_ge(s_a.h, AA[('m2', qt, jt - 1, tt)])
                                if qt >= 1 and jt == 1:
                                    v.wait_ge(s_t.h, TT_[('m3', qt - 1, 3, 21)])
                                v.tensor_tensor(h2[0:wjp, (jt - 1) * 1024 + tt * 512:(jt - 1) * 1024 + (tt + 1) * 512],
                                                ab[tt][0:wjp, :], gb[tt][0:wjp, :], OP.mult).then_inc(s_v.h, 1)
                        if qt < 3:
                            emit_xb(qt + 1)
                        emit_res(qt, 0)
                        if qt < 3:
                            emit_stats_chain(qt + 1)
                        emit_res(qt, 1)
                        if qt < 3:
                            emit_planes_apply(qt + 1)
                        emit_res(qt, 2)
                        emit_res(qt, 3)

    def p7_combine(self):
        nc = self.nc
        s_t, s_v = self.s_t, self.s_v
        sd = self.s_dma
        xs3 = self.xstream.rearrange("p (c t) -> p c t", t=TS)
        with ExitStack() as es:
            xstg = [es.enter_context(nc.sbuf_tensor(f"p7_x{i}", [128, D], F32)) for i in range(2)]
            y3stg = [es.enter_context(nc.sbuf_tensor(f"p7_y{i}", [128, D], BF16)) for i in range(2)]
            outb = [es.enter_context(nc.sbuf_tensor(f"p7_o{i}", [128, D], F32)) for i in range(2)]
            ps_f = [es.enter_context(nc.psum_tensor(f"p7_ps{i}", [128, D], BF16)) for i in range(2)]
            NT = T // 128
            xsl_ = [0, 3]
            ysl_ = [1, 5]
            osl_ = [2, 6]
            xld = [sd[xsl_[i % 2]].tick(16) for i in range(NT)]
            y3ld = [sd[ysl_[i % 2]].tick(16) for i in range(NT)]
            stt = [sd[osl_[i % 2]].tick(16) for i in range(NT)]
            t_f = [s_t.tick() for _ in range(NT)]
            v_cmb = [s_v.tick() for _ in range(NT)]
            with nc.Block() as b:
                @b.sync
                def _(sync):
                    for i in range(NT):
                        if i >= 2:
                            sync.wait_ge(s_v.h, v_cmb[i - 2])
                        sync.dma_start(out=xstg[i % 2][:, :],
                                       in_=self.din["x"][i * 128:(i + 1) * 128, :]).then_inc(sd[xsl_[i % 2]].h, 16)
                        sync.dma_start(out=y3stg[i % 2][:, :],
                                       in_=self.y3_dram.ap()[:, i * 1024:(i + 1) * 1024]).then_inc(sd[ysl_[i % 2]].h, 16)
                        if i >= 1:
                            sync.wait_ge(s_v.h, v_cmb[i - 1])
                            sync.dma_start(out=self.out[(i - 1) * 128:i * 128, :],
                                           in_=outb[(i - 1) % 2][:, :]).then_inc(sd[osl_[(i - 1) % 2]].h, 16)
                    sync.wait_ge(s_v.h, v_cmb[NT - 1])
                    sync.dma_start(out=self.out[(NT - 1) * 128:NT * 128, :],
                                   in_=outb[(NT - 1) % 2][:, :]).then_inc(sd[osl_[(NT - 1) % 2]].h, 16)
                    sync.wait_ge(sd[osl_[0]].h, stt[NT - 2])
                    sync.wait_ge(sd[osl_[1]].h, stt[NT - 1])

                @b.tensor
                def _(t):
                    for i in range(NT):
                        if i >= 2:
                            t.wait_ge(s_v.h, v_cmb[i - 2])
                        for ck in range(8):
                            mm = t.transpose(ps_f[i % 2][:, ck * 128:(ck + 1) * 128],
                                             xs3[:, ck, 1 + i * 128:1 + (i + 1) * 128],
                                             self.ident[:, :])
                        mm.then_inc(s_t.h, 1)

                @b.vector
                def _(v):
                    for i in range(NT):
                        v.wait_ge(s_t.h, t_f[i])
                        v.wait_ge(sd[xsl_[i % 2]].h, xld[i])
                        v.wait_ge(sd[ysl_[i % 2]].h, y3ld[i])
                        if i >= 2:
                            v.wait_ge(sd[osl_[i % 2]].h, stt[i - 2])
                        v.tensor_tensor(outb[i % 2][:, :], ps_f[i % 2][:, :], y3stg[i % 2][:, :], OP.add)
                        v.drain()
                        v.tensor_tensor(outb[i % 2][:, :], outb[i % 2][:, :], xstg[i % 2][:, :], OP.add).then_inc(s_v.h, 1)

def build_program(dbg=(), upto=99, cores=8, hy_groups=16, ffn_layers=2):
    k = K(dbg)
    k.cores = cores
    k.hy_groups = hy_groups
    k.ffn_layers = ffn_layers
    k.declare_dram()
    es = ExitStack()
    with es:
        k.alloc(es)
        nc = k.nc
        with ExitStack() as es1:
            pa = es1.enter_context(nc.sbuf_tensor("pa", [128, T], F32))
            xtok = [es1.enter_context(nc.sbuf_tensor(f"xtok{i}", [128, D], F32)) for i in range(2)]
            xn_tok = es1.enter_context(nc.sbuf_tensor("xn_tok", [128, D], BF16))
            ps_t = [es1.enter_context(nc.psum_tensor(f"ps_t{i}", [128, 1024], BF16)) for i in range(2)]
            with nc.named_scope("p01"):
                k.p0_seed(pa, pa)
                if upto >= 1:
                    k.p1_ln1(pa, xtok, xn_tok, ps_t)
        if upto >= 2:
            k.barrier()
            with nc.named_scope("p2"):
                k.p2_proj()
        if upto >= 3:
            with nc.named_scope("hyena"):
                k.p_hyena()
        if upto >= 4:
            k.p_ffn()
        if upto >= 5:
            with nc.named_scope("p7"):
                k.p7_combine()
    return k


_PROG_CACHE = {}


def kernel(**inputs):
    X = inputs["L0_attn_X"]
    Wp = np.ascontiguousarray(inputs["L0_attn_Wp"], np.float32)
    bp = np.ascontiguousarray(inputs["L0_attn_bp"], np.float32)
    Wf = np.ascontiguousarray(inputs["L0_attn_Wf"], np.float32)
    bf = np.ascontiguousarray(inputs["L0_attn_bf"], np.float32)
    shared = {
        "Wp": Wp, "bp": bp, "Wf": Wf, "bf": bf,
        "wqkv": np.ascontiguousarray(inputs["L0_ffn_wqkv"], np.float32),
        "q_scale": np.ascontiguousarray(inputs["L0_ffn_q_scale"], np.float32),
        "k_scale": np.ascontiguousarray(inputs["L0_ffn_k_scale"], np.float32),
        "wo": np.ascontiguousarray(inputs["L0_ffn_wo"], np.float32),
        "ln_g": np.ascontiguousarray(inputs["L0_ffn_ln_g"], np.float32),
        "ln_b": np.ascontiguousarray(inputs["L0_ffn_ln_b"], np.float32),
        "w1": np.ascontiguousarray(inputs["L0_ffn_w1"], np.float32),
        "b1": np.ascontiguousarray(inputs["L0_ffn_b1"], np.float32),
        "w2": np.ascontiguousarray(inputs["L0_ffn_w2"], np.float32),
        "b2": np.ascontiguousarray(inputs["L0_ffn_b2"], np.float32),
    }
    if "prog" not in _PROG_CACHE:
        _PROG_CACHE["prog"] = build_program(upto=5, cores=8, hy_groups=16, ffn_layers=2)
    k = _PROG_CACHE["prog"]
    in_maps = []
    for core in range(8):
        cols = np.concatenate([i * D + core * 128 + np.arange(128) for i in range(3)])
        m = dict(shared)
        m["x"] = np.ascontiguousarray(X[core], np.float32)
        m["Wf_my"] = np.ascontiguousarray(Wf[:, cols])
        m["bf_my"] = np.ascontiguousarray(bf[cols])
        in_maps.append(m)
    res = run_bass_kernel_spmd(k.nc, in_maps, core_ids=list(range(8)))
    out = np.stack([res.results[c]["out"] for c in range(8)], 0).astype(np.float32)
    return out


if __name__ == "__main__":
    k = build_program(upto=5, cores=8)
    print("built ok")


# version 11 (final)
# speedup vs baseline: 1.0178x; 1.0178x over previous
import numpy as np
import math
from contextlib import ExitStack

import concourse.bass as bass
import concourse.mybir as mybir
from concourse.alu_op_type import AluOpType
from concourse.bass_utils import run_bass_kernel_spmd

F32 = mybir.dt.float32
BF16 = mybir.dt.bfloat16
I32 = mybir.dt.int32
AF = mybir.ActivationFunctionType
OP = AluOpType
AX = mybir.AxisListType.X

T = 4096
D = 1024
L = 8192
NCH = D // 128
TS = T + 1
NK1 = 33
HID = 2730
INNER = 512
HEADS = 8
DH = 64
QK_SCALE = 8.0
ENGS = ("tensor", "vector", "scalar", "gpsimd", "sync")


class Sem:
    def __init__(self, es, nc, name):
        self.h = es.enter_context(nc.semaphore(name))
        self.v = 0

    def tick(self, k=1):
        self.v += k
        return self.v


class K:
    def __init__(self, dbg=()):
        self.dbg = set(dbg)
        self.nc = bass.Bass("TRN2", target_bir_lowering=False, debug=False)
        self.dbg_out = {}

    def declare_dram(self):
        nc = self.nc
        self.din = {}
        for name, shape in [
            ("x", [T, D]), ("Wp", [D, 4 * D]), ("bp", [4 * D]),
            ("Wf", [D, 3 * D]), ("bf", [3 * D]),
            ("Wf_my", [D, 384]), ("bf_my", [384]),
            ("wqkv", [2, D, 3 * INNER]), ("q_scale", [2, DH]), ("k_scale", [2, DH]),
            ("wo", [2, INNER, D]), ("ln_g", [2, D]), ("ln_b", [2, D]),
            ("w1", [2, D, 2 * HID]), ("b1", [2, 2 * HID]),
            ("w2", [2, HID, D]), ("b2", [2, D]),
        ]:
            self.din[name] = nc.dram_tensor(name, shape, F32, kind="ExternalInput").ap()
        self.out = nc.dram_tensor("out", [T, D], F32, kind="ExternalOutput").ap()
        self.xp_dram = nc.dram_tensor("xp_dram", [4 * D, T], BF16)
        self.f_dram = nc.dram_tensor("f_dram", [384, T], BF16)
        self.fsh_dram = nc.dram_tensor("fsh_dram", [128, 384 * 66], BF16)
        self.fall_dram = nc.dram_tensor("fall_dram", [8 * 128, 384 * 66], BF16)
        self.y3_dram = nc.dram_tensor("y3_dram", [128, 32 * D], BF16)
        self.pl_dram = nc.dram_tensor("pl_dram", [128, 4 * T], BF16)
        self.q_dram = nc.dram_tensor("q_dram", [4 * 128, T], BF16)
        self.k_dram = nc.dram_tensor("k_dram", [4 * 128, T], BF16)
        self.v_dram = nc.dram_tensor("v_dram", [T, INNER], BF16)
        self.w1bf = nc.dram_tensor("w1bf", [128, 8 * 2 * HID], BF16)
        self.w2bf = nc.dram_tensor("w2bf", [HID, D], BF16)

    def dbg_tap(self, name, shape, dt=F32):
        if name in self.dbg and name not in self.dbg_out:
            self.dbg_out[name] = self.nc.dram_tensor("dbg_" + name, shape, dt, kind="ExternalOutput").ap()
        return self.dbg_out.get(name)

    def alloc(self, es):
        nc = self.nc
        sb = lambda n, s, d=BF16: es.enter_context(nc.sbuf_tensor(n, s, d))
        self.es_root = es
        self.xstream = sb("xstream", [128, NCH * TS])
        self.ident = sb("ident", [128, 128])
        self.pi2 = sb("pi2", [128, 2], F32)
        self.iotap = sb("iotap", [128, 4], F32)
        self.r2pi = sb("r2pi", [128, 1], F32)
        self.scr = sb("scr", [128, 64], F32)
        self.lncol = sb("lncol", [128, 8], F32)
        self.w64stk = sb("w64stk", [32, 99])
        self.g_tab = sb("g_tab", [66, 32])
        self.CONSTS = [1e-5, -0.5, 0.5, -1.0, 2.0, 1e-24, 0.25, -0.25, -8.0,
                       8.0, 0.125, -2048.0, 0.00390625, 64.0, -64.0]
        self.fc = sb("fc", [128, len(self.CONSTS)], F32)
        for i, val in enumerate(self.CONSTS):
            nc.const_aps.register(F32, val, self.fc[:, i:i + 1])
        self.s_ld = Sem(es, nc, "s_ld")
        self.s_st = Sem(es, nc, "s_st")
        self.s_t = Sem(es, nc, "s_t")
        self.s_v = Sem(es, nc, "s_v")
        self.s_a = Sem(es, nc, "s_a")
        self.s_p = Sem(es, nc, "s_p")
        self.s_cc = Sem(es, nc, "s_cc")
        self.s_bar = Sem(es, nc, "s_bar")
        self.s_dma = [Sem(es, nc, f"s_dma{i}") for i in range(8)]

    def barrier(self, engines=("tensor", "vector", "scalar", "sync", "gpsimd")):
        nc = self.nc
        tgt = self.s_bar.tick(len(engines))
        with nc.Block() as b:
            for e in engines:
                def mk(e=e):
                    def fn(eng):
                        eng.drain().then_inc(self.s_bar.h, 1)
                        eng.wait_ge(self.s_bar.h, tgt)
                    return fn
                getattr(b, e)(mk())

    def p0_seed(self, plane, _unused=None):
        nc, s_p, s_v, s_a = self.nc, self.s_p, self.s_v, self.s_a
        pa = plane[:, 0:128]
        pb = plane[:, 128:256]
        pa_i, pb_i = pa.bitcast(I32), pb.bitcast(I32)
        t_io = s_p.tick()
        t_id = s_v.tick()
        t_pi = s_a.tick()
        t_r2 = s_v.tick()
        with nc.Block() as b:
            @b.gpsimd
            def _(g):
                g.iota(pa_i[:, :], [[1, 128]], channel_multiplier=0)
                g.iota(pb_i[:, :], [[0, 128]], channel_multiplier=1).then_inc(s_p.h, 1)

            @b.vector
            def _(v):
                v.wait_ge(s_p.h, t_io)
                v.tensor_copy(self.iotap[:, 0:1], pb_i[:, 0:1])
                v.tensor_tensor(pa[:, :], pa_i[:, :], pb_i[:, :], OP.is_equal)
                v.drain()
                v.tensor_copy(self.ident[:], pa[:, :])
                for i, val in enumerate(self.CONSTS):
                    v.memset(self.fc[:, i:i + 1], val)
                v.memset(self.xstream[:, :], 0.0)
                v.drain().then_inc(s_v.h, 1)
                ii = pb_i[:, 1:2]
                v.tensor_scalar(ii, pb_i[:, 0:1], 1, None, OP.arith_shift_right)
                v.drain()
                v.tensor_copy(self.iotap[:, 1:2], ii)
                v.drain()
                v.tensor_tensor(self.iotap[:, 2:3], self.iotap[:, 1:2], self.iotap[:, 1:2], OP.add)
                v.drain()
                v.tensor_tensor(self.iotap[:, 2:3], self.iotap[:, 0:1], self.iotap[:, 2:3], OP.subtract)
                v.drain()
                v.tensor_scalar(self.iotap[:, 2:3], self.iotap[:, 2:3], 0.25, None, OP.mult)
                v.drain()
                v.wait_ge(s_a.h, t_pi)
                v.reciprocal(self.r2pi[:, 0:1], self.pi2[:, 1:2])
                v.drain().then_inc(s_v.h, 1)

            @b.scalar
            def _(s):
                s.wait_ge(s_v.h, t_id)
                s.activation(self.pi2[:, 0:1], self.iotap[:, 0:1], AF.Identity, bias=1.0, scale=0.0)
                s.drain()
                s.activation(self.pi2[:, 0:1], self.pi2[:, 0:1], AF.Arctan)
                s.drain()
                s.mul(self.pi2[:, 0:1], self.pi2[:, 0:1], 4.0)
                s.drain()
                s.mul(self.pi2[:, 1:2], self.pi2[:, 0:1], 2.0)
                s.drain()
                s.activation(self.iotap[:, 3:4], self.iotap[:, 0:1], AF.Identity, bias=1.0, scale=0.0)
                s.drain()
                s.activation(self.iotap[:, 3:4], self.iotap[:, 3:4], AF.Ln, scale=10000.0)
                s.drain().then_inc(s_a.h, 1)
        self.t_seed_a = t_pi
        self.t_seed_v2 = t_r2

    def p1_ln1(self, pa, xtok, xn_tok, ps_t):
        nc = self.nc
        s_t, s_v, s_a = self.s_t, self.s_v, self.s_a
        sd = self.s_dma
        ld = [sd[tb % 2].tick(16) for tb in range(32)]
        a_sq, v_st, a_rs, v_bi, a_ap, t_tp, v_ev = [], [], [], [], [], [], []
        for tb in range(32):
            a_sq.append(s_a.tick())
            v_st.append(s_v.tick())
            a_rs.append(s_a.tick())
            v_bi.append(s_v.tick())
            a_ap.append(s_a.tick())
            t_tp.append(s_t.tick())
            v_ev.append(s_v.tick())
        self.p1_done_v = v_ev[31]

        with nc.Block() as b:
            @b.sync
            def _(sync):
                for tb in range(32):
                    if tb >= 2:
                        sync.wait_ge(s_a.h, a_ap[tb - 2])
                        sync.wait_ge(s_v.h, v_st[tb - 2])
                    sync.dma_start(out=xtok[tb % 2][:, :],
                                   in_=self.din["x"][tb * 128:(tb + 1) * 128, :]).then_inc(sd[tb % 2].h, 16)

            @b.scalar
            def _(s):
                c = self.lncol
                s.wait_ge(self.s_a.h, self.t_seed_a)
                for tb in range(32):
                    s.wait_ge(sd[tb % 2].h, ld[tb])
                    s.activation(pa[:, 0:D], xtok[tb % 2][:, :], AF.Square,
                                 accum_out=c[:, 1:2]).then_inc(s_a.h, 1)
                    s.wait_ge(s_v.h, v_st[tb])
                    s.activation(c[:, 5:6], c[:, 4:5], AF.Ln, bias=1e-5)
                    s.drain()
                    s.activation(c[:, 6:7], c[:, 5:6], AF.Exp, scale=-0.5)
                    s.drain().then_inc(s_a.h, 1)
                    s.wait_ge(s_v.h, v_bi[tb])
                    s.activation(xn_tok[:, :], xtok[tb % 2][:, :], AF.Identity,
                                 bias=c[:, 7:8], scale=c[:, 6:7]).then_inc(s_a.h, 1)

            @b.vector
            def _(v):
                c = self.lncol
                for tb in range(32):
                    v.wait_ge(sd[tb % 2].h, ld[tb])
                    v.wait_ge(s_a.h, a_sq[tb])
                    v.reduce_sum(c[:, 0:1], xtok[tb % 2][:, :], axis=AX)
                    v.drain()
                    v.tensor_scalar(c[:, 2:3], c[:, 0:1], 1.0 / D, None, OP.mult)
                    v.tensor_scalar(c[:, 3:4], c[:, 1:2], 1.0 / D, None, OP.mult)
                    v.drain()
                    v.tensor_tensor(c[:, 4:5], c[:, 2:3], c[:, 2:3], OP.mult)
                    v.drain()
                    v.tensor_tensor(c[:, 4:5], c[:, 3:4], c[:, 4:5], OP.subtract)
                    v.drain().then_inc(s_v.h, 1)
                    v.wait_ge(s_a.h, a_rs[tb])
                    v.tensor_tensor(c[:, 7:8], c[:, 2:3], c[:, 6:7], OP.mult)
                    v.drain()
                    v.tensor_scalar(c[:, 7:8], c[:, 7:8], -1.0, None, OP.mult)
                    v.drain().then_inc(s_v.h, 1)
                    v.wait_ge(s_t.h, t_tp[tb])
                    pt = ps_t[tb % 2]
                    v.tensor_copy(
                        self.xstream.rearrange("p (c t) -> p c t", t=TS)[:, :, 1 + tb * 128:1 + (tb + 1) * 128],
                        pt.rearrange("p (c t) -> p c t", t=128)[:, :, :],
                    ).then_inc(s_v.h, 1)

            @b.tensor
            def _(t):
                for tb in range(32):
                    t.wait_ge(s_a.h, a_ap[tb])
                    if tb >= 2:
                        t.wait_ge(s_v.h, v_ev[tb - 2])
                    pt = ps_t[tb % 2]
                    for dc in range(NCH):
                        mm = t.transpose(pt[:, dc * 128:(dc + 1) * 128],
                                         xn_tok[:, dc * 128:(dc + 1) * 128], self.ident[:])
                    mm.then_inc(s_t.h, 1)

        tap = self.dbg_tap("xn_feat", [128, NCH * TS], BF16)
        if tap is not None:
            with nc.Block() as b:
                td = self.s_st.tick(16)

                @b.sync
                def _(sync):
                    sync.wait_ge(s_v.h, v_ev[31])
                    sync.dma_start(out=tap, in_=self.xstream[:, :]).then_inc(self.s_st.h, 16)
                    sync.wait_ge(self.s_st.h, td)

    def p2_proj(self):
        nc = self.nc
        s_t, s_v, s_a, s_p = self.s_t, self.s_v, self.s_a, self.s_p
        sd = self.s_dma
        with ExitStack() as es:
            sb = lambda n, s, d=BF16: es.enter_context(nc.sbuf_tensor(n, s, d))
            ps = lambda n, s, d=F32: es.enter_context(nc.psum_tensor(n, s, d))
            posc = sb("p2_posc", [128, NCH * T])
            g = sb("p2_g", [128, T], F32)
            ucol = sb("p2_ucol", [128, 8], F32)
            upl = sb("p2_upl", [128, T], F32)
            fr = sb("p2_fr", [128, T], F32)
            wstg = [sb(f"p2_wstg{i}", [128, 1024], F32) for i in range(2)]
            wbf = [sb(f"p2_wbf{i}", [128, 1024]) for i in range(2)]
            ostg = [sb(f"p2_ostg{i}", [128, 512]) for i in range(2)]
            brows = sb("p2_brows", [35, 128], F32)
            brows_bf = sb("p2_brows_bf", [35, 128])
            bcols = sb("p2_bcols", [128, 35], F32)
            psA = ps("p2_psA", [128, 512])
            psB = ps("p2_psB", [128, 512])
            psbc = ps("p2_psbc", [128, 35], BF16)

            t_bld = sd[6].tick(16 * 2)
            t_bcast = s_v.tick()
            t_bT = s_t.tick()
            t_bc = s_v.tick()
            with nc.Block() as b:
                @b.sync
                def _(sync):
                    sync.dma_start(out=brows[0:32, :],
                                   in_=self.din["bp"].rearrange("(a c) -> a c", c=128)).then_inc(sd[6].h, 16)
                    sync.dma_start(out=brows[32:35, :],
                                   in_=self.din["bf_my"].rearrange("(a c) -> a c", c=128)).then_inc(sd[6].h, 16)

                @b.vector
                def _(v):
                    v.wait_ge(sd[6].h, t_bld)
                    v.tensor_copy(brows_bf[:, :], brows[:, :])
                    v.drain().then_inc(s_v.h, 1)
                    v.wait_ge(s_t.h, t_bT)
                    v.tensor_copy(bcols[:, :], psbc[:, :])
                    v.drain().then_inc(s_v.h, 1)

                @b.tensor
                def _(t):
                    t.wait_ge(s_v.h, t_bcast)
                    t.transpose(psbc[:, :], brows_bf[:, :], self.ident[0:35, 0:35]).then_inc(s_t.h, 1)

            p_io, v_e, a_if, v_tf, a_up, v_fr, a_sin = [], [], [], [], [], [], []
            for ck in range(NCH):
                p_io.append(s_p.tick())
                v_e.append(s_v.tick())
                a_if.append(s_a.tick())
                v_tf.append(s_v.tick())
                a_up.append(s_a.tick())
                v_fr.append(s_v.tick())
                a_sin.append(s_a.tick())
            with nc.Block() as b:
                @b.gpsimd
                def _(gp):
                    for ck in range(NCH):
                        if ck >= 1:
                            gp.wait_ge(s_v.h, v_fr[ck - 1])
                        gp.iota(upl.bitcast(I32)[:, 0:T], [[1, T]], channel_multiplier=0).then_inc(s_p.h, 1)

                @b.vector
                def _(v):
                    for ck in range(NCH):
                        if ck == 0:
                            v.wait_ge(s_v.h, self.t_seed_v2)
                        v.tensor_scalar(ucol[:, 0:1], self.iotap[:, 1:2], -1.0 / 512, None, OP.mult)
                        v.drain()
                        v.tensor_scalar(ucol[:, 0:1], ucol[:, 0:1], -ck * 0.125, None, OP.add)
                        v.drain()
                        v.drain().then_inc(s_v.h, 1)
                        v.wait_ge(s_a.h, a_if[ck])
                        v.tensor_tensor(ucol[:, 2:3], ucol[:, 1:2], self.r2pi[:, 0:1], OP.mult)
                        v.drain()
                        v.wait_ge(s_p.h, p_io[ck])
                        v.tensor_copy(upl[:, :], upl.bitcast(I32)[:, 0:T])
                        v.drain().then_inc(s_v.h, 1)
                        v.wait_ge(s_a.h, a_up[ck])
                        v.tensor_copy(g.bitcast(I32)[:, 0:T], upl[:, :])
                        v.drain()
                        v.tensor_copy(g[:, :], g.bitcast(I32)[:, 0:T])
                        v.drain()
                        v.tensor_tensor(fr[:, :], upl[:, :], g[:, :], OP.subtract)
                        v.drain()
                        v.tensor_scalar(g[:, :], fr[:, :], 0.5, None, OP.is_gt)
                        v.drain()
                        v.tensor_tensor(fr[:, :], fr[:, :], g[:, :], OP.subtract)
                        v.drain()
                        v.tensor_scalar(g[:, :], fr[:, :], -0.5, None, OP.is_lt)
                        v.drain()
                        v.tensor_tensor(fr[:, :], fr[:, :], g[:, :], OP.add)
                        v.drain().then_inc(s_v.h, 1)

                @b.scalar
                def _(s):
                    for ck in range(NCH):
                        s.wait_ge(s_v.h, v_e[ck])
                        s.activation(ucol[:, 1:2], ucol[:, 0:1], AF.Exp,
                                     scale=self.iotap[:, 3:4]).then_inc(s_a.h, 1)
                        s.wait_ge(s_v.h, v_tf[ck])
                        s.activation(upl[:, :], upl[:, :], AF.Identity,
                                     scale=ucol[:, 2:3], bias=self.iotap[:, 2:3]).then_inc(s_a.h, 1)
                        s.wait_ge(s_v.h, v_fr[ck])
                        s.activation(posc[:, ck * T:(ck + 1) * T], fr[:, :], AF.Sin,
                                     scale=self.pi2[:, 1:2]).then_inc(s_a.h, 1)

            NT2 = 35
            w_ld = [sd[2 + ft % 2].tick(16) for ft in range(NT2)]
            v_cast = [s_v.tick() for _ in range(NT2)]
            t_mm, a_ev, st_out = [], [], []
            for ft in range(NT2):
                t_mm.append([s_t.tick() for _ in range(8)])
                a_ev.append([s_a.tick() for _ in range(8)])
                st_out.append([sd[4 + (ft * 8 + tt) % 2].tick(16) for tt in range(8)])

            with nc.Block() as b:
                @b.sync
                def _(sync):
                    def emit_load(ft):
                        if ft >= 2:
                            sync.wait_ge(s_v.h, v_cast[ft - 2])
                        if ft < 32:
                            wsrc = self.din["Wp"][:, ft * 128:(ft + 1) * 128]
                        else:
                            wsrc = self.din["Wf_my"][:, (ft - 32) * 128:(ft - 31) * 128]
                        wsrc3 = wsrc.rearrange("(c p) f -> c p f", p=128).transpose([1, 0, 2])
                        sync.dma_start(out=wstg[ft % 2].rearrange("p (c f) -> p c f", f=128),
                                       in_=wsrc3).then_inc(sd[2 + ft % 2].h, 16)

                    def emit_stores(ft):
                        for tt in range(8):
                            u = ft * 8 + tt
                            sync.wait_ge(s_a.h, a_ev[ft][tt])
                            dst = self.xp_dram if ft < 32 else self.f_dram
                            r0 = ft * 128 if ft < 32 else (ft - 32) * 128
                            sync.dma_start(out=dst.ap()[r0:r0 + 128, tt * 512:(tt + 1) * 512],
                                           in_=ostg[u % 2][:, :]).then_inc(sd[4 + u % 2].h, 16)
                    for ft in range(NT2):
                        emit_load(ft)
                        if ft >= 1:
                            emit_stores(ft - 1)
                    emit_stores(NT2 - 1)
                    sync.wait_ge(sd[4].h, sd[4].v)
                    sync.wait_ge(sd[5].h, sd[5].v)

                @b.vector
                def _(v):
                    for ft in range(NT2):
                        v.wait_ge(sd[2 + ft % 2].h, w_ld[ft])
                        if ft >= 2:
                            v.wait_ge(s_t.h, t_mm[ft - 2][7])
                        v.tensor_copy(wbf[ft % 2][:, :], wstg[ft % 2][:, :])
                        v.drain().then_inc(s_v.h, 1)

                @b.tensor
                def _(t):
                    for ft in range(NT2):
                        t.wait_ge(s_v.h, v_cast[ft])
                        if ft == 0:
                            t.wait_ge(s_v.h, self.p1_done_v)
                        if ft == 32:
                            t.wait_ge(s_a.h, a_sin[7])
                        for tt in range(8):
                            u = ft * 8 + tt
                            pp = psA if u % 2 == 0 else psB
                            if u >= 2:
                                t.wait_ge(s_a.h, a_ev[(u - 2) // 8][(u - 2) % 8])
                            for ck in range(NCH):
                                if ft < 32:
                                    rhs = self.xstream.rearrange("p (c t) -> p c t", t=TS)[:, ck, 1 + tt * 512:1 + (tt + 1) * 512]
                                else:
                                    rhs = posc[:, ck * T + tt * 512:ck * T + (tt + 1) * 512]
                                mm = t.matmul(pp[:, :], wbf[ft % 2][:, ck * 128:(ck + 1) * 128], rhs,
                                              start=(ck == 0), stop=(ck == 7))
                            mm.then_inc(s_t.h, 1)

                @b.scalar
                def _(s):
                    for ft in range(NT2):
                        bc = bcols[:, ft:ft + 1]
                        s.wait_ge(s_v.h, t_bc)
                        for tt in range(8):
                            u = ft * 8 + tt
                            pp = psA if u % 2 == 0 else psB
                            s.wait_ge(s_t.h, t_mm[ft][tt])
                            if u >= 2:
                                s.wait_ge(sd[4 + u % 2].h, st_out[(u - 2) // 8][(u - 2) % 8])
                            s.activation(ostg[u % 2][:, :], pp[:, :], AF.Identity,
                                         bias=bc, scale=1.0).then_inc(s_a.h, 1)
        self.p2_end = dict(v=self.s_v.v, a=self.s_a.v, t=self.s_t.v)

    def p_hyena(self):
        nc = self.nc
        s_t, s_v, s_a, s_p, s_cc = self.s_t, self.s_v, self.s_a, self.s_p, self.s_cc
        sd = self.s_dma
        NG = 16
        with ExitStack() as es:
            sb = lambda n, s, d=BF16: es.enter_context(nc.sbuf_tensor(n, s, d))
            ps = lambda n, s, d=F32: es.enter_context(nc.psum_tensor(n, s, d))
            m_re = sb("hy_mre", [128, NK1 * 128])
            m_im = sb("hy_mim", [128, NK1 * 128])
            n_re = sb("hy_nre", [128, NK1 * 128])
            n_im = sb("hy_nim", [128, NK1 * 128])
            n_imn = sb("hy_nimn", [128, NK1 * 128])
            w64r = sb("hy_w64r", [128, 99])
            w64b = sb("hy_w64b", [128, 396])
            g_tab = sb("hy_gtab", [128, 32])
            qtb = sb("hy_qtb", [128, 8192])
            ytime = [sb(f"hy_yt{i}", [128, 2048]) for i in range(2)]
            xpg = sb("hy_xpg", [128, 2048])
            ps1 = [ps("hy_ps1a", [128, 396]), ps("hy_ps1b", [128, 396])]
            psX = [ps("hy_xre0", [128, 512]), ps("hy_xim0", [128, 512]),
                   ps("hy_xre1", [128, 512]), ps("hy_xim1", [128, 512])]
            psQ = [ps("hy_qre", [128, 512]), ps("hy_qim", [128, 512])]
            psT = psX[1].bitcast(BF16)
            psT2 = psX[3].bitcast(BF16)
            psI2 = psX[0]
            psI2b = psX[2]
            def s1ps(bb):
                return [ps1[0][:, :], ps1[1][:, :], psQ[0][:, 0:396], psQ[1][:, 0:396]][bb % 4]

            with ExitStack() as es2:
                sb2 = lambda n, s, d=F32: es2.enter_context(nc.sbuf_tensor(n, s, d))
                ta = sb2("tg_a", [128, NK1 * 128])
                tb = sb2("tg_b", [128, NK1 * 128])
                tgs = sb2("tg_s", [128, NK1 * 128])

                def wrap(v, fr, gg):
                    v.tensor_copy(gg.bitcast(I32), fr)
                    v.drain()
                    v.tensor_copy(gg, gg.bitcast(I32))
                    v.drain()
                    v.tensor_tensor(fr, fr, gg, OP.subtract)
                    v.drain()
                    v.tensor_scalar(gg, fr, 0.5, None, OP.is_gt)
                    v.drain()
                    v.tensor_tensor(fr, fr, gg, OP.subtract)
                    v.drain()
                    v.tensor_scalar(gg, fr, -0.5, None, OP.is_lt)
                    v.drain()
                    v.tensor_tensor(fr, fr, gg, OP.add)
                    v.drain()

                ticks_v = [s_v.tick() for _ in range(8)]
                ticks_a = [s_a.tick() for _ in range(8)]
                ticks_p = [s_p.tick() for _ in range(3)]
                with nc.Block() as b:
                    @b.gpsimd
                    def _(g):
                        g.iota(ta[:, :].bitcast(I32), [[1, NK1], [64, 128]], channel_multiplier=0)
                        g.iota(tb[:, :].bitcast(I32), [[0, NK1], [0, 128]], channel_multiplier=1).then_inc(s_p.h, 1)
                        g.wait_ge(s_a.h, ticks_a[1])
                        g.iota(ta[:, :].bitcast(I32), [[0, NK1], [1, 128]], channel_multiplier=0)
                        g.iota(tb[:, :].bitcast(I32), [[1, NK1], [0, 128]], channel_multiplier=64).then_inc(s_p.h, 1)
                        g.wait_ge(s_a.h, ticks_a[4])
                        g.iota(tb[:, :].bitcast(I32), [[1, NK1], [0, 128]], channel_multiplier=0)
                        g.iota(tgs[:, :].bitcast(I32)[:, 0:1], [[0, 1]], channel_multiplier=1)
                        g.iota(tgs[:, :].bitcast(I32)[:, 1:34], [[1, 33]], channel_multiplier=0)
                        g.iota(tgs[:, :].bitcast(I32)[:, 34:66], [[1, 32]], channel_multiplier=0).then_inc(s_p.h, 1)

                    @b.vector
                    def _(v):
                        v.wait_ge(s_p.h, ticks_p[0])
                        v.tensor_tensor(ta[:, :].bitcast(I32), ta[:, :].bitcast(I32), tb[:, :].bitcast(I32), OP.mult)
                        v.drain()
                        v.tensor_copy(ta[:, :], ta[:, :].bitcast(I32))
                        v.drain()
                        v.tensor_scalar(ta[:, :], ta[:, :], 2.0 ** -13, None, OP.mult)
                        v.drain()
                        v.tensor_scalar(tb[:, :], ta[:, :], -1.0, None, OP.mult)
                        v.drain()
                        wrap(v, tb[:, :], tgs[:, :])
                        v.drain().then_inc(s_v.h, 1)
                        v.wait_ge(s_a.h, ticks_a[0])
                        v.tensor_scalar(tb[:, :], ta[:, :], 0.25, None, OP.add)
                        v.drain()
                        wrap(v, tb[:, :], tgs[:, :])
                        v.drain().then_inc(s_v.h, 1)
                        v.wait_ge(s_p.h, ticks_p[1])
                        v.wait_ge(s_a.h, ticks_a[1])
                        v.tensor_tensor(ta[:, :].bitcast(I32), ta[:, :].bitcast(I32), tb[:, :].bitcast(I32), OP.mult)
                        v.drain()
                        v.tensor_copy(ta[:, :], ta[:, :].bitcast(I32))
                        v.drain()
                        v.tensor_scalar(ta[:, :], ta[:, :], 2.0 ** -13, None, OP.mult)
                        v.drain()
                        v.tensor_copy(tb[:, :], ta[:, :])
                        v.drain()
                        wrap(v, tb[:, :], tgs[:, :])
                        v.drain().then_inc(s_v.h, 1)
                        v.wait_ge(s_a.h, ticks_a[2])
                        v.tensor_scalar(tb[:, :], ta[:, :], 0.25, None, OP.add)
                        v.drain()
                        wrap(v, tb[:, :], tgs[:, :])
                        v.drain().then_inc(s_v.h, 1)
                        v.wait_ge(s_a.h, ticks_a[3])
                        v.drain().then_inc(s_v.h, 1)
                        v.wait_ge(s_p.h, ticks_p[2])
                        v.tensor_copy(tb[:, :], tb[:, :].bitcast(I32))
                        v.drain()
                        v.tensor_scalar(ta[:, :], tb[:, :], 0.0, None, OP.is_equal)
                        v.drain()
                        v.tensor_scalar(tb[:, :], tb[:, :], 32.0, None, OP.is_equal)
                        v.drain()
                        v.tensor_tensor(ta[:, :], ta[:, :], tb[:, :], OP.add)
                        v.drain()
                        v.tensor_scalar(tb[:, :], ta[:, :], -1.0, None, OP.mult)
                        v.drain()
                        v.tensor_scalar(tb[:, :], tb[:, :], 2.0, None, OP.add)
                        v.drain()
                        v.tensor_scalar(tb[:, :], tb[:, :], 2.0 ** -13, None, OP.mult)
                        v.drain()
                        v.tensor_tensor(n_re[:, :], n_re[:, :], tb[:, :], OP.mult)
                        v.drain()
                        v.tensor_tensor(n_im[:, :], n_im[:, :], tb[:, :], OP.mult)
                        v.drain()
                        v.tensor_scalar(n_imn[:, :], n_im[:, :], -1.0, None, OP.mult)
                        v.drain().then_inc(s_v.h, 1)
                        ic = tgs[:, :].bitcast(I32)
                        v.memset(w64b[:, :], 0.0)
                        v.drain()
                        v.tensor_scalar(ic[:, 70:71], ic[:, 0:1], 5, None, OP.arith_shift_right)
                        v.drain()
                        v.tensor_scalar(ic[:, 70:71], ic[:, 70:71], 5, None, OP.arith_shift_left)
                        v.drain()
                        v.tensor_tensor(ic[:, 70:71], ic[:, 0:1], ic[:, 70:71], OP.subtract)
                        v.drain()
                        v.tensor_copy(self.scr[:, 9:10], ic[:, 70:71])
                        v.drain()
                        v.tensor_copy(ta[:, 0:33], ic[:, 1:34])
                        v.drain()
                        v.tensor_tensor(ta[:, 0:33], ta[:, 0:33],
                                        self.scr[:, 9:10].broadcast_to([128, 33]), OP.mult)
                        v.drain()
                        v.tensor_scalar(ta[:, 0:33], ta[:, 0:33], 1.0 / 64, None, OP.mult)
                        v.drain()
                        v.tensor_scalar(tb[:, 0:33], ta[:, 0:33], 0.25, None, OP.add)
                        v.tensor_scalar(tb[:, 33:66], ta[:, 0:33], -1.0, None, OP.mult)
                        v.tensor_copy(tb[:, 66:99], ta[:, 0:33])
                        v.drain()
                        wrap(v, tb[:, 0:99], tgs[:, 100:199])
                        v.drain().then_inc(s_v.h, 1)
                        v.wait_ge(s_a.h, ticks_a[5])
                        v.tensor_copy(ta[:, 0:32], ic[:, 34:66])
                        v.drain()
                        v.tensor_tensor(ta[:, 0:32], ta[:, 0:32],
                                        self.iotap[:, 1:2].broadcast_to([128, 32]), OP.mult)
                        v.drain()
                        v.tensor_scalar(ta[:, 0:32], ta[:, 0:32], 1.0 / 64, None, OP.mult)
                        v.drain()
                        v.tensor_scalar(self.scr[:, 10:11], self.iotap[:, 2:3], -8.0, None, OP.mult)
                        v.drain()
                        v.tensor_scalar(self.scr[:, 10:11], self.scr[:, 10:11], 1.0, None, OP.add)
                        v.drain()
                        v.tensor_scalar(self.scr[:, 11:12], self.iotap[:, 2:3], -1.0, None, OP.mult)
                        v.drain()
                        v.tensor_scalar(self.scr[:, 11:12], self.scr[:, 11:12], 0.25, None, OP.add)
                        v.drain()
                        v.tensor_tensor(ta[:, 0:32], ta[:, 0:32],
                                        self.scr[:, 10:11].broadcast_to([128, 32]), OP.mult)
                        v.drain()
                        v.tensor_tensor(ta[:, 0:32], ta[:, 0:32],
                                        self.scr[:, 11:12].broadcast_to([128, 32]), OP.add)
                        v.drain()
                        wrap(v, ta[:, 0:32], tgs[:, 200:232])
                        v.drain().then_inc(s_v.h, 1)

                    @b.scalar
                    def _(s):
                        s.wait_ge(s_v.h, ticks_v[0])
                        s.activation(m_im[:, :], tb[:, :], AF.Sin, scale=self.pi2[:, 1:2])
                        s.drain().then_inc(s_a.h, 1)
                        s.wait_ge(s_v.h, ticks_v[1])
                        s.activation(m_re[:, :], tb[:, :], AF.Sin, scale=self.pi2[:, 1:2])
                        s.drain().then_inc(s_a.h, 1)
                        s.wait_ge(s_v.h, ticks_v[2])
                        s.activation(n_im[:, :], tb[:, :], AF.Sin, scale=self.pi2[:, 1:2])
                        s.drain().then_inc(s_a.h, 1)
                        s.wait_ge(s_v.h, ticks_v[3])
                        s.activation(n_re[:, :], tb[:, :], AF.Sin, scale=self.pi2[:, 1:2])
                        s.drain().then_inc(s_a.h, 1)
                        s.drain().then_inc(s_a.h, 1)
                        s.wait_ge(s_v.h, ticks_v[5])
                        s.drain().then_inc(s_a.h, 1)
                        s.wait_ge(s_v.h, ticks_v[6])
                        s.activation(w64r[:, :], tb[:, 0:99], AF.Sin, scale=self.pi2[:, 1:2])
                        s.drain()
                        for q_ in range(4):
                            s.activation(w64b[32 * q_:32 * (q_ + 1), q_ * 99:(q_ + 1) * 99],
                                         tb[32 * q_:32 * (q_ + 1), 0:99], AF.Sin,
                                         scale=self.pi2[32 * q_:32 * (q_ + 1), 1:2])
                            s.drain()
                        s.drain().then_inc(s_a.h, 1)
                        s.wait_ge(s_v.h, ticks_v[7])
                        s.activation(g_tab[:, :], ta[:, 0:32], AF.Sin, scale=self.pi2[:, 1:2])
                        s.drain().then_inc(s_a.h, 1)

            tapw = self.dbg_tap("w64r", [128, 99], BF16)
            tapm = self.dbg_tap("m_re", [128, NK1 * 128], BF16)
            tapmi = self.dbg_tap("m_im", [128, NK1 * 128], BF16)
            tapn = self.dbg_tap("n_re", [128, NK1 * 128], BF16)
            tapg = self.dbg_tap("g_tab", [128, 32], BF16)
            if tapw is not None:
                with nc.Block() as b:
                    tdd = self.s_st.tick(16 * 5)
                    @b.sync
                    def _(sync):
                        sync.wait_ge(s_a.h, s_a.v)
                        sync.wait_ge(s_v.h, s_v.v)
                        sync.dma_start(out=tapw, in_=w64r[:, :]).then_inc(self.s_st.h, 16)
                        sync.dma_start(out=tapm, in_=m_re[:, :]).then_inc(self.s_st.h, 16)
                        sync.dma_start(out=tapmi, in_=m_im[:, :]).then_inc(self.s_st.h, 16)
                        sync.dma_start(out=tapn, in_=n_re[:, :]).then_inc(self.s_st.h, 16)
                        sync.dma_start(out=tapg, in_=g_tab[:, :]).then_inc(self.s_st.h, 16)
                        sync.wait_ge(self.s_st.h, tdd)

            with ExitStack() as es3:
                sb3 = lambda n, s, d=BF16: es3.enter_context(nc.sbuf_tensor(n, s, d))
                cbuf = sb3("hy_cbuf", [128, 99 * 64])
                pbuf = sb3("hy_pbuf", [128, 66 * 64])
                qbuf = sb3("hy_qbuf", [128, 66 * 64])
                ftile = sb3("hy_ftile", [128, 4224])
                fmt = [sb3(f"hy_fmt{i}", [128, 512]) for i in range(4)]
                y3st = sb3("hy_y3st", [128, 1024])
                OCTS = [(0, 8), (8, 8), (16, 8), (24, 8), (32, 1)]

                def v1_read_ap(dram, row_base):
                    a = dram.ap()
                    a4 = a.rearrange("(r1 q) t -> r1 q t", q=4)
                    a5 = a4.rearrange("r1 q (hi lo) -> r1 q hi lo", lo=128)
                    sl = a5[row_base // 4:row_base // 4 + 16, :, :, :]
                    return sl.transpose([1, 2, 0, 3])

                def fwd_block(ysrc, g, st, fmode, fg=None):
                    ynxt = ytime[(st + 1) % 2]
                    ld_y0 = sd[0].tick(16) if (fmode == 'P' and st == 0) or fmode == 'F' else None
                    ld_f = sd[1].tick(16) if fmode == 'P' else None
                    ld_g = sd[2].tick(16) if fmode == 'P' else None
                    v_s1 = [s_v.tick() for _ in range(8)]
                    a_s1 = [s_a.tick() for _ in range(8)]
                    t_s1 = [s_t.tick() for _ in range(16)]
                    t_s3 = [s_t.tick() for _ in range(len(OCTS))]
                    if fmode == 'F':
                        v_fm = [s_v.tick() for _ in range(len(OCTS))]
                        st_f = sd[3].tick(16)
                    else:
                        a_ev = [s_a.tick() for _ in range(len(OCTS))]
                        v_fm = [s_v.tick() for _ in range(len(OCTS))]
                        t_i1 = [s_t.tick() for _ in range(len(OCTS))]
                        vq = [s_v.tick() for _ in range(len(OCTS))]
                        aq = [s_a.tick() for _ in range(len(OCTS))]
                        t_T = [s_t.tick() for _ in range(8)]
                        v_T = [s_v.tick() for _ in range(8)]
                        t_i2 = [s_t.tick() for _ in range(4)]
                        v_g = [s_v.tick() for _ in range(4)]

                    with nc.Block() as b:
                        @b.sync
                        def _(sync):
                            if fmode == 'F':
                                sync.dma_start(out=ytime[0][:, :],
                                               in_=v1_read_ap(self.f_dram, fg * 64)).then_inc(sd[0].h, 16)
                                sync.wait_ge(s_v.h, v_fm[-1])
                                sync.dma_start(out=self.fsh_dram.ap()[:, fg * 64 * 66:(fg + 1) * 64 * 66],
                                               in_=pbuf[:, :]).then_inc(sd[3].h, 16)
                                sync.wait_ge(sd[3].h, st_f)
                            else:
                                if st == 0:
                                    sync.dma_start(out=ytime[0][:, :],
                                                   in_=v1_read_ap(self.xp_dram, g * 64)).then_inc(sd[0].h, 16)
                                sync.dma_start(out=ftile[:, :],
                                               in_=self.fall_dram.ap()[(g // 2) * 128:(g // 2) * 128 + 128,
                                                                       (st * 128 + (g % 2) * 64) * 66:
                                                                       (st * 128 + (g % 2) * 64 + 64) * 66]).then_inc(sd[1].h, 16)
                                sync.dma_start(out=xpg[:, :],
                                               in_=v1_read_ap(self.xp_dram, (st + 1) * D + g * 64)).then_inc(sd[2].h, 16)

                        @b.tensor
                        def _(t):
                            ysrc_t = ytime[0] if (fmode == 'F' or st == 0) else ytime[st % 2]
                            for bb in range(16):
                                if bb == 0:
                                    if fmode == 'F' or st == 0:
                                        t.wait_ge(sd[0].h, ld_y0)
                                if bb >= 4:
                                    if (bb - 4) % 2 == 0:
                                        t.wait_ge(s_v.h, v_s1[(bb - 4) // 2])
                                    else:
                                        t.wait_ge(s_a.h, a_s1[(bb - 4) // 2])
                                mm = t.matmul(s1ps(bb)[:, 0:396],
                                              ysrc_t[:, bb * 128:(bb + 1) * 128],
                                              w64b[:, :],
                                              start=True, stop=True)
                                mm.then_inc(s_t.h, 1)
                            for j, (k0, kn) in enumerate(OCTS):
                                t.wait_ge(s_v.h, v_s1[7])
                                t.wait_ge(s_a.h, a_s1[7])
                                if fmode == 'F':
                                    if j >= 2:
                                        t.wait_ge(s_v.h, v_fm[j - 2])
                                else:
                                    if j >= 2:
                                        t.wait_ge(s_a.h, a_ev[j - 2])
                                xre, xim = psX[2 * (j % 2)], psX[2 * (j % 2) + 1]
                                for kk in range(kn):
                                    k1 = k0 + kk
                                    c3 = cbuf[:, :].rearrange("p (d s) -> p d s", s=99)
                                    are = c3[:, :, k1]
                                    aim = c3[:, :, 33 + k1]
                                    aimn = c3[:, :, 66 + k1]
                                    mt_re = m_re[:, k1 * 128:(k1 + 1) * 128]
                                    mt_im = m_im[:, k1 * 128:(k1 + 1) * 128]
                                    co = kk * 64
                                    t.matmul(xre[:, co:co + 64], mt_re, are, start=True, stop=False)
                                    t.matmul(xim[:, co:co + 64], mt_im, are, start=True, stop=False)
                                    t.matmul(xim[:, co:co + 64], mt_re, aim, start=False, stop=True)
                                    mm = t.matmul(xre[:, co:co + 64], mt_im, aimn, start=False, stop=True)
                                mm.then_inc(s_t.h, 1)
                            if fmode == 'P':
                                for j, (k0, kn) in enumerate(OCTS):
                                    t.wait_ge(s_v.h, v_fm[j])
                                    if j >= 2:
                                        t.wait_ge(s_v.h, vq[j - 2])
                                        t.wait_ge(s_a.h, aq[j - 2])
                                    qre, qim = (psQ[0], psQ[1]) if j % 2 == 0 else (psX[2], psX[3])
                                    if j == 0:
                                        t.wait_ge(s_a.h, a_ev[-1])
                                        t.wait_ge(s_v.h, v_fm[-1])
                                    for kk in range(kn):
                                        k1 = k0 + kk
                                        pre = pbuf[:, k1 * 64:k1 * 64 + 64]
                                        pim = pbuf[:, 2112 + k1 * 64:2112 + k1 * 64 + 64]
                                        nt_re = n_re[:, k1 * 128:(k1 + 1) * 128]
                                        nt_im = n_im[:, k1 * 128:(k1 + 1) * 128]
                                        nt_imn = n_imn[:, k1 * 128:(k1 + 1) * 128]
                                        co = kk * 64
                                        t.matmul(qre[:, co:co + 64], nt_re, pre, start=True, stop=False)
                                        t.matmul(qim[:, co:co + 64], nt_im, pre, start=True, stop=False)
                                        t.matmul(qim[:, co:co + 64], nt_re, pim, start=False, stop=True)
                                        mm = t.matmul(qre[:, co:co + 64], nt_imn, pim, start=False, stop=True)
                                    mm.then_inc(s_t.h, 1)
                                for tb8 in range(8):
                                    t.wait_ge(s_v.h, vq[-1])
                                    t.wait_ge(s_a.h, aq[-1])
                                    if tb8 >= 2:
                                        t.wait_ge(s_v.h, v_T[tb8 - 2])
                                    pstc = [psT, psT2][tb8 % 2]
                                    for ss in range(8):
                                        d = tb8 * 8 + ss
                                        mm = t.transpose(pstc[0:66, ss * 128:(ss + 1) * 128],
                                                         qbuf[:, d * 66:(d + 1) * 66], self.ident[:, :])
                                    mm.then_inc(s_t.h, 1)
                                qt4 = qtb[0:66, :].rearrange("p (d m) -> p d m", m=128)
                                for gg in range(4):
                                    t.wait_ge(s_v.h, v_T[(gg * 16 + 15) // 8])
                                    if gg >= 2:
                                        t.wait_ge(s_v.h, v_g[gg - 2])
                                    psic = [psI2, psI2b][gg % 2]
                                    for q in range(4):
                                        rhs4 = qt4[:, gg * 16 + q:gg * 16 + q + 13:4, :]
                                        mm = t.matmul(psic[32 * q:32 * (q + 1), 0:512].rearrange("p (d4 m) -> p d4 m", m=128),
                                                      g_tab[0:66, :],
                                                      rhs4,
                                                      start=True, stop=True, tile_position=(0, 32 * q))
                                    mm.then_inc(s_t.h, 1)

                        @b.vector
                        def _(v):
                            for bb in range(0, 16, 2):
                                v.wait_ge(s_t.h, t_s1[bb])
                                v.tensor_copy(cbuf[:, bb * 396:(bb + 1) * 396], s1ps(bb)[:, 0:396]).then_inc(s_v.h, 1)
                            for j, (k0, kn) in enumerate(OCTS):
                                nn = kn * 64
                                xre, xim = psX[2 * (j % 2)], psX[2 * (j % 2) + 1]
                                if fmode == 'F':
                                    v.wait_ge(s_t.h, t_s3[j])
                                    v.tensor_copy(pbuf[:, k0 * 64:k0 * 64 + nn], xre[:, 0:nn])
                                    v.tensor_copy(pbuf[:, 2112 + k0 * 64:2112 + k0 * 64 + nn], xim[:, 0:nn]).then_inc(s_v.h, 1)
                                else:
                                    if j == 0:
                                        v.wait_ge(sd[1].h, ld_f)
                                    v.wait_ge(s_a.h, a_ev[j])
                                    fre = ftile[:, k0 * 64:k0 * 64 + nn]
                                    fim = ftile[:, 2112 + k0 * 64:2112 + k0 * 64 + nn]
                                    fA = fmt[0][:, 0:nn]
                                    fB = fmt[1][:, 0:nn]
                                    t1 = fmt[2][:, 0:nn]
                                    t2 = fmt[3][:, 0:nn]
                                    pre_dst = pbuf[:, k0 * 64:k0 * 64 + nn]
                                    pim_dst = pbuf[:, 2112 + k0 * 64:2112 + k0 * 64 + nn]
                                    v.tensor_tensor(t1, fA, fre, OP.mult)
                                    v.tensor_tensor(t2, fB, fim, OP.mult)
                                    v.drain()
                                    v.tensor_tensor(pre_dst, t1, t2, OP.subtract)
                                    v.drain()
                                    v.tensor_tensor(t1, fA, fim, OP.mult)
                                    v.tensor_tensor(t2, fB, fre, OP.mult)
                                    v.drain()
                                    v.tensor_tensor(pim_dst, t1, t2, OP.add).then_inc(s_v.h, 1)
                            if fmode == 'P':
                                for j, (k0, kn) in enumerate(OCTS):
                                    qre, qim = (psQ[0], psQ[1]) if j % 2 == 0 else (psX[2], psX[3])
                                    v.wait_ge(s_t.h, t_i1[j])
                                    v.tensor_copy(
                                        qbuf[:, :].rearrange("p (d s) -> p d s", s=66)[:, :, 2 * k0:2 * (k0 + kn):2],
                                        qre[:, :].rearrange("p (k d) -> p k d", d=64)[:, 0:kn, :].transpose([0, 2, 1])).then_inc(s_v.h, 1)
                                for tb8 in range(8):
                                    v.wait_ge(s_t.h, t_T[tb8])
                                    v.tensor_copy(qtb[0:66, tb8 * 1024:(tb8 + 1) * 1024],
                                                  [psT, psT2][tb8 % 2][0:66, 0:1024]).then_inc(s_v.h, 1)
                                for gg in range(4):
                                    v.wait_ge(s_t.h, t_i2[gg])
                                    v.wait_ge(sd[2].h, ld_g)
                                    v.tensor_tensor(ynxt[:, gg * 512:(gg + 1) * 512],
                                                    [psI2, psI2b][gg % 2][:, :],
                                                    xpg[:, gg * 512:(gg + 1) * 512], OP.mult).then_inc(s_v.h, 1)

                        @b.scalar
                        def _(s):
                            for bb in range(1, 16, 2):
                                s.wait_ge(s_t.h, t_s1[bb])
                                s.activation(cbuf[:, bb * 396:(bb + 1) * 396], s1ps(bb)[:, 0:396], AF.Identity).then_inc(s_a.h, 1)
                            if fmode == 'P':
                                for j, (k0, kn) in enumerate(OCTS):
                                    s.wait_ge(s_t.h, t_s3[j])
                                    if j >= 1:
                                        s.wait_ge(s_v.h, v_fm[j - 1])
                                    xre, xim = psX[2 * (j % 2)], psX[2 * (j % 2) + 1]
                                    s.activation(fmt[0][:, 0:kn * 64], xre[:, 0:kn * 64], AF.Identity)
                                    s.activation(fmt[1][:, 0:kn * 64], xim[:, 0:kn * 64], AF.Identity).then_inc(s_a.h, 1)
                                for j in range(len(OCTS)):
                                    s.wait_ge(s_t.h, t_i1[j])
                                    qre, qim = (psQ[0], psQ[1]) if j % 2 == 0 else (psX[2], psX[3])
                                    k0, kn = OCTS[j]
                                    s.activation(
                                        qbuf[:, :].rearrange("p (d s) -> p d s", s=66)[:, :, 2 * k0 + 1:2 * (k0 + kn):2],
                                        qim[:, :].rearrange("p (k d) -> p k d", d=64)[:, 0:kn, :].transpose([0, 2, 1]),
                                        AF.Identity).then_inc(s_a.h, 1)

                if getattr(self, 'hy_upto', 3) < 1:
                    return
                for fg in range(min(6, getattr(self, 'hy_fblocks', 6))):
                    fwd_block(None, 0, 0, 'F', fg=fg)
                if getattr(self, 'hy_upto', 3) < 2:
                    return
                cc_t = s_cc.tick(1 if self.cores > 1 else 16)
                with nc.Block() as b:
                    if self.cores > 1 and not getattr(self, 'force_no_cc', False):
                        @b.gpsimd
                        def _(g):
                            g.collective_compute(
                                "AllGather", OP.bypass,
                                replica_groups=[list(range(self.cores))],
                                ins=[self.fsh_dram.ap().opt()],
                                outs=[self.fall_dram.ap().opt()],
                            ).then_inc(s_cc.h, 1)
                    else:
                        @b.gpsimd
                        def _(g):
                            g.dma_start(out=self.fall_dram.ap()[0:128, :],
                                        in_=self.fsh_dram.ap()[:, :]).then_inc(s_cc.h, 16)
                    @b.sync
                    def _(sync):
                        sync.wait_ge(s_cc.h, cc_t)

                if getattr(self, 'hy_upto', 3) < 3:
                    return
                for g in range(self.hy_groups):
                    for st in range(3):
                        fwd_block(None, g, st, 'P')
                    t_y3 = [s_t.tick() for _ in range(2)]
                    v_y3 = [s_v.tick() for _ in range(2)]
                    st_y3 = [sd[4].tick(16) for _ in range(2)]
                    with nc.Block() as b:
                        @b.tensor
                        def _(t):
                            for hh in range(2):
                                for d4 in range(8):
                                    dd4 = hh * 8 + d4
                                    mm = t.transpose([psT, psT2][hh][:, d4 * 128:(d4 + 1) * 128],
                                                     ytime[1][:, dd4 * 128:(dd4 + 1) * 128],
                                                     self.ident[:, :])
                                mm.then_inc(s_t.h, 1)
                        @b.vector
                        def _(v):
                            for hh in range(2):
                                v.wait_ge(s_t.h, t_y3[hh])
                                if hh == 1:
                                    v.wait_ge(sd[4].h, st_y3[0])
                                v.tensor_copy(
                                    y3st[:, :].rearrange("p (m d4 q) -> p d4 q m", d4=8, q=4),
                                    [psT, psT2][hh][:, 0:1024].rearrange("p (d4 q m) -> p d4 q m", d4=8, q=4)).then_inc(s_v.h, 1)
                        @b.sync
                        def _(sync):
                            for hh in range(2):
                                sync.wait_ge(s_v.h, v_y3[hh])
                                base = g * 64 + hh * 32
                                sync.dma_start(
                                    out=self.y3_dram.ap().rearrange("p (m dcol) -> p m dcol", dcol=1024)[:, :, base:base + 32],
                                    in_=y3st[:, :]).then_inc(sd[4].h, 16)
                            sync.wait_ge(sd[4].h, st_y3[1])
        self.hy_v = self.s_v.v

    def p_ffn(self):
        nc = self.nc
        xs3 = self.xstream.rearrange("p (c t) -> p c t", t=TS)

        def xsl(ck, t0, t1, shift):
            off = 0 if (shift and ck >= 4) else 1
            return xs3[:, ck, t0 + off:t1 + off]

        with ExitStack() as es:
            sb = lambda n, s, d=BF16: es.enter_context(nc.sbuf_tensor(n, s, d))

            class NS:
                pass
            B = NS()
            B.xsl = xsl
            B.selb = sb("ff_sel", [128, 128])
            B.selT = sb("ff_selT", [16, 128 * 8])
            B.selq = sb("ff_selq", [16, 512])
            B.qs_col = sb("ff_qscol", [128, 8], F32)
            B.fmask = sb("ff_fmask", [128, 256], F32)
            B.onecol = sb("ff_onecol", [128, 128])
            B.onesr = sb("ff_onesr", [16, 512])
            B.lnrowg = sb("ff_lnrowg", [2, 1024])
            B.lnrowb = sb("ff_lnrowb", [2, 1024])
            B.b2row = sb("ff_b2row", [128, 64], F32)
            B.msnap = sb("ff_msnap", [128, 4])
            with nc.named_scope("ffn_prolog"):
                self._ffn_prologue(B)
            for l in range(getattr(self, "ffn_layers", 2)):
                with nc.named_scope(f"qkv{l}"):
                    self._ffn_layer_qkv(l, B)
                    self.barrier()
                with nc.named_scope(f"attn{l}"):
                    self._ffn_layer_attn(l, B)
                with nc.named_scope(f"mlp{l}"):
                    self._ffn_layer_mlp(l, B)
                self.barrier()
        tap = self.dbg_tap("ffn_x", [128, NCH * TS], BF16)
        if tap is not None:
            td = self.s_ld.tick(16)
            with nc.Block() as b:
                @b.sync
                def _(sync):
                    sync.dma_start(out=tap[:, :], in_=self.xstream[:, :]).then_inc(self.s_ld.h, 16)
                    sync.wait_ge(self.s_ld.h, td)

    def _ffn_prologue(self, B):
        nc = self.nc
        s_t, s_v, s_a, s_p = self.s_t, self.s_v, self.s_a, self.s_p
        sd = self.s_dma
        with ExitStack() as es2:
            sb2 = lambda n, s, d=F32: es2.enter_context(nc.sbuf_tensor(n, s, d))
            ua = sb2("fp_ua", [128, T])
            ub = sb2("fp_ub", [128, T])
            ug = sb2("fp_ug", [128, T])
            xsb = sb2("fp_xs", [128, T])
            plb = sb2("fp_plb", [128, 4 * T], BF16)
            pst = es2.enter_context(nc.psum_tensor("fp_pst", [128, 128], BF16))
            scr = self.scr

            def wrap(v, fr, gg):
                v.tensor_copy(gg.bitcast(I32), fr)
                v.drain()
                v.tensor_copy(gg, gg.bitcast(I32))
                v.drain()
                v.tensor_tensor(fr, fr, gg, OP.subtract)
                v.drain()
                v.tensor_scalar(gg, fr, 0.5, None, OP.is_gt)
                v.drain()
                v.tensor_tensor(fr, fr, gg, OP.subtract)
                v.drain()
                v.tensor_scalar(gg, fr, -0.5, None, OP.is_lt)
                v.drain()
                v.tensor_tensor(fr, fr, gg, OP.add)
                v.drain()

            NV, NA, NP, NT_ = 5, 5, 2, 8
            tv = [s_v.tick() for _ in range(NV)]
            ta_ = [s_a.tick() for _ in range(NA)]
            tp = [s_p.tick() for _ in range(NP)]
            tt_ = [s_t.tick() for _ in range(NT_)]
            tv2 = [s_v.tick() for _ in range(NT_)]
            st_pl = sd[0].tick(16)
            with nc.Block() as b:
                @b.gpsimd
                def _(g):
                    g.iota(ua[:, :].bitcast(I32)[:, 0:T], [[1, T]], channel_multiplier=0).then_inc(s_p.h, 1)
                    g.wait_ge(s_v.h, tv[3])
                    g.iota(ua[:, :].bitcast(I32)[:, 0:256], [[0, 2], [1, 128]], channel_multiplier=0)
                    g.iota(ua[:, :].bitcast(I32)[:, 256:272], [[1, 16]], channel_multiplier=0)
                    g.iota(ua[:, :].bitcast(I32)[:, 928:1440], [[1, 8], [0, 64]], channel_multiplier=0).then_inc(s_p.h, 1)

                @b.vector
                def _(v):
                    ic = scr[:, :].bitcast(I32)
                    v.tensor_copy(ic[:, 30:31], self.iotap[:, 0:1])
                    v.drain()
                    v.tensor_scalar(ic[:, 31:32], ic[:, 30:31], 5, None, OP.arith_shift_right)
                    v.drain()
                    v.tensor_copy(scr[:, 19:20], ic[:, 31:32])
                    v.drain()
                    v.tensor_scalar(ic[:, 31:32], ic[:, 31:32], 5, None, OP.arith_shift_left)
                    v.drain()
                    v.tensor_tensor(ic[:, 31:32], ic[:, 30:31], ic[:, 31:32], OP.subtract)
                    v.drain()
                    v.tensor_copy(scr[:, 12:13], ic[:, 31:32])
                    v.drain()
                    v.tensor_scalar(scr[:, 13:14], scr[:, 12:13], -1.0 / 32, None, OP.mult)
                    v.drain()
                    v.tensor_scalar(scr[:, 16:17], scr[:, 12:13], 10.0, None, OP.mult)
                    v.drain()
                    v.tensor_scalar(scr[:, 16:17], scr[:, 16:17], 128.0, None, OP.add)
                    v.drain()
                    v.tensor_scalar(scr[:, 16:17], scr[:, 16:17], 1.0 / 448, None, OP.mult)
                    v.drain()
                    v.wait_ge(s_p.h, tp[0])
                    v.tensor_copy(ug[:, :], ua[:, :].bitcast(I32)[:, 0:T])
                    v.drain().then_inc(s_v.h, 1)
                    v.wait_ge(s_a.h, ta_[0])
                    v.tensor_tensor(scr[:, 15:16], scr[:, 14:15], self.r2pi[:, 0:1], OP.mult)
                    v.drain().then_inc(s_v.h, 1)
                    v.wait_ge(s_a.h, ta_[1])
                    v.tensor_scalar(ub[:, 0:T], ua[:, 0:T], 0.25, None, OP.add)
                    v.drain()
                    wrap(v, ub[:, 0:T], xsb[:, 0:T])
                    wrap(v, ua[:, 0:T], xsb[:, 0:T])
                    v.drain().then_inc(s_v.h, 1)
                    v.wait_ge(s_a.h, ta_[3])
                    v.tensor_tensor(plb[:, 0 * T:1 * T], ub[:, 0:T], xsb[:, 0:T], OP.mult)
                    v.drain()
                    v.tensor_tensor(plb[:, 1 * T:2 * T], ua[:, 0:T], xsb[:, 0:T], OP.mult)
                    v.drain()
                    v.reciprocal(xsb[:, 0:T], xsb[:, 0:T])
                    v.drain()
                    v.tensor_tensor(plb[:, 2 * T:3 * T], ub[:, 0:T], xsb[:, 0:T], OP.mult)
                    v.drain()
                    v.tensor_tensor(plb[:, 3 * T:4 * T], ua[:, 0:T], xsb[:, 0:T], OP.mult)
                    v.drain().then_inc(s_v.h, 1)
                    v.wait_ge(s_p.h, tp[1])
                    ia = ua[:, :].bitcast(I32)
                    v.tensor_copy(ub[:, 0:272], ia[:, 0:272])
                    v.drain()
                    v.tensor_tensor(ub[:, 512:640], ub[:, 0:128],
                                    self.iotap[:, 0:1].broadcast_to([128, 128]), OP.is_ge)
                    v.drain()
                    v.tensor_tensor(ub[:, 640:768], ub[:, 128:256],
                                    self.iotap[:, 0:1].broadcast_to([128, 128]), OP.is_le)
                    v.drain()
                    v.tensor_scalar(ub[:, 512:768], ub[:, 512:768], -1.0, None, OP.add)
                    v.drain()
                    v.tensor_scalar(ub[:, 512:768], ub[:, 512:768], 1e30, None, OP.mult)
                    v.drain()
                    v.tensor_copy(B.fmask[:, :], ub[:, 512:768])
                    v.drain()
                    for c in range(8):
                        base = [0.0, 0.0, 4.0, 4.0, 8.0, 8.0, 12.0, 12.0][c]
                        v.tensor_scalar(scr[:, 20:21], scr[:, 19:20], base, None, OP.add)
                        v.drain()
                        v.tensor_tensor(ub[:, 800 + c * 16:816 + c * 16], ub[:, 256:272],
                                        scr[:, 20:21].broadcast_to([128, 16]), OP.is_equal)
                        v.drain()
                    v.tensor_copy(B.selb[:, :], ub[:, 800:928])
                    v.tensor_copy(ub[:, 928:1440], ua[:, :].bitcast(I32)[:, 928:1440])
                    v.drain()
                    v.tensor_tensor(ub[:, 928:1440], ub[:, 928:1440],
                                    self.iotap[:, 0:1].broadcast_to([128, 512]), OP.is_equal)
                    v.drain()
                    v.tensor_copy(B.selq[0:16, :], ub[0:16, 928:1440])
                    v.drain()
                    v.memset(B.onecol[:, :], 0.0)
                    v.drain()
                    v.memset(B.onecol[:, 0:1], 1.0)
                    v.memset(B.onecol[:, 96:97], 1.0)
                    v.memset(B.onesr[:, :], 0.0)
                    v.drain()
                    v.memset(B.onesr[0:1, :], 1.0)
                    v.drain().then_inc(s_v.h, 1)
                    for c in range(8):
                        v.wait_ge(s_t.h, tt_[c])
                        v.tensor_copy(B.selT[0:16, c * 128:(c + 1) * 128], pst[0:16, 0:128])
                        v.drain().then_inc(s_v.h, 1)

                @b.scalar
                def _(s):
                    s.wait_ge(s_v.h, tv[0])
                    s.activation(scr[:, 14:15], scr[:, 13:14], AF.Exp, scale=self.iotap[:, 3:4])
                    s.drain()
                    s.activation(scr[:, 21:22], scr[:, 16:17], AF.Ln)
                    s.drain().then_inc(s_a.h, 1)
                    s.wait_ge(s_v.h, tv[1])
                    s.activation(ua[:, 0:T], ug[:, 0:T], AF.Identity, scale=scr[:, 15:16])
                    s.drain().then_inc(s_a.h, 1)
                    s.wait_ge(s_v.h, tv[2])
                    s.activation(ub[:, 0:T], ub[:, 0:T], AF.Sin, scale=self.pi2[:, 1:2])
                    s.drain()
                    s.activation(ua[:, 0:T], ua[:, 0:T], AF.Sin, scale=self.pi2[:, 1:2])
                    s.drain().then_inc(s_a.h, 1)
                    s.activation(xsb[:, 0:T], ug[:, 0:T], AF.Identity, bias=-8.0, scale=0.00390625)
                    s.drain()
                    s.activation(xsb[:, 0:T], xsb[:, 0:T], AF.Exp, scale=scr[:, 21:22])
                    s.drain().then_inc(s_a.h, 1)
                    s.drain().then_inc(s_a.h, 1)

                @b.tensor
                def _(t):
                    t.wait_ge(s_v.h, tv[4])
                    for c in range(8):
                        if c >= 1:
                            t.wait_ge(s_v.h, tv2[c - 1])
                        t.transpose(pst[0:16, 0:128], B.selb[:, c * 16:(c + 1) * 16],
                                    self.ident[:, :]).then_inc(s_t.h, 1)

                @b.sync
                def _(sync):
                    sync.wait_ge(s_v.h, tv[3])
                    sync.dma_start(out=self.pl_dram.ap()[:, :], in_=plb[:, :]).then_inc(sd[0].h, 16)
                    sync.wait_ge(sd[0].h, st_pl)

    def _ffn_layer_qkv(self, l, B):
        nc = self.nc
        s_t, s_v, s_a, s_p = self.s_t, self.s_v, self.s_a, self.s_p
        sd = self.s_dma
        with ExitStack() as es:
            sb = lambda n, s, d=BF16: es.enter_context(nc.sbuf_tensor(f"l{l}" + n, s, d))
            ps = lambda n, s, d=F32: es.enter_context(nc.psum_tensor(f"l{l}" + n, s, d))
            wqkvb = sb("fq_wqkv", [128, 8 * 1536])
            stg = sb("fq_stg", [128, 1536], F32)
            bstg = sb("fq_bstg", [52, 128], F32)
            bstgb = sb("fq_bstgb", [52, 128])
            plg = sb("fq_plg", [128, 4096])
            qg = sb("fq_qg", [128, 4096])
            kg = sb("fq_kg", [128, 4096])
            vg = sb("fq_vg", [128, 4096])
            sqst = sb("fq_sq", [128, 2048])
            rsd = sb("fq_rsd", [16, 2048])
            rpl = sb("fq_rpl", [128, 1024])
            tmA = sb("fq_tA", [128, 1024])
            tmB = sb("fq_tB", [128, 1024])
            ps_a = ps("fq_psa", [128, 512])
            ps_b = ps("fq_psb", [128, 512])
            ps_v = ps("fq_psv", [128, 512])
            ps_pl = ps("fq_pspl", [128, 512])
            pbt = es.enter_context(nc.psum_tensor(f"l{l}fq_pbt", [128, 52], BF16))

            w = self.din["wqkv"][l]
            for c in range(8):
                ld = sd[0].tick(16 * 9)
                cst = s_v.tick()
                with nc.Block() as b:
                    @b.sync
                    def _(sync):
                        rows = w[c * 128:(c + 1) * 128, :]
                        for part in range(2):
                            src4 = rows[:, part * 512:(part + 1) * 512].rearrange(
                                "p (g hh e j) -> p g hh e j", g=2, hh=4, e=2)
                            for g2 in range(2):
                                for e in range(2):
                                    o0 = part * 512 + (g2 * 2 + e) * 128
                                    dst = stg[:, o0:o0 + 128].rearrange("p (hh j) -> p hh j", hh=4)
                                    sync.dma_start(out=dst, in_=src4[:, g2, :, e, :]).then_inc(sd[0].h, 16)
                        sync.dma_start(out=stg[:, 1024:1536], in_=rows[:, 1024:1536]).then_inc(sd[0].h, 16)

                    @b.vector
                    def _(v):
                        v.wait_ge(sd[0].h, ld)
                        v.tensor_copy(wqkvb[:, c * 1536:(c + 1) * 1536], stg[:, :])
                        v.drain().then_inc(s_v.h, 1)

            vz = s_v.tick()
            with nc.Block() as b:
                @b.vector
                def _(v):
                    v.memset(bstg[32:52, :], 0.0)
                    v.drain().then_inc(s_v.h, 1)
            ldq = sd[0].tick(16 * 16)
            ldl = sd[1].tick(16 * 2)
            ldb = sd[2].tick(16 * 4)
            tT = s_t.tick()
            tvb = s_v.tick()
            tvc = s_v.tick()
            with nc.Block() as b:
                @b.sync
                def _(sync):
                    qs = self.din["q_scale"][l]
                    ks = self.din["k_scale"][l]
                    for i, srcv in enumerate([qs[0:32], qs[32:64], ks[0:32], ks[32:64]]):
                        for r in range(4):
                            sync.dma_start(out=B.qs_col[32 * r:32 * r + 32, i:i + 1],
                                           in_=srcv.rearrange("(j o) -> j o", o=1)).then_inc(sd[0].h, 16)
                    sync.dma_start(out=stg[0:1, 0:1024], in_=self.din["ln_g"][l].rearrange("(o d) -> o d", o=1)).then_inc(sd[1].h, 16)
                    sync.dma_start(out=stg[32:33, 0:1024], in_=self.din["ln_b"][l].rearrange("(o d) -> o d", o=1)).then_inc(sd[1].h, 16)
                    b1 = self.din["b1"][l]
                    sync.dma_start(out=bstg[0:22, :], in_=b1[0:2816].rearrange("(a c) -> a c", c=128)).then_inc(sd[2].h, 16)
                    sync.dma_start(out=bstg[22:43, :], in_=b1[2730:5418].rearrange("(a c) -> a c", c=128)).then_inc(sd[2].h, 16)
                    sync.dma_start(out=bstg[43:44, 0:42], in_=b1[5418:5460].rearrange("(o a) -> o a", o=1)).then_inc(sd[2].h, 16)
                    sync.dma_start(out=bstg[44:52, :], in_=self.din["b2"][l].rearrange("(a c) -> a c", c=128)).then_inc(sd[2].h, 16)

                @b.vector
                def _(v):
                    v.wait_ge(sd[1].h, ldl)
                    v.tensor_copy(B.lnrowg[0:1, :], stg[0:1, 0:1024])
                    v.drain()
                    v.tensor_copy(B.lnrowb[0:1, :], stg[32:33, 0:1024])
                    v.drain()
                    v.wait_ge(sd[2].h, ldb)
                    v.tensor_copy(bstgb[:, :], bstg[:, :])
                    v.drain().then_inc(s_v.h, 1)
                    v.wait_ge(s_t.h, tT)
                    v.tensor_copy(B.b2row[:, 0:52], pbt[:, 0:52])
                    v.drain().then_inc(s_v.h, 1)

                @b.tensor
                def _(t):
                    t.wait_ge(s_v.h, tvb)
                    t.transpose(pbt[:, 0:52], bstgb[:, :], self.ident[0:52, 0:52]).then_inc(s_t.h, 1)

            prev_st = {"q": 0, "k": 0, "v": 0}
            for g in range(4):
                t0 = g * 1024
                ld_pl = sd[2].tick(16)
                t_q, t_k, t_v = [], [], []
                t_s = []
                t_pl = []
                for _ in range(4):
                    t_q.append(s_t.tick())
                t_s.append(s_t.tick())
                for _ in range(4):
                    t_q.append(s_t.tick())
                t_s.append(s_t.tick())
                for _ in range(4):
                    t_k.append(s_t.tick())
                t_s.append(s_t.tick())
                for _ in range(4):
                    t_k.append(s_t.tick())
                t_s.append(s_t.tick())
                for _ in range(8):
                    t_v.append(s_t.tick())
                for _ in range(8):
                    t_pl.append(s_t.tick())
                a_u = []
                a_ln = []
                for ph in range(4):
                    for _ in range(4):
                        a_u.append(s_a.tick())
                    a_ln.append(s_a.tick())
                v_cl = [s_v.tick() for _ in range(4)]
                v_vv = [s_v.tick() for _ in range(8)]
                v_pl = []
                v_rot = []
                for pr4 in range(4):
                    v_pl.append(s_v.tick())
                    v_pl.append(s_v.tick())
                    v_rot.append(s_v.tick())
                st_v = sd[3].tick(16)
                st_q = [sd[6].tick(16) for _ in range(2)]
                st_k = [sd[7].tick(16) for _ in range(2)]

                with nc.Block() as b:
                    @b.tensor
                    def _(t):
                        for ph in range(4):
                            kind = "q" if ph < 2 else "k"
                            tt = ph % 2
                            base = 0 if kind == "q" else 512
                            tks = t_q if kind == "q" else t_k
                            for c in range(4):
                                i = (ph % 2) * 4 + c if kind == "q" else (ph - 2) * 4 + c
                                ui = ph * 4 + c
                                pp = ps_a if ui % 2 == 0 else ps_b
                                if ui >= 2:
                                    t.wait_ge(s_a.h, a_u[ui - 2])
                                for ck in range(8):
                                    mm = t.matmul(pp[:, :],
                                                  wqkvb[:, ck * 1536 + base + c * 128:ck * 1536 + base + c * 128 + 128],
                                                  B.xsl(ck, t0 + tt * 512, t0 + (tt + 1) * 512, True),
                                                  start=(ck == 0), stop=(ck == 7))
                                mm.then_inc(s_t.h, 1)
                            t.wait_ge(s_a.h, a_u[ph * 4 + 3])
                            if ph >= 1:
                                t.wait_ge(s_v.h, v_cl[ph - 1])
                            for c in range(4):
                                mm = t.matmul(ps_pl[0:16, 0:512],
                                              B.selb[:, c * 16:(c + 1) * 16],
                                              sqst[:, c * 512:(c + 1) * 512],
                                              start=(c == 0), stop=(c == 3))
                            mm.then_inc(s_t.h, 1)
                        for vt in range(8):
                            pp = ps_v if vt % 2 == 0 else ps_a
                            if vt == 1:
                                t.wait_ge(s_a.h, a_u[14])
                            if vt >= 2:
                                t.wait_ge(s_v.h, v_vv[vt - 2])
                            for ck in range(8):
                                mm = t.matmul(pp[:, 0:512],
                                              B.xsl(ck, t0 + vt * 128, t0 + (vt + 1) * 128, True),
                                              wqkvb[:, ck * 1536 + 1024:ck * 1536 + 1536],
                                              start=(ck == 0), stop=(ck == 7))
                            mm.then_inc(s_t.h, 1)
                        for idx in range(8):
                            kind = idx // 4
                            pr = (idx % 4) // 2
                            tt = idx % 2
                            if idx == 0:
                                t.wait_ge(s_v.h, v_cl[3])
                            else:
                                t.wait_ge(s_v.h, v_pl[idx - 1])
                            t.wait_ge(s_a.h, a_ln[kind * 2 + tt])
                            mm = t.matmul(ps_pl[:, 0:512],
                                          B.selT[0:16, (2 * pr) * 128:(2 * pr + 1) * 128],
                                          rsd[0:16, kind * 1024 + tt * 512:kind * 1024 + (tt + 1) * 512],
                                          start=True, stop=True)
                            mm.then_inc(s_t.h, 1)

                    @b.scalar
                    def _(s):
                        for ph in range(4):
                            kind = "q" if ph < 2 else "k"
                            tt = ph % 2
                            dstb = qg if kind == "q" else kg
                            tks = t_q if kind == "q" else t_k
                            for c in range(4):
                                ui = ph * 4 + c
                                pp = ps_a if ui % 2 == 0 else ps_b
                                if c == 0 and ph >= 1:
                                    s.wait_ge(s_t.h, t_s[ph - 1])
                                if c == 0 and tt == 0 and g >= 1:
                                    s.wait_ge(sd[6 if kind == "q" else 7].h, prev_st[kind])
                                if c == 0 and ph == 0 and g == 0:
                                    s.wait_ge(sd[0].h, ldq)
                                s.wait_ge(s_t.h, tks[tt * 4 + c])
                                s.activation(sqst[:, c * 512:(c + 1) * 512], pp[:, :], AF.Square)
                                scol = B.qs_col[:, (0 if kind == "q" else 2) + c % 2:
                                                (0 if kind == "q" else 2) + c % 2 + 1]
                                s.activation(dstb[:, c * 1024 + tt * 512:c * 1024 + (tt + 1) * 512],
                                             pp[:, :], AF.Identity, scale=scol).then_inc(s_a.h, 1)
                            s.wait_ge(s_v.h, v_cl[ph])
                            kk0 = (0 if kind == "q" else 1024) + tt * 512
                            s.activation(rsd[0:16, kk0:kk0 + 512], rsd[0:16, kk0:kk0 + 512], AF.Ln)
                            s.drain()
                            s.activation(rsd[0:16, kk0:kk0 + 512], rsd[0:16, kk0:kk0 + 512], AF.Exp, scale=-0.5).then_inc(s_a.h, 1)

                    @b.vector
                    def _(v):
                        for ph in range(4):
                            kind = 0 if ph < 2 else 1
                            tt = ph % 2
                            v.wait_ge(s_t.h, t_s[ph])
                            kk0 = kind * 1024 + tt * 512
                            v.tensor_scalar(rsd[0:16, kk0:kk0 + 512], ps_pl[0:16, 0:512],
                                            1e-24, None, OP.max).then_inc(s_v.h, 1)
                        for vt in range(8):
                            pp = ps_v if vt % 2 == 0 else ps_a
                            if vt == 0 and g >= 1:
                                v.wait_ge(sd[3].h, prev_st["v"])

                            v.wait_ge(s_t.h, t_v[vt])
                            v.tensor_copy(vg[:, vt * 512:(vt + 1) * 512], pp[:, 0:512]).then_inc(s_v.h, 1)
                        v.wait_ge(sd[2].h, ld_pl)
                        for pr4 in range(4):
                            kind = pr4 // 2
                            pr = pr4 % 2
                            buf = qg if kind == 0 else kg
                            for tt in range(2):
                                v.wait_ge(s_t.h, t_pl[pr4 * 2 + tt])
                                v.drain()
                                v.tensor_copy(rpl[:, tt * 512:(tt + 1) * 512], ps_pl[:, 0:512]).then_inc(s_v.h, 1)
                            lo = buf[:, (2 * pr) * 1024:(2 * pr) * 1024 + 1024]
                            hi = buf[:, (2 * pr + 1) * 1024:(2 * pr + 1) * 1024 + 1024]
                            plc = plg[:, (0 if kind == 0 else 2) * 1024:(0 if kind == 0 else 2) * 1024 + 1024]
                            pls = plg[:, (1 if kind == 0 else 3) * 1024:(1 if kind == 0 else 3) * 1024 + 1024]
                            v.tensor_tensor(tmA[:, :], hi, pls, OP.mult)
                            v.drain()
                            v.tensor_tensor(tmB[:, :], hi, plc, OP.mult)
                            v.drain()
                            v.tensor_tensor(hi, lo, pls, OP.mult)
                            v.drain()
                            v.tensor_tensor(hi, hi, tmB[:, :], OP.add)
                            v.drain()
                            v.tensor_tensor(lo, lo, plc, OP.mult)
                            v.drain()
                            v.tensor_tensor(lo, lo, tmA[:, :], OP.subtract)
                            v.drain()
                            v.tensor_tensor(lo, lo, rpl[:, :], OP.mult)
                            v.tensor_tensor(hi, hi, rpl[:, :], OP.mult).then_inc(s_v.h, 1)

                    @b.sync
                    def _(sync):
                        sync.dma_start(out=plg[:, :].rearrange("p (pi t) -> p pi t", t=1024),
                                       in_=self.pl_dram.ap().rearrange("p (pi t) -> p pi t", t=T)[:, :, t0:t0 + 1024]).then_inc(sd[2].h, 16)
                        sync.wait_ge(s_v.h, v_vv[7])
                        sync.dma_start(out=self.v_dram.ap()[t0:t0 + 1024, :].rearrange("(vt p) i -> p vt i", p=128),
                                       in_=vg[:, :].rearrange("p (vt i) -> p vt i", i=512)).then_inc(sd[3].h, 16)
                        for pr4 in range(4):
                            kind = pr4 // 2
                            pr = pr4 % 2
                            buf = qg if kind == 0 else kg
                            dr = self.q_dram if kind == 0 else self.k_dram
                            sync.wait_ge(s_v.h, v_rot[pr4])
                            sync.dma_start(
                                out=dr.ap().rearrange("(c p) t -> p c t", p=128)[:, 2 * pr:2 * pr + 2, t0:t0 + 1024],
                                in_=buf[:, (2 * pr) * 1024:(2 * pr + 2) * 1024].rearrange("p (c t) -> p c t", t=1024)).then_inc(sd[6 if kind == 0 else 7].h, 16)

                prev_st = {"q": st_q[1], "k": st_k[1], "v": st_v}
            with ExitStack() as esc:
                cwst = [esc.enter_context(nc.sbuf_tensor(f"l{l}cw{i}", [128, 2048], F32)) for i in range(2)]
                cbf = [esc.enter_context(nc.sbuf_tensor(f"l{l}cb{i}", [128, 2048], BF16)) for i in range(2)]
                w1s = self.din["w1"][l]
                w2s = self.din["w2"][l]
                pieces = [(0, 2048), (2048, 2048), (4096, 1364)]
                jobs = []
                for c in range(8):
                    for off, wdt in pieces:
                        jobs.append(("w1", c, off, wdt))
                for kk in range(22):
                    jobs.append(("w2", kk, 0, 1024))
                ldsem = [self.s_st, sd[2]]
                stsem = [self.s_ld, sd[0]]
                ldc = []
                stc = []
                vcst = []
                for i in range(len(jobs)):
                    ldc.append(ldsem[i % 2].tick(16))
                    stc.append(stsem[i % 2].tick(16))
                    vcst.append(s_v.tick())
                with nc.Block() as b:
                    @b.sync
                    def _(sync):
                        for i, (kind, c, off, wdt) in enumerate(jobs):
                            if i >= 2:
                                sync.wait_ge(s_v.h, vcst[i - 2])
                            if kind == "w1":
                                sync.dma_start(out=cwst[i % 2][:, 0:wdt],
                                               in_=w1s[c * 128:(c + 1) * 128, off:off + wdt]).then_inc(ldsem[i % 2].h, 16)
                            else:
                                wj = min(128, HID - c * 128)
                                sync.dma_start(out=cwst[i % 2][0:wj, 0:1024],
                                               in_=w2s[c * 128:c * 128 + wj, :]).then_inc(ldsem[i % 2].h, 16)
                            if i >= 1:
                                km, cm, om, wm = jobs[i - 1]
                                sync.wait_ge(s_v.h, vcst[i - 1])
                                if km == "w1":
                                    sync.dma_start(out=self.w1bf.ap()[:, cm * 5460 + om:cm * 5460 + om + wm],
                                                   in_=cbf[(i - 1) % 2][:, 0:wm]).then_inc(stsem[(i - 1) % 2].h, 16)
                                else:
                                    wj = min(128, HID - cm * 128)
                                    sync.dma_start(out=self.w2bf.ap()[cm * 128:cm * 128 + wj, :],
                                                   in_=cbf[(i - 1) % 2][0:wj, 0:1024]).then_inc(stsem[(i - 1) % 2].h, 16)
                        km, cm, om, wm = jobs[-1]
                        sync.wait_ge(s_v.h, vcst[-1])
                        sync.dma_start(out=self.w2bf.ap()[cm * 128:cm * 128 + min(128, HID - cm * 128), :],
                                       in_=cbf[(len(jobs) - 1) % 2][0:min(128, HID - cm * 128), 0:1024]).then_inc(stsem[(len(jobs) - 1) % 2].h, 16)
                        le = max(i for i in range(len(jobs)) if i % 2 == 0)
                        lo = max(i for i in range(len(jobs)) if i % 2 == 1)
                        sync.wait_ge(stsem[0].h, stc[le])
                        sync.wait_ge(stsem[1].h, stc[lo])

                    @b.vector
                    def _(v):
                        for i, (kind, c, off, wdt) in enumerate(jobs):
                            v.wait_ge(ldsem[i % 2].h, ldc[i])
                            if i >= 2:
                                v.wait_ge(stsem[i % 2].h, stc[i - 2])
                            if kind == "w1":
                                v.tensor_copy(cbf[i % 2][:, 0:wdt], cwst[i % 2][:, 0:wdt])
                            else:
                                wj = min(128, HID - c * 128)
                                v.tensor_copy(cbf[i % 2][0:wj, 0:1024], cwst[i % 2][0:wj, 0:1024])
                            v.drain().then_inc(s_v.h, 1)
            fin3, fin6, fin7 = sd[3].v, sd[6].v, sd[7].v
            with nc.Block() as b:
                @b.sync
                def _(sync):
                    sync.wait_ge(sd[3].h, fin3)
                    sync.wait_ge(sd[6].h, fin6)
                    sync.wait_ge(sd[7].h, fin7)

    def _ffn_layer_attn(self, l, B):
        nc = self.nc
        s_t, s_v, s_a, s_p = self.s_t, self.s_v, self.s_a, self.s_p
        sd = self.s_dma
        es_l = ExitStack()
        sb = lambda n, s, d=BF16: es_l.enter_context(nc.sbuf_tensor(f"l{l}" + n, s, d))
        ps = lambda n, s, d=F32: es_l.enter_context(nc.psum_tensor(f"l{l}" + n, s, d))
        B.wob = sb("ff_wo", [128, 4 * 1024])
        B.qbuf = [sb(f"ff_q{i}", [128, 4 * 1024]) for i in range(2)]
        B.kbuf = [sb(f"ff_k{i}", [128, 4 * 1536]) for i in range(2)]
        B.vbuf = [sb(f"ff_v{i}", [128, 12 * 512]) for i in range(2)]
        B.pbufA = sb("ff_pA", [128, 640])
        B.ptA = sb("ff_ptA", [128, 640])
        B.obuf = sb("ff_obuf", [128, 4 * 1024])
        B.oplane = sb("ff_oplane", [128, 512])
        B.nmax = sb("ff_nmax", [128, 16], F32)
        B.rs8 = sb("ff_rs8", [128, 8], F32)
        B.scbuf = sb("ff_scbuf", [128, 640], F32)
        B.rcq = sb("ff_rcq", [128, 8])
        B.rcT = sb("ff_rcT", [16, 128])
        stgo = sb("ff_stgo", [128, 1024], F32)
        B.ps_sb = [ps("ff_pss0", [128, 512]), ps("ff_pss1", [128, 512])]
        B.ps_ss = [ps("ff_pst0", [128, 128]), ps("ff_pst1", [128, 128])]
        B.ps_T = ps("ff_psT", [128, 1024], BF16)
        B.ps_o = ps("ff_pso", [128, 512])
        B.ps_pl = ps("ff_pspl", [128, 512])
        B.ps_a = B.ps_sb[0]
        B.ps_b = B.ps_sb[1]

        wo = self.din["wo"][l]
        for c in range(4):
            ld = sd[0].tick(16)
            cst = s_v.tick()
            with nc.Block() as b:
                @b.sync
                def _(sync):
                    sync.dma_start(out=stgo[:, 0:1024], in_=wo[c * 128:(c + 1) * 128, :]).then_inc(sd[0].h, 16)

                @b.vector
                def _(v):
                    v.wait_ge(sd[0].h, ld)
                    v.tensor_copy(B.wob[:, c * 1024:(c + 1) * 1024], stgo[:, 0:1024])
                    v.drain().then_inc(s_v.h, 1)

        def emit_band_load(bb):
            t0 = bb * 1024
            kw0 = max(0, t0 - 512)
            kwN = t0 + 1024 - kw0
            slot = 4 + bb % 2
            tk = sd[slot].tick(16 * 3)
            with nc.Block() as b:
                @b.sync
                def _(sync):
                    sync.dma_start(out=B.qbuf[bb % 2][:, :].rearrange("p (c t) -> p c t", t=1024),
                                   in_=self.q_dram.ap().rearrange("(c p) t -> p c t", p=128)[:, :, t0:t0 + 1024]).then_inc(sd[slot].h, 16)
                    sync.dma_start(out=B.kbuf[bb % 2][:, :].rearrange("p (c t) -> p c t", t=1536)[:, :, 0:kwN],
                                   in_=self.k_dram.ap().rearrange("(c p) t -> p c t", p=128)[:, :, kw0:kw0 + kwN]).then_inc(sd[slot].h, 16)
                    sync.dma_start(out=B.vbuf[bb % 2][:, 0:kwN * 4].rearrange("p (vt i) -> p vt i", i=512),
                                   in_=self.v_dram.ap()[kw0:kw0 + kwN, :].rearrange("(vt p) i -> p vt i", p=128)).then_inc(sd[slot].h, 16)
            return tk

        ldt = [emit_band_load(0), emit_band_load(1)]
        for bb in range(4):
            t0 = bb * 1024
            kw0 = max(0, t0 - 512)
            kwN = t0 + 1024 - kw0
            B.qr = B.qbuf[bb % 2]
            B.kr = B.kbuf[bb % 2]
            B.vtok = B.vbuf[bb % 2]
            B.ld_wait = (4 + bb % 2, ldt[bb % 2])
            for qg_ in range(2):
                for pp4 in range(4):
                    self._attn_inner(l, B, bb, t0, kw0, qg_, pp4)
            t_w = [s_t.tick() for _ in range(16)]
            v_w = [s_v.tick() for _ in range(16)]
            with nc.Block() as b:
                @b.tensor
                def _(t):
                    u = 0
                    for dt_ in range(8):
                        for tt in range(2):
                            pp = [B.ps_a, B.ps_b, B.ps_pl][u % 3]
                            if u >= 3:
                                t.wait_ge(s_v.h, v_w[u - 3])
                            for oc in range(4):
                                mm = t.matmul(pp[:, :], B.wob[:, oc * 1024 + dt_ * 128:oc * 1024 + (dt_ + 1) * 128],
                                              B.obuf[:, oc * 1024 + tt * 512:oc * 1024 + (tt + 1) * 512],
                                              start=(oc == 0), stop=(oc == 3))
                            mm.then_inc(s_t.h, 1)
                            u += 1

                @b.vector
                def _(v):
                    xs3 = self.xstream.rearrange("p (c t) -> p c t", t=TS)
                    u = 0
                    for dt_ in range(8):
                        for tt in range(2):
                            pp = [B.ps_a, B.ps_b, B.ps_pl][u % 3]
                            v.wait_ge(s_t.h, t_w[u])
                            dstx = xs3[:, dt_, 1 + t0 + tt * 512:1 + t0 + (tt + 1) * 512]
                            v.tensor_tensor(dstx, dstx, pp[:, :], OP.add).then_inc(s_v.h, 1)
                            u += 1
            if bb + 2 < 4:
                ldt[bb % 2] = emit_band_load(bb + 2)
        self.barrier()
        es_l.close()

    def _attn_inner(self, l, B, bb, t0, kw0, qg, pp4):
        nc = self.nc
        s_t, s_v, s_a, s_p = self.s_t, self.s_v, self.s_a, self.s_p
        units = [(qb, hh) for qb in range(4) for hh in range(2)]
        t_sc = {}
        t_pt = {}
        t_pv = {}
        for i, u in enumerate(units):
            t_sc[u] = s_t.tick()
            if i >= 1:
                t_pt[units[i - 1]] = s_t.tick()
                t_pv[units[i - 1]] = s_t.tick()
        t_pt[units[-1]] = s_t.tick()
        t_pv[units[-1]] = s_t.tick()
        t_rcT = s_t.tick()
        t_plo = s_t.tick()
        v_sc = {}
        v_mx = {}
        v_pt = {}
        for u in units:
            v_sc[u] = s_v.tick()
            v_mx[u] = s_v.tick()
            v_pt[u] = s_v.tick()
        v_rc8 = s_v.tick()
        v_rcT = s_v.tick()
        v_oev = s_v.tick()
        a_cp = {}
        a_ex = {}
        for u in units:
            a_cp[u] = s_a.tick()
            a_ex[u] = s_a.tick()

        def kbs(qb):
            qbg = (t0 // 128) + qg * 4 + qb
            lo = max(0, qbg - 4)
            return [kb for kb in range(lo, qbg + 1)], qbg

        def sc_dst(u_i, ki):
            if ki < 4:
                return B.ps_sb[u_i % 2][:, ki * 128:(ki + 1) * 128]
            return B.ps_ss[u_i % 2][:, 0:128]

        with nc.Block() as b:
            @b.tensor
            def _(t):
                if qg == 0 and pp4 == 0:
                    t.wait_ge(self.s_dma[B.ld_wait[0]].h, B.ld_wait[1])

                def emit_scores(u):
                    qb, hh = u
                    h = 2 * pp4 + hh
                    ch = (h // 4) * 2
                    st_ = (h % 4) * 32
                    kl, qbg = kbs(qb)
                    qcol = qg * 512 + qb * 128
                    u_i = qb * 2 + hh
                    if u not in (units[0], units[1]):
                        prev = units[units.index(u) - 2]
                        t.wait_ge(s_v.h, v_sc[prev])
                        t.wait_ge(s_a.h, a_cp[prev])
                    nkb = len(kl)
                    nb4 = min(nkb, 4)
                    kcol0 = kl[0] * 128 - kw0
                    qlo = B.qr[st_:st_ + 32, ch * 1024 + qcol:ch * 1024 + qcol + 128]
                    qhi = B.qr[st_:st_ + 32, (ch + 1) * 1024 + qcol:(ch + 1) * 1024 + qcol + 128]
                    t.matmul(B.ps_sb[u_i % 2][:, 0:nb4 * 128], qlo,
                             B.kr[st_:st_ + 32, ch * 1536 + kcol0:ch * 1536 + kcol0 + nb4 * 128],
                             start=True, stop=False, tile_position=(st_, 0))
                    mm = t.matmul(B.ps_sb[u_i % 2][:, 0:nb4 * 128], qhi,
                                  B.kr[st_:st_ + 32, (ch + 1) * 1536 + kcol0:(ch + 1) * 1536 + kcol0 + nb4 * 128],
                                  start=False, stop=True, tile_position=(st_, 0))
                    if nkb == 5:
                        t.matmul(B.ps_ss[u_i % 2][:, 0:128], qlo,
                                 B.kr[st_:st_ + 32, ch * 1536 + kcol0 + 512:ch * 1536 + kcol0 + 640],
                                 start=True, stop=False, tile_position=(st_, 0))
                        mm = t.matmul(B.ps_ss[u_i % 2][:, 0:128], qhi,
                                      B.kr[st_:st_ + 32, (ch + 1) * 1536 + kcol0 + 512:(ch + 1) * 1536 + kcol0 + 640],
                                      start=False, stop=True, tile_position=(st_, 0))
                    mm.then_inc(s_t.h, 1)

                def emit_tpv(u):
                    qb, hh = u
                    h = 2 * pp4 + hh
                    kl, qbg = kbs(qb)
                    nkb = len(kl)
                    t.wait_ge(s_a.h, a_ex[u])
                    for ki in range(nkb):
                        mm = t.transpose(B.ps_T[0:128, ki * 128:(ki + 1) * 128],
                                         B.pbufA[:, ki * 128:(ki + 1) * 128], self.ident[:, :])
                    mm.then_inc(s_t.h, 1)
                    t.wait_ge(s_v.h, v_pt[u])
                    for ki, kb in enumerate(kl):
                        vloc = kb * 128 - kw0
                        mm = t.matmul(B.ps_o[hh * 64:hh * 64 + 64, qb * 128:(qb + 1) * 128],
                                      B.vtok[:, (vloc // 128) * 512 + h * 64:(vloc // 128) * 512 + (h + 1) * 64],
                                      B.ptA[:, ki * 128:(ki + 1) * 128],
                                      start=(ki == 0), stop=(ki == nkb - 1),
                                      tile_position=(0, hh * 64))
                    mm.then_inc(s_t.h, 1)

                for i, u in enumerate(units):
                    emit_scores(u)
                    if i >= 1:
                        emit_tpv(units[i - 1])
                emit_tpv(units[-1])
                t.wait_ge(s_v.h, v_rc8)
                t.transpose(B.ps_T[0:8, 896:1024],
                            B.rcq[:, 0:8], self.ident[:, :]).then_inc(s_t.h, 1)
                t.wait_ge(s_v.h, v_rcT)
                for qb in range(4):
                    mm = t.matmul(B.ps_pl[:, qb * 128:(qb + 1) * 128], B.selq[0:8, qb * 128:(qb + 1) * 128],
                                  B.rcT[0:8, 0:128], start=True, stop=True)
                mm.then_inc(s_t.h, 1)

            @b.vector
            def _(v):
                for (qb, hh) in units:
                    kl, qbg = kbs(qb)
                    nkb = len(kl)
                    u_i = qb * 2 + hh
                    v.wait_ge(s_t.h, t_sc[(qb, hh)])
                    ops = []
                    for ki in range(nkb):
                        src = sc_dst(u_i, ki)
                        dstc = B.scbuf[:, ki * 128:(ki + 1) * 128]
                        if ki == nkb - 1:
                            ops.append((dstc, src, B.fmask[:, 128:256]))
                        elif ki == 0 and qbg >= 4:
                            ops.append((dstc, src, B.fmask[:, 0:128]))
                    for d_, s_, m_ in ops[:-1]:
                        v.tensor_tensor(d_, s_, m_, OP.add)
                    v.tensor_tensor(*ops[-1], OP.add).then_inc(s_v.h, 1)
                    v.wait_ge(s_a.h, a_cp[(qb, hh)])
                    v.drain()
                    v.reduce_max(B.nmax[:, 0:1], B.scbuf[:, 0:nkb * 128], axis=AX)
                    v.drain()
                    v.tensor_scalar(B.nmax[:, 1:2], B.nmax[:, 0:1], -8.0, None, OP.mult).then_inc(s_v.h, 1)
                    v.wait_ge(s_t.h, t_pt[(qb, hh)])
                    v.tensor_copy(B.ptA[:, 0:nkb * 128], B.ps_T[0:128, 0:nkb * 128]).then_inc(s_v.h, 1)
                v.wait_ge(s_a.h, a_ex[units[-1]])
                with nc.allow_low_precision(reason="1/Z stored bf16 for PE transpose, as before"):
                    v.reciprocal(B.rcq[:, 0:8], B.rs8[:, 0:8]).then_inc(s_v.h, 1)
                v.wait_ge(s_t.h, t_rcT)
                v.tensor_copy(B.rcT[0:8, 0:128], B.ps_T[0:8, 896:1024]).then_inc(s_v.h, 1)
                v.wait_ge(s_t.h, t_plo)
                v.tensor_copy(B.oplane[:, 0:512], B.ps_pl[:, 0:512])
                v.drain()
                v.wait_ge(s_t.h, t_pv[units[-1]])
                v.tensor_tensor(B.obuf[:, pp4 * 1024 + qg * 512:pp4 * 1024 + (qg + 1) * 512],
                                B.ps_o[:, 0:512], B.oplane[:, 0:512], OP.mult).then_inc(s_v.h, 1)

            @b.scalar
            def _(s):
                for (qb, hh) in units:
                    kl, qbg = kbs(qb)
                    nkb = len(kl)
                    u_i = qb * 2 + hh
                    s.wait_ge(s_t.h, t_sc[(qb, hh)])
                    s.drain()
                    m0 = 1 if qbg >= 4 else 0
                    m1_ = min(nkb - 1, 4)
                    if m1_ > m0:
                        s.activation(B.scbuf[:, m0 * 128:m1_ * 128],
                                     B.ps_sb[u_i % 2][:, m0 * 128:m1_ * 128],
                                     AF.Identity).then_inc(s_a.h, 1)
                    else:
                        s.drain().then_inc(s_a.h, 1)
                    s.wait_ge(s_v.h, v_mx[(qb, hh)])
                    iu = units.index((qb, hh))
                    if iu >= 1:
                        s.wait_ge(s_t.h, t_pt[units[iu - 1]])
                    s.activation(B.pbufA[:, 0:nkb * 128], B.scbuf[:, 0:nkb * 128], AF.Exp,
                                 scale=8.0, bias=B.nmax[:, 1:2],
                                 accum_out=B.rs8[:, u_i:u_i + 1]).then_inc(s_a.h, 1)

    def _ffn_layer_mlp(self, l, B):
        nc = self.nc
        s_t, s_v, s_a, s_p = self.s_t, self.s_v, self.s_a, self.s_p
        sd = self.s_dma
        xs3 = self.xstream.rearrange("p (c t) -> p c t", t=TS)
        with ExitStack() as es:
            sb = lambda n, s, d=BF16: es.enter_context(nc.sbuf_tensor(f"l{l}" + n, s, d))
            ps = lambda n, s, d=F32: es.enter_context(nc.psum_tensor(f"l{l}" + n, s, d))
            ubuf = sb("fm_ubuf", [128, 2 * 8192])
            h2 = sb("fm_h2", [128, 22 * 1024])
            w1b = [sb(f"fm_w1b{i}", [128, 2048]) for i in range(2)]
            w2b = [sb(f"fm_w2b{i}", [128, 256]) for i in range(2)]
            stA = sb("fm_stA", [16, 1536], F32)
            stB = sb("fm_stB", [16, 2048])
            sq = [sb(f"fm_sq{i}", [128, 512]) for i in range(2)]
            gb = [sb(f"fm_gb{i}", [128, 512]) for i in range(2)]
            ab = [sb(f"fm_ab{i}", [128, 512]) for i in range(2)]
            pP = sb("fm_pP", [128, 512])
            pQ = sb("fm_pQ", [128, 512])
            rtmp = sb("fm_rtmp", [128, 512])
            rtmp2 = sb("fm_rtmp2", [128, 512])
            xbuf4 = sb("fm_xb4", [128, 4 * 512])
            ps8 = [ps(f"fm_ps{i}", [128, 512]) for i in range(8)]

            def xlnv(qt):
                return ubuf[:, (qt % 2) * 8192:(qt % 2) * 8192 + 8192].rearrange(
                    "p (c t) -> p c t", t=1024)

            def xsrc(ck, tt, qt, w0, w1_):
                if qt >= 1 and tt == 0 and ck >= 4:
                    return xbuf4[:, (ck - 4) * 512 + w0:(ck - 4) * 512 + w1_]
                ts0 = qt * 1024 + tt * 512
                return B.xsl(ck, ts0 + w0, ts0 + w1_, True)

            TT_ = {}
            AA = {}
            VV = {}
            DD = {}
            cast_done = self.s_ld.v
            cast_done2 = sd[0].v

            def alloc_m1_t(qt):
                for tt in range(2):
                    for ck in range(8):
                        TT_[('st', qt, tt, ck)] = s_t.tick()
                for tt in range(2):
                    for ck in range(8):
                        TT_[('pl', qt, tt, ck)] = s_t.tick()

            def alloc_m1_a(qt):
                for tt in range(2):
                    for ck in range(8):
                        AA[('sq', qt, tt, ck)] = s_a.tick()
                    AA[('ln', qt, tt)] = s_a.tick()

            def alloc_m1v_stats(qt):
                for tt in range(2):
                    VV[('stA', qt, tt)] = s_v.tick()
                    VV[('stB', qt, tt)] = s_v.tick()

            def alloc_m1v_planes(qt):
                for tt in range(2):
                    for ck in range(8):
                        VV[('pl', qt, tt, ck)] = s_v.tick()

            def alloc_m1_v(qt):
                alloc_m1v_stats(qt)
                alloc_m1v_planes(qt)

            for qt in range(4):
                if qt == 0:
                    alloc_m1_t(0)
                for jt in range(22):
                    for tt in range(2):
                        TT_[('m2', qt, jt, tt)] = s_t.tick()
                for dtg in range(4):
                    for kk in range(22):
                        TT_[('m3', qt, dtg, kk)] = s_t.tick()
                    if dtg == 1 and qt < 3:
                        for tt in range(2):
                            for ck in range(8):
                                TT_[('st', qt + 1, tt, ck)] = s_t.tick()
                    if dtg == 2 and qt < 3:
                        for tt in range(2):
                            for ck in range(8):
                                TT_[('pl', qt + 1, tt, ck)] = s_t.tick()
            for qt in range(4):
                if qt == 0:
                    alloc_m1_a(0)
                for jt in range(22):
                    for tt in range(2):
                        AA[('m2', qt, jt, tt)] = s_a.tick()
                for pidx in range(4):
                    AA[('m3', qt, 0, pidx)] = s_a.tick()
                if qt < 3:
                    alloc_m1_a(qt + 1)
                for dtg in range(1, 4):
                    for pidx in range(4):
                        AA[('m3', qt, dtg, pidx)] = s_a.tick()
            def alloc_res(qt, dtg):
                for pidx in range(4):
                    VV[('res', qt, dtg, pidx)] = s_v.tick()
            for qt in range(4):
                if qt == 0:
                    alloc_m1_v(0)
                for jt in range(1, 23):
                    for tt in range(2):
                        VV[('h2', qt, jt - 1, tt)] = s_v.tick()
                if qt < 3:
                    VV[('xb', qt + 1)] = s_v.tick()
                alloc_res(qt, 0)
                if qt < 3:
                    for tt in range(2):
                        VV[('stA', qt + 1, tt)] = s_v.tick()
                        VV[('stB', qt + 1, tt)] = s_v.tick()
                alloc_res(qt, 1)
                if qt < 3:
                    alloc_m1v_planes(qt + 1)
                alloc_res(qt, 2)
                alloc_res(qt, 3)
            for qt in range(4):
                for jt in range(22):
                    DD[('m2', qt, jt)] = sd[3 + jt % 2].tick(32)
                for dtg in range(4):
                    for kk in range(22):
                        DD[('m3', qt, dtg, kk)] = sd[5 + kk % 2].tick(16)

            w1bf = self.w1bf.ap()
            w2bf = self.w2bf.ap()

            with nc.Block() as b:
                @b.sync
                def _(sync):
                    sync.wait_ge(self.s_ld.h, cast_done)
                    sync.wait_ge(sd[0].h, cast_done2)
                    for qt in range(4):
                        for jt in range(22):
                            wj = min(128, HID - jt * 128)
                            if jt >= 2:
                                sync.wait_ge(s_t.h, TT_[('m2', qt, jt - 2, 1)])
                            elif qt >= 1:
                                sync.wait_ge(s_t.h, TT_[('m2', qt - 1, jt + 20, 1)])
                            w3 = w1bf.rearrange("p (c e) -> p c e", e=5460)
                            sync.dma_start(out=w1b[jt % 2][:, 0:8 * wj].rearrange("p (c w) -> p c w", w=wj),
                                           in_=w3[:, :, jt * 128:jt * 128 + wj]).then_inc(sd[3 + jt % 2].h, 16)
                            sync.dma_start(out=w1b[jt % 2][:, 1024:1024 + 8 * wj].rearrange("p (c w) -> p c w", w=wj),
                                           in_=w3[:, :, HID + jt * 128:HID + jt * 128 + wj]).then_inc(sd[3 + jt % 2].h, 16)
                        for dtg in range(4):
                            for kk in range(22):
                                wj = min(128, HID - kk * 128)
                                if kk >= 2:
                                    sync.wait_ge(s_t.h, TT_[('m3', qt, dtg, kk - 2)])
                                elif dtg >= 1:
                                    sync.wait_ge(s_t.h, TT_[('m3', qt, dtg - 1, kk + 20)])
                                elif qt >= 1:
                                    sync.wait_ge(s_t.h, TT_[('m3', qt - 1, 3, kk + 20)])
                                sync.dma_start(out=w2b[kk % 2][0:wj, 0:256],
                                               in_=w2bf[kk * 128:kk * 128 + wj, dtg * 256:(dtg + 1) * 256]).then_inc(sd[5 + kk % 2].h, 16)

                @b.tensor
                def _(t):
                    def emit_m1_stats(qt):
                        for tt in range(2):
                            if tt == 1:
                                t.wait_ge(s_v.h, VV[('stA', qt, 0)])
                            for ck in range(8):
                                t.wait_ge(s_a.h, AA[('sq', qt, tt, ck)])
                                t.matmul(ps8[4][0:64, 0:512], B.onecol[:, 0:64],
                                         xsrc(ck, tt, qt, 0, 512),
                                         start=(ck == 0), stop=False)
                                t.matmul(ps8[4][0:64, 0:512], B.onecol[:, 64:128],
                                         sq[ck % 2][:, :],
                                         start=False, stop=(ck == 7)).then_inc(s_t.h, 1)

                    def emit_m1_planes(qt):
                        t.wait_ge(s_v.h, VV[('stB', qt, 1)])
                        for tt in range(2):
                            for ck in range(8):
                                i = tt * 8 + ck
                                pp_, qq_ = (ps8[5], ps8[6]) if i % 2 == 0 else (ps8[7], ps8[4])
                                if i >= 2:
                                    t.wait_ge(s_v.h, VV[('pl', qt, (i - 2) // 8, (i - 2) % 8)])
                                t.matmul(pp_[:, :], B.lnrowg[0:1, ck * 128:(ck + 1) * 128],
                                         stB[0:1, tt * 1024:tt * 1024 + 512], start=True, stop=True)
                                t.matmul(qq_[:, :], B.lnrowg[0:1, ck * 128:(ck + 1) * 128],
                                         stB[0:1, tt * 1024 + 512:tt * 1024 + 1024], start=True, stop=False)
                                t.matmul(qq_[:, :], B.lnrowb[0:1, ck * 128:(ck + 1) * 128],
                                         B.onesr[0:1, :], start=False, stop=True).then_inc(s_t.h, 1)

                    for qt in range(4):
                        if qt == 0:
                            emit_m1_stats(0)
                            emit_m1_planes(0)
                        for jt in range(22):
                            wj = min(128, HID - jt * 128)
                            t.wait_ge(sd[3 + jt % 2].h, DD[('m2', qt, jt)])
                            for tt in range(2):
                                if jt >= 1:
                                    t.wait_ge(s_a.h, AA[('m2', qt, jt - 1, tt)])
                                elif qt >= 1:
                                    t.wait_ge(s_a.h, AA[('m3', qt - 1, 3, 3)])
                                if jt == 0:
                                    t.wait_ge(s_v.h, VV[('pl', qt, tt, 7)])
                                for ck in range(8):
                                    t.matmul(ps8[tt][0:wj, :], w1b[jt % 2][:, ck * wj:(ck + 1) * wj],
                                             xlnv(qt)[:, ck, tt * 512:(tt + 1) * 512],
                                             start=(ck == 0), stop=(ck == 7))
                                    mm = t.matmul(ps8[2 + tt][0:wj, :], w1b[jt % 2][:, 1024 + ck * wj:1024 + (ck + 1) * wj],
                                                  xlnv(qt)[:, ck, tt * 512:(tt + 1) * 512],
                                                  start=(ck == 0), stop=(ck == 7))
                                mm.then_inc(s_t.h, 1)
                        for dtg in range(4):
                            if dtg == 0:
                                t.wait_ge(s_a.h, AA[('m2', qt, 21, 1)])
                                t.wait_ge(s_v.h, VV[('h2', qt, 21, 0)])
                                t.wait_ge(s_v.h, VV[('h2', qt, 21, 1)])
                            for kk in range(22):
                                wj = min(128, HID - kk * 128)
                                t.wait_ge(sd[5 + kk % 2].h, DD[('m3', qt, dtg, kk)])
                                if kk == 0 and dtg >= 1:
                                    for pidx in range(4):
                                        t.wait_ge(s_a.h, AA[('m3', qt, dtg - 1, pidx)])
                                for ii in range(2):
                                    for tt in range(2):
                                        pidx = ii * 2 + tt
                                        mm = t.matmul(ps8[pidx][:, :],
                                                      w2b[kk % 2][0:wj, ii * 128:(ii + 1) * 128],
                                                      h2[0:wj, kk * 1024 + tt * 512:kk * 1024 + (tt + 1) * 512],
                                                      start=(kk == 0), stop=(kk == 21))
                                mm.then_inc(s_t.h, 1)
                            if dtg == 1 and qt < 3:
                                emit_m1_stats(qt + 1)
                            if dtg == 2 and qt < 3:
                                emit_m1_planes(qt + 1)

                @b.scalar
                def _(s):
                    def emit_sq(qt):
                        for tt in range(2):
                            for ck in range(8):
                                if qt >= 1 and tt == 0 and ck == 4:
                                    s.wait_ge(s_v.h, VV[('xb', qt)])
                                if tt * 8 + ck >= 2:
                                    pc = tt * 8 + ck - 2
                                    s.wait_ge(s_t.h, TT_[('st', qt, pc // 8, pc % 8)])
                                s.activation(sq[ck % 2][:, :], xsrc(ck, tt, qt, 0, 512), AF.Square).then_inc(s_a.h, 1)
                            s.wait_ge(s_v.h, VV[('stA', qt, tt)])
                            s.activation(stA[0:1, 512:1024], stA[0:1, 512:1024], AF.Ln, bias=1e-5)
                            s.drain()
                            s.activation(stA[0:1, 512:1024], stA[0:1, 512:1024], AF.Exp, scale=-0.5).then_inc(s_a.h, 1)

                    for qt in range(4):
                        if qt == 0:
                            emit_sq(0)
                        for jt in range(22):
                            wj = min(128, HID - jt * 128)
                            for tt in range(2):
                                s.wait_ge(s_t.h, TT_[('m2', qt, jt, tt)])
                                if jt >= 1:
                                    s.wait_ge(s_v.h, VV[('h2', qt, jt - 1, tt)])
                                elif qt >= 1:
                                    s.wait_ge(s_v.h, VV[('res', qt - 1, 3, 3)])
                                s.activation(ab[tt][0:wj, :], ps8[tt][0:wj, :], AF.Identity,
                                             bias=B.b2row[0:wj, jt:jt + 1])
                                s.activation(gb[tt][0:wj, :], ps8[2 + tt][0:wj, :], AF.Gelu,
                                             bias=B.b2row[0:wj, 22 + jt:23 + jt]).then_inc(s_a.h, 1)
                        def emit_m3a(dtg):
                            s.wait_ge(s_t.h, TT_[('m3', qt, dtg, 21)])
                            for pidx in range(4):
                                ii, tt = pidx // 2, pidx % 2
                                dt_ = dtg * 2 + ii
                                dst = [rtmp, ab[1], gb[1], rtmp2][pidx]
                                if dtg >= 1:
                                    s.wait_ge(s_v.h, VV[('res', qt, dtg - 1, pidx)])
                                s.activation(dst[:, :], ps8[pidx][:, :], AF.Identity,
                                             bias=B.b2row[:, 44 + dt_:45 + dt_]).then_inc(s_a.h, 1)
                        emit_m3a(0)
                        if qt < 3:
                            emit_sq(qt + 1)
                        for dtg in range(1, 4):
                            emit_m3a(dtg)

                @b.vector
                def _(v):
                    def emit_xb(qt):
                        for ck in range(4, 8):
                            v.tensor_copy(xbuf4[:, (ck - 4) * 512:(ck - 4) * 512 + 1],
                                          xs3[:, ck, qt * 1024:qt * 1024 + 1])
                            v.drain()
                            v.tensor_copy(xbuf4[:, (ck - 4) * 512 + 1:(ck - 4) * 512 + 512],
                                          xs3[:, ck, qt * 1024 + 1:qt * 1024 + 512])
                            v.drain()
                        v.drain().then_inc(s_v.h, 1)

                    def emit_stats_chain(qt):
                        for tt in range(2):
                            v.wait_ge(s_t.h, TT_[('st', qt, tt, 7)])
                            v.tensor_scalar(stA[0:1, 0:512], ps8[4][0:1, 0:512], 1.0 / D, None, OP.mult)
                            v.drain()
                            v.tensor_scalar(stA[0:1, 512:1024], ps8[4][32:33, 0:512], 1.0 / D, None, OP.mult)
                            v.drain()
                            v.tensor_tensor(stA[0:1, 1024:1536], stA[0:1, 0:512], stA[0:1, 0:512], OP.mult)
                            v.drain()
                            v.tensor_tensor(stA[0:1, 512:1024], stA[0:1, 512:1024], stA[0:1, 1024:1536], OP.subtract)
                            v.drain().then_inc(s_v.h, 1)
                            v.wait_ge(s_a.h, AA[('ln', qt, tt)])
                            v.tensor_copy(stB[0:1, tt * 1024:tt * 1024 + 512], stA[0:1, 512:1024])
                            v.drain()
                            v.tensor_tensor(stA[0:1, 1024:1536], stA[0:1, 0:512], stA[0:1, 512:1024], OP.mult)
                            v.drain()
                            v.tensor_scalar(stB[0:1, tt * 1024 + 512:tt * 1024 + 1024], stA[0:1, 1024:1536], -1.0, None, OP.mult)
                            v.drain().then_inc(s_v.h, 1)

                    def emit_planes_apply(qt):
                        for tt in range(2):
                            for ck in range(8):
                                i = tt * 8 + ck
                                pp_, qq_ = (ps8[5], ps8[6]) if i % 2 == 0 else (ps8[7], ps8[4])
                                v.wait_ge(s_t.h, TT_[('pl', qt, tt, ck)])
                                v.drain()
                                v.tensor_copy(pP[:, :], pp_[:, :])
                                v.tensor_copy(pQ[:, :], qq_[:, :])
                                v.drain()
                                dst = xlnv(qt)[:, ck, tt * 512:(tt + 1) * 512]
                                v.tensor_tensor(dst, xsrc(ck, tt, qt, 0, 512), pP[:, :], OP.mult)
                                v.drain()
                                v.tensor_tensor(dst, dst, pQ[:, :], OP.add).then_inc(s_v.h, 1)

                    def emit_res(qt, dtg):
                        for pidx in range(4):
                            ii, tt = pidx // 2, pidx % 2
                            dt_ = dtg * 2 + ii
                            v.wait_ge(s_a.h, AA[('m3', qt, dtg, pidx)])
                            src = [rtmp, ab[1], gb[1], rtmp2][pidx]
                            dstx = xs3[:, dt_, 1 + qt * 1024 + tt * 512:1 + qt * 1024 + (tt + 1) * 512]
                            v.tensor_tensor(dstx, dstx, src[:, :], OP.add).then_inc(s_v.h, 1)

                    for qt in range(4):
                        if qt == 0:
                            emit_stats_chain(0)
                            emit_planes_apply(0)
                        for jt in range(1, 23):
                            wjp = min(128, HID - (jt - 1) * 128)
                            for tt in range(2):
                                v.wait_ge(s_a.h, AA[('m2', qt, jt - 1, tt)])
                                if qt >= 1 and jt == 1:
                                    v.wait_ge(s_t.h, TT_[('m3', qt - 1, 3, 21)])
                                v.tensor_tensor(h2[0:wjp, (jt - 1) * 1024 + tt * 512:(jt - 1) * 1024 + (tt + 1) * 512],
                                                ab[tt][0:wjp, :], gb[tt][0:wjp, :], OP.mult).then_inc(s_v.h, 1)
                        if qt < 3:
                            emit_xb(qt + 1)
                        emit_res(qt, 0)
                        if qt < 3:
                            emit_stats_chain(qt + 1)
                        emit_res(qt, 1)
                        if qt < 3:
                            emit_planes_apply(qt + 1)
                        emit_res(qt, 2)
                        emit_res(qt, 3)

    def p7_combine(self):
        nc = self.nc
        s_t, s_v = self.s_t, self.s_v
        sd = self.s_dma
        xs3 = self.xstream.rearrange("p (c t) -> p c t", t=TS)
        with ExitStack() as es:
            xstg = [es.enter_context(nc.sbuf_tensor(f"p7_x{i}", [128, D], F32)) for i in range(2)]
            y3stg = [es.enter_context(nc.sbuf_tensor(f"p7_y{i}", [128, D], BF16)) for i in range(2)]
            outb = [es.enter_context(nc.sbuf_tensor(f"p7_o{i}", [128, D], F32)) for i in range(2)]
            ps_f = [es.enter_context(nc.psum_tensor(f"p7_ps{i}", [128, D], BF16)) for i in range(2)]
            NT = T // 128
            xsl_ = [0, 3]
            ysl_ = [1, 5]
            osl_ = [2, 6]
            xld = [sd[xsl_[i % 2]].tick(16) for i in range(NT)]
            y3ld = [sd[ysl_[i % 2]].tick(16) for i in range(NT)]
            stt = [sd[osl_[i % 2]].tick(16) for i in range(NT)]
            t_f = [s_t.tick() for _ in range(NT)]
            v_cmb = [s_v.tick() for _ in range(NT)]
            with nc.Block() as b:
                @b.sync
                def _(sync):
                    for i in range(NT):
                        if i >= 2:
                            sync.wait_ge(s_v.h, v_cmb[i - 2])
                        sync.dma_start(out=xstg[i % 2][:, :],
                                       in_=self.din["x"][i * 128:(i + 1) * 128, :]).then_inc(sd[xsl_[i % 2]].h, 16)
                        sync.dma_start(out=y3stg[i % 2][:, :],
                                       in_=self.y3_dram.ap()[:, i * 1024:(i + 1) * 1024]).then_inc(sd[ysl_[i % 2]].h, 16)
                        if i >= 1:
                            sync.wait_ge(s_v.h, v_cmb[i - 1])
                            sync.dma_start(out=self.out[(i - 1) * 128:i * 128, :],
                                           in_=outb[(i - 1) % 2][:, :]).then_inc(sd[osl_[(i - 1) % 2]].h, 16)
                    sync.wait_ge(s_v.h, v_cmb[NT - 1])
                    sync.dma_start(out=self.out[(NT - 1) * 128:NT * 128, :],
                                   in_=outb[(NT - 1) % 2][:, :]).then_inc(sd[osl_[(NT - 1) % 2]].h, 16)
                    sync.wait_ge(sd[osl_[0]].h, stt[NT - 2])
                    sync.wait_ge(sd[osl_[1]].h, stt[NT - 1])

                @b.tensor
                def _(t):
                    for i in range(NT):
                        if i >= 2:
                            t.wait_ge(s_v.h, v_cmb[i - 2])
                        for ck in range(8):
                            mm = t.transpose(ps_f[i % 2][:, ck * 128:(ck + 1) * 128],
                                             xs3[:, ck, 1 + i * 128:1 + (i + 1) * 128],
                                             self.ident[:, :])
                        mm.then_inc(s_t.h, 1)

                @b.vector
                def _(v):
                    for i in range(NT):
                        v.wait_ge(s_t.h, t_f[i])
                        v.wait_ge(sd[xsl_[i % 2]].h, xld[i])
                        v.wait_ge(sd[ysl_[i % 2]].h, y3ld[i])
                        if i >= 2:
                            v.wait_ge(sd[osl_[i % 2]].h, stt[i - 2])
                        v.tensor_tensor(outb[i % 2][:, :], ps_f[i % 2][:, :], y3stg[i % 2][:, :], OP.add)
                        v.drain()
                        v.tensor_tensor(outb[i % 2][:, :], outb[i % 2][:, :], xstg[i % 2][:, :], OP.add).then_inc(s_v.h, 1)

def build_program(dbg=(), upto=99, cores=8, hy_groups=16, ffn_layers=2):
    k = K(dbg)
    k.cores = cores
    k.hy_groups = hy_groups
    k.ffn_layers = ffn_layers
    k.declare_dram()
    es = ExitStack()
    with es:
        k.alloc(es)
        nc = k.nc
        with ExitStack() as es1:
            pa = es1.enter_context(nc.sbuf_tensor("pa", [128, T], F32))
            xtok = [es1.enter_context(nc.sbuf_tensor(f"xtok{i}", [128, D], F32)) for i in range(2)]
            xn_tok = es1.enter_context(nc.sbuf_tensor("xn_tok", [128, D], BF16))
            ps_t = [es1.enter_context(nc.psum_tensor(f"ps_t{i}", [128, 1024], BF16)) for i in range(2)]
            with nc.named_scope("p01"):
                k.p0_seed(pa, pa)
                if upto >= 1:
                    k.p1_ln1(pa, xtok, xn_tok, ps_t)
        if upto >= 2:
            k.barrier()
            with nc.named_scope("p2"):
                k.p2_proj()
        if upto >= 3:
            with nc.named_scope("hyena"):
                k.p_hyena()
        if upto >= 4:
            k.p_ffn()
        if upto >= 5:
            with nc.named_scope("p7"):
                k.p7_combine()
    return k


_PROG_CACHE = {}


def kernel(**inputs):
    X = inputs["L0_attn_X"]
    Wp = np.ascontiguousarray(inputs["L0_attn_Wp"], np.float32)
    bp = np.ascontiguousarray(inputs["L0_attn_bp"], np.float32)
    Wf = np.ascontiguousarray(inputs["L0_attn_Wf"], np.float32)
    bf = np.ascontiguousarray(inputs["L0_attn_bf"], np.float32)
    shared = {
        "Wp": Wp, "bp": bp, "Wf": Wf, "bf": bf,
        "wqkv": np.ascontiguousarray(inputs["L0_ffn_wqkv"], np.float32),
        "q_scale": np.ascontiguousarray(inputs["L0_ffn_q_scale"], np.float32),
        "k_scale": np.ascontiguousarray(inputs["L0_ffn_k_scale"], np.float32),
        "wo": np.ascontiguousarray(inputs["L0_ffn_wo"], np.float32),
        "ln_g": np.ascontiguousarray(inputs["L0_ffn_ln_g"], np.float32),
        "ln_b": np.ascontiguousarray(inputs["L0_ffn_ln_b"], np.float32),
        "w1": np.ascontiguousarray(inputs["L0_ffn_w1"], np.float32),
        "b1": np.ascontiguousarray(inputs["L0_ffn_b1"], np.float32),
        "w2": np.ascontiguousarray(inputs["L0_ffn_w2"], np.float32),
        "b2": np.ascontiguousarray(inputs["L0_ffn_b2"], np.float32),
    }
    if "prog" not in _PROG_CACHE:
        _PROG_CACHE["prog"] = build_program(upto=5, cores=8, hy_groups=16, ffn_layers=2)
    k = _PROG_CACHE["prog"]
    in_maps = []
    for core in range(8):
        cols = np.concatenate([i * D + core * 128 + np.arange(128) for i in range(3)])
        m = dict(shared)
        m["x"] = np.ascontiguousarray(X[core], np.float32)
        m["Wf_my"] = np.ascontiguousarray(Wf[:, cols])
        m["bf_my"] = np.ascontiguousarray(bf[cols])
        in_maps.append(m)
    res = run_bass_kernel_spmd(k.nc, in_maps, core_ids=list(range(8)))
    out = np.stack([res.results[c]["out"] for c in range(8)], 0).astype(np.float32)
    return out


if __name__ == "__main__":
    k = build_program(upto=5, cores=8)
    print("built ok")
